# version 01
# baseline (speedup 1.0000x reference)
import math
from contextlib import ExitStack

import numpy as np
import ml_dtypes
import concourse.bass as bass
import concourse.mybir as mybir
from concourse.bass_utils import run_bass_kernel_spmd

F32 = mybir.dt.float32
BF16 = mybir.dt.bfloat16
AF = mybir.ActivationFunctionType
ALU = mybir.AluOpType
AX = mybir.AxisListType

NCORE = 8
B = 2
T = 8192
TOK = 2048
D = 1024
EI = 2048
DST = 16
DTR = 64
HALO = 514
F32EPS = float(np.finfo(np.float32).eps)
BF = ml_dtypes.bfloat16


class Ctr:

    def __init__(self, sem):
        self.sem = sem
        self.v = 0

    def tick(self, instr, n=1):
        instr.then_inc(self.sem, n)
        self.v += n
        return instr

    def wait(self, eng):
        eng.wait_ge(self.sem, self.v)

    def mark(self):
        return (self, self.v)


_OPS = {
    "tensor_copy", "tensor_tensor", "tensor_add", "tensor_sub", "tensor_mul",
    "tensor_max", "tensor_scalar", "tensor_single_scalar", "tensor_scalar_add",
    "tensor_scalar_sub", "tensor_scalar_mul", "tensor_scalar_max",
    "tensor_scalar_min", "tensor_relu", "tensor_reduce", "reduce_max",
    "reduce_sum", "scalar_tensor_tensor", "tensor_tensor_scan", "memset",
    "memzero", "reciprocal", "select", "transpose", "bn_stats", "bn_aggr",
    "iota", "activation", "square", "sqrt", "mul", "copy", "add", "sign",
    "engine_nop", "stream_shuffle", "pool", "pool_avg", "pool_max",
    "copy_predicated",
}


class EngProxy:

    def __init__(self, eng, ctr):
        self._eng = eng
        self._ctr = ctr

    def __getattr__(self, name):
        attr = getattr(self._eng, name)
        if name not in _OPS:
            return attr

        def wrap(*a, **k):
            self._eng.wait_ge(self._ctr.sem, self._ctr.v)
            instr = attr(*a, **k)
            self._ctr.tick(instr)
            return instr

        return wrap

    def wait_ge(self, sem, v):
        return self._eng.wait_ge(sem, v)

    def wait(self, mark):
        ctr, v = mark if isinstance(mark, tuple) else (mark, mark.v)
        return self._eng.wait_ge(ctr.sem, v)


class KB:

    def __init__(self):
        self.nc = bass.Bass("TRN2", target_bir_lowering=False,
                            detect_race_conditions=False)
        self.ctx = ExitStack()
        self._n = 0

    def name(self, p):
        self._n += 1
        return f"{p}{self._n}"

    def inp(self, name, shape, dt=F32):
        return self.nc.dram_tensor(name, shape, dt, kind="ExternalInput").ap()

    def out(self, name, shape, dt=F32):
        return self.nc.dram_tensor(name, shape, dt, kind="ExternalOutput").ap()

    def sb(self, shape, dt=F32, tag="sb"):
        return self.ctx.enter_context(
            self.nc.sbuf_tensor(self.name(tag), shape, dt))

    def ps(self, shape, dt=F32, tag="ps"):
        return self.ctx.enter_context(
            self.nc.psum_tensor(self.name(tag), shape, dt))

    def ctr(self, tag="sem"):
        return Ctr(self.ctx.enter_context(self.nc.semaphore(self.name(tag))))

    def open(self):
        self.blk = self.ctx.enter_context(self.nc.Block())
        self.ld = self.ctr("ld")
        self.gq = self.ctr("gq")
        self.st = self.ctr("st")
        self.pe = self.ctr("pe")
        self.ve = self.ctr("ve")
        self.se = self.ctr("se")
        self.ge = self.ctr("ge")
        self.v = EngProxy(self.nc.vector, self.ve)
        self.s = EngProxy(self.nc.scalar, self.se)
        self.g = EngProxy(self.nc.gpsimd, self.ge)
        self.t = self.nc.tensor
        return self

    def finish(self):
        self.nc.sync.wait_ge(self.st.sem, self.st.v)
        self.ctx.close()
        return self.nc

    def load(self, dst_ap, src_ap, gp=False, slow=False):
        q = self.nc.gpsimd if gp else self.nc.sync
        ctr = self.gq if gp else self.ld
        kw = dict(allow_slow_non_contiguous=True) if slow else {}
        ctr.tick(q.dma_start(out=dst_ap, in_=src_ap, **kw), 16)

    def store(self, dst_ap, src_ap, slow=False):
        kw = dict(allow_slow_non_contiguous=True) if slow else {}
        self.st.tick(self.nc.sync.dma_start(out=dst_ap, in_=src_ap, **kw), 16)

    def fence(self, consumer, *ctrs):
        eng = consumer._eng if isinstance(consumer, EngProxy) else consumer
        for c in ctrs:
            eng.wait_ge(c.sem, c.v)



def build_p1():
    kb = KB()
    nc = kb.nc
    xs = kb.inp("xs", [3 + TOK, D])
    w_in0 = kb.inp("w_in0", [128, 8 * EI])
    w_in1 = kb.inp("w_in1", [128, 8 * EI])
    conv_w = kb.inp("conv_w", [128, 16 * 4])
    conv_b = kb.inp("conv_b", [128, 16])
    w_dtl = kb.inp("w_dtl", [128, 16 * DTR])
    w_Bp = kb.inp("w_Bp", [128, 16 * DST])
    w_Cp = kb.inp("w_Cp", [128, 16 * DST])
    w_dtp = kb.inp("w_dtp", [DTR, EI])
    dtp_b = kb.inp("dtp_b", [128, 16])
    ident = kb.inp("ident", [128, 128], BF16)

    o_xn0 = kb.out("o_xn0", [TOK, D])
    o_sz = kb.out("o_sz", [EI, TOK])
    o_xconv = kb.out("o_xconv", [EI, TOK])
    o_dt = kb.out("o_dt", [EI, TOK])
    o_B = kb.out("o_B", [DST, TOK])
    o_C = kb.out("o_C", [DST, TOK])

    NT = TOK // 128
    T2 = 3 + TOK
    kb.open()
    v, s, t = kb.v, kb.s, kb.t

    idt = kb.sb([128, 128], BF16, "idt")
    rn_fm = kb.sb([128, 8 * T2], BF16, "rnfm")
    wh = kb.sb([128, 8 * EI], BF16, "wh")
    xcb = kb.sb([128, 16 * TOK], BF16, "xcb")
    cw = kb.sb([128, 16 * 4], F32, "cw")
    cb = kb.sb([128, 16], F32, "cb")
    dtpb = kb.sb([128, 16], F32, "dtpb")
    wb_dtl = kb.sb([128, 16 * DTR], BF16, "wdtl")
    wb_B = kb.sb([128, 16 * DST], BF16, "wB")
    wb_C = kb.sb([128, 16 * DST], BF16, "wC")
    wb_dtp = kb.sb([DTR, EI], BF16, "wdtp")
    dtl_b = kb.sb([DTR, TOK], BF16, "dtl")

    kb.load(idt[:], ident)
    kb.load(cw[:], conv_w)
    kb.load(cb[:], conv_b)
    kb.load(dtpb[:], dtp_b)
    kb.load(wb_dtl[:], w_dtl, gp=True)
    kb.load(wb_B[:], w_Bp, gp=True)
    kb.load(wb_C[:], w_Cp, gp=True)
    kb.load(wb_dtp[:], w_dtp, gp=True)

    x_tm = kb.sb([128, D], F32, "xtm")
    sq = kb.sb([128, D], F32, "sq")
    s1 = kb.sb([128, 1], F32, "s1")
    s2 = kb.sb([128, 1], F32, "s2")
    var = kb.sb([128, 1], F32, "var")
    rstd = kb.sb([128, 1], F32, "rstd")
    rrms = kb.sb([128, 1], F32, "rrms")
    xn_tm = kb.sb([128, D], F32, "xntm")
    rn_tm = kb.sb([128, D], BF16, "rntm")
    ps_tr = kb.ps([128, 128], BF16, "pstr")

    def ln_rms_tile(rows, x_ap, col0):
        kb.fence(v, kb.ld, kb.st)
        kb.fence(s, kb.ld, kb.st)
        v.reduce_sum(s1[0:rows], x_ap, axis=AX.X)
        s.activation(sq[0:rows], x_ap, AF.Square)
        kb.fence(v, kb.se)
        v.reduce_sum(s2[0:rows], sq[0:rows], axis=AX.X)
        v.tensor_scalar_mul(s1[0:rows], s1[0:rows], -1.0 / D)
        v.tensor_mul(var[0:rows], s1[0:rows], s1[0:rows])
        v.scalar_tensor_tensor(var[0:rows], s2[0:rows], 1.0 / D, var[0:rows],
                               ALU.mult, ALU.subtract)
        v.tensor_scalar_add(var[0:rows], var[0:rows], 1e-5)
        kb.fence(s, kb.ve)
        s.activation(rstd[0:rows], var[0:rows], AF.Sqrt)
        kb.fence(v, kb.se)
        v.reciprocal(rstd[0:rows], rstd[0:rows])
        v.tensor_scalar(xn_tm[0:rows], x_ap, s1[0:rows], rstd[0:rows],
                        ALU.add, ALU.mult)
        v.tensor_scalar_add(rrms[0:rows], var[0:rows], -1e-5)
        v.tensor_mul(rrms[0:rows], rrms[0:rows], rstd[0:rows])
        v.tensor_mul(rrms[0:rows], rrms[0:rows], rstd[0:rows])
        v.tensor_scalar_add(rrms[0:rows], rrms[0:rows], F32EPS)
        kb.fence(s, kb.ve)
        s.activation(rrms[0:rows], rrms[0:rows], AF.Ln)
        kb.fence(v, kb.se)
        v.tensor_scalar_mul(rrms[0:rows], rrms[0:rows], -0.5)
        kb.fence(s, kb.ve)
        s.activation(rrms[0:rows], rrms[0:rows], AF.Exp)
        kb.fence(v, kb.se)
        v.tensor_scalar_mul(rn_tm[0:rows], xn_tm[0:rows], rrms[0:rows])
        kb.fence(t, kb.ve)
        for c in range(8):
            kb.pe.tick(t.transpose(ps_tr[0:128, 0:rows],
                                   rn_tm[0:rows, 128 * c:128 * (c + 1)],
                                   idt[0:rows, 0:rows]))
            kb.fence(v, kb.pe)
            v.tensor_copy(rn_fm[:, c * T2 + col0: c * T2 + col0 + rows],
                          ps_tr[0:128, 0:rows])
            kb.fence(t, kb.ve)

    xh = kb.sb([3, D], F32, "xh")
    kb.load(xh[:], xs[0:3, :])
    ln_rms_tile(3, xh[:], 0)
    for i in range(NT):
        kb.fence(nc.sync, kb.ve, kb.se)
        kb.load(x_tm[:], xs[3 + 128 * i: 3 + 128 * (i + 1), :])
        ln_rms_tile(128, x_tm[:], 3 + 128 * i)
        kb.fence(nc.sync, kb.ve)
        kb.store(o_xn0[128 * i:128 * (i + 1), :], xn_tm[0:128, :])

    ps_mm = [kb.ps([128, 512], F32, "pmm") for _ in range(6)]
    bank_ready = [None] * 6
    xin_c = kb.sb([128, T2], F32, "xinc")
    conv_t = kb.sb([128, TOK], F32, "convt")
    z_t = conv_t

    def tiles_for(total, off):
        out, p = [], off
        while p < off + total:
            w = min(512, off + total - p)
            out.append((p, w))
            p += w
        return out

    kb.load(wh[:], w_in0, gp=True)

    def bank_wait(bk):
        if bank_ready[bk] is not None:
            c, val = bank_ready[bk]
            t.wait_ge(c.sem, val)

    def in_proj_chunk(o, tiles, dst, dst_off, evac_eng):
        for j, (c0, w) in enumerate(tiles):
            bk = (o * len(tiles) + j) % 4
            bank_wait(bk)
            for k in range(8):
                mi = t.matmul(ps_mm[bk][:, 0:w],
                              wh[:, k * EI + 128 * o: k * EI + 128 * (o + 1)],
                              rn_fm[:, k * T2 + c0: k * T2 + c0 + w],
                              start=(k == 0), stop=(k == 7))
            kb.pe.tick(mi)
            kb.fence(evac_eng, kb.pe)
            evac_eng.tensor_copy(dst[:, c0 + dst_off:c0 + dst_off + w],
                                 ps_mm[bk][:, 0:w])
            bank_ready[bk] = evac_eng._ctr.mark()

    xin_tiles = tiles_for(T2, 0)
    kb.fence(t, kb.ve, kb.gq)
    for o in range(16):
        in_proj_chunk(o, xin_tiles, xin_c, 0, v)
        cw_o = [cw[:, o * 4 + k: o * 4 + k + 1] for k in range(4)]
        v.tensor_scalar(conv_t[:], xin_c[:, 0:TOK], cw_o[0], cb[:, o:o + 1],
                        ALU.mult, ALU.add)
        for k in (1, 2, 3):
            v.scalar_tensor_tensor(conv_t[:], xin_c[:, k:k + TOK],
                                   cw_o[k], conv_t[:], ALU.mult, ALU.add)
        kb.fence(s, kb.ve, kb.st)
        s.activation(xin_c[:, 0:TOK], conv_t[:], AF.Silu)
        s.activation(xcb[:, o * TOK:(o + 1) * TOK], conv_t[:], AF.Silu)
        kb.fence(nc.sync, kb.se)
        kb.store(o_xconv[128 * o:128 * (o + 1), :], xin_c[:, 0:TOK])
        kb.fence(v, kb.st)

    kb.fence(nc.gpsimd, kb.pe)
    kb.load(wh[:], w_in1, gp=True)
    z_tiles = tiles_for(TOK, 3)
    kb.fence(t, kb.gq)
    for o in range(16):
        in_proj_chunk(o, z_tiles, z_t, -3, v)
        kb.fence(s, kb.ve)
        s.activation(z_t[:], z_t[:], AF.Silu)
        kb.fence(nc.sync, kb.se)
        kb.store(o_sz[128 * o:128 * (o + 1), :], z_t[:])
        kb.fence(v, kb.st)
        kb.fence(s, kb.st)

    bc_stage = kb.sb([DST, 2 * 512], F32, "bcst")
    kb.fence(t, kb.se, kb.gq)
    for j, (c0, w) in enumerate(tiles_for(TOK, 0)):
        for (dst_out, wmat, rows) in ((None, wb_dtl, DTR), (o_B, wb_B, DST),
                                      (o_C, wb_C, DST)):
            bk = {DTR: j % 2, DST: 4 + j % 2}[rows]
            bank_wait(bk)
            cols = {DTR: DTR, DST: DST}[rows]
            for k in range(16):
                mi = t.matmul(ps_mm[bk][0:rows, 0:w],
                              wmat[:, k * cols:(k + 1) * cols],
                              xcb[:, k * TOK + c0: k * TOK + c0 + w],
                              start=(k == 0), stop=(k == 15))
            kb.pe.tick(mi)
            kb.fence(v, kb.pe)
            if dst_out is None:
                v.tensor_copy(dtl_b[0:rows, c0:c0 + w], ps_mm[bk][0:rows, 0:w])
            else:
                slot = 0 if dst_out is o_B else 1
                kb.fence(v, kb.st)
                v.tensor_copy(bc_stage[0:DST, 512 * slot:512 * slot + w],
                              ps_mm[bk][0:rows, 0:w])
                kb.fence(nc.sync, kb.ve)
                kb.store(dst_out[:, c0:c0 + w],
                         bc_stage[0:DST, 512 * slot:512 * slot + w])
            bank_ready[bk] = kb.ve.mark()

    t_relu = kb.sb([128, 512], F32, "trelu")
    t_sp = kb.sb([128, 512], F32, "tsp")
    dt_stage = kb.sb([128, 2 * 512], F32, "dtstage")
    kb.fence(t, kb.ve)
    for o in range(16):
        for j, (c0, w) in enumerate(tiles_for(TOK, 0)):
            bk = 2 + (o * 4 + j) % 2
            slot = (o * 4 + j) % 2
            bank_wait(bk)
            mi = t.matmul(ps_mm[bk][:, 0:w], wb_dtp[:, 128 * o:128 * (o + 1)],
                          dtl_b[:, c0:c0 + w], start=True, stop=True)
            kb.pe.tick(mi)
            kb.fence(v, kb.pe)
            b_o = dtpb[:, o:o + 1]
            v.tensor_scalar(t_relu[:, 0:w], ps_mm[bk][:, 0:w], b_o, 0.0,
                            ALU.add, ALU.max)
            kb.fence(s, kb.ve)
            s.activation(t_sp[:, 0:w], ps_mm[bk][:, 0:w], AF.Abs, bias=b_o)
            bank_ready[bk] = kb.se.mark()
            s.activation(t_sp[:, 0:w], t_sp[:, 0:w], AF.Exp, scale=-1.0)
            s.activation(t_sp[:, 0:w], t_sp[:, 0:w], AF.Ln, bias=1.0)
            kb.fence(v, kb.se, kb.st)
            v.tensor_add(dt_stage[:, 512 * slot:512 * slot + w],
                         t_relu[:, 0:w], t_sp[:, 0:w])
            kb.fence(nc.sync, kb.ve)
            kb.store(o_dt[128 * o:128 * (o + 1), c0:c0 + w],
                     dt_stage[:, 512 * slot:512 * slot + w])
    return kb.finish()


def chunk_fm(a, p=128):
    K, c = a.shape
    return np.ascontiguousarray(
        a.reshape(K // p, p, c).transpose(1, 0, 2).reshape(p, -1))


_IDENT = np.eye(128).astype(BF)


def p1_inmaps(inputs):
    x = inputs["L0_attn_x"]
    w_inT = inputs["L0_attn_in_proj_w"].T
    w_in0 = chunk_fm(np.ascontiguousarray(w_inT[:, 0:EI]))
    w_in1 = chunk_fm(np.ascontiguousarray(w_inT[:, EI:2 * EI]))
    conv_w = chunk_fm(inputs["L0_attn_conv_w"])
    conv_b = chunk_fm(inputs["L0_attn_conv_b"][:, None])
    xp = inputs["L0_attn_x_proj_w"].T
    w_dtl = chunk_fm(np.ascontiguousarray(xp[:, 0:DTR]))
    w_Bp = chunk_fm(np.ascontiguousarray(xp[:, DTR:DTR + DST]))
    w_Cp = chunk_fm(np.ascontiguousarray(xp[:, DTR + DST:DTR + 2 * DST]))
    w_dtp = np.ascontiguousarray(inputs["L0_attn_dt_proj_w"].T)
    dtp_b = chunk_fm(inputs["L0_attn_dt_proj_b"][:, None])
    shared = dict(w_in0=w_in0, w_in1=w_in1, conv_w=conv_w, conv_b=conv_b,
                  w_dtl=w_dtl, w_Bp=w_Bp, w_Cp=w_Cp, w_dtp=w_dtp,
                  dtp_b=dtp_b, ident=_IDENT)
    maps = []
    for c in range(NCORE):
        b, ch = c // 4, c % 4
        g0 = ch * TOK
        xs = np.zeros((3 + TOK, D), np.float32)
        xs[3:] = x[b, g0:g0 + TOK]
        if g0 >= 3:
            xs[0:3] = x[b, g0 - 3:g0]
        maps.append(dict(xs=np.ascontiguousarray(xs), **shared))
    return maps


def kernel(**inputs):
    raise NotImplementedError("under construction")


def build_p2():
    kb = KB()
    nc = kb.nc
    dts = kb.inp("dts", [512, T])
    xcs = kb.inp("xcs", [512, T])
    A_log_s = kb.inp("A_log_s", [512, DST])
    D_s = kb.inp("D_s", [128, 4])
    Bsf = kb.inp("Bsf", [DST, T])
    Csf = kb.inp("Csf", [DST, T])
    o_y2 = kb.out("o_y2", [512, T])

    TC = 512
    NTC = T // TC
    kb.open()
    v, s, g, t = kb.v, kb.s, kb.g, kb.t

    A_f = kb.sb([128, 4 * DST], F32, "Af")
    Dv = kb.sb([128, 4], F32, "Dv")
    carries = kb.sb([128, 4 * DST], F32, "carry")
    dA = kb.sb([128, DST * TC], F32, "dA")
    wB = kb.sb([128, DST * TC], F32, "wB")
    hh = kb.sb([128, DST * TC], F32, "hh")
    Bb = kb.sb([128, DST * TC], F32, "Bb")
    Cb = kb.sb([128, DST * TC], F32, "Cb")
    dt_t = kb.sb([128, 2 * TC], F32, "dtt")
    xc_t = kb.sb([128, 2 * TC], F32, "xct")
    w_t = kb.sb([128, TC], F32, "wt")
    y_st = kb.sb([128, 2 * TC], F32, "yst")

    kb.load(Dv[:], D_s)
    al = kb.sb([128, 4 * DST], F32, "al")
    for ec in range(4):
        kb.load(al[:, ec * DST:(ec + 1) * DST],
                A_log_s[ec * 128:(ec + 1) * 128, :])
    kb.fence(s, kb.ld)
    s.activation(A_f[:], al[:], AF.Exp)
    kb.fence(v, kb.se)
    v.tensor_scalar_mul(A_f[:], A_f[:], -1.0)
    v.memset(carries[:], 0.0)

    for tc in range(NTC):
        kb.fence(nc.sync, kb.ge)
        for ss in range(DST):
            kb.load(Bb[:, ss * TC:(ss + 1) * TC],
                    Bsf[ss:ss + 1, tc * TC:(tc + 1) * TC]
                    .partition_broadcast(128).squeeze(1))
            kb.load(Cb[:, ss * TC:(ss + 1) * TC],
                    Csf[ss:ss + 1, tc * TC:(tc + 1) * TC]
                    .partition_broadcast(128).squeeze(1))
        for ec in range(4):
            buf = ec % 2
            kb.fence(nc.sync, kb.se, kb.ge, kb.ve)
            kb.load(dt_t[:, buf * TC:(buf + 1) * TC],
                    dts[ec * 128:(ec + 1) * 128, tc * TC:(tc + 1) * TC])
            kb.load(xc_t[:, buf * TC:(buf + 1) * TC],
                    xcs[ec * 128:(ec + 1) * 128, tc * TC:(tc + 1) * TC])
            dtb = dt_t[:, buf * TC:(buf + 1) * TC]
            xcb_ = xc_t[:, buf * TC:(buf + 1) * TC]
            kb.fence(s, kb.ld, kb.ve, kb.ge)
            for ss in range(DST):
                s.activation(dA[:, ss * TC:(ss + 1) * TC], dtb, AF.Exp,
                             scale=A_f[:, ec * DST + ss:ec * DST + ss + 1])
            kb.fence(g, kb.ld, kb.ve)
            g.tensor_mul(w_t[:], dtb, xcb_)
            for ss in range(DST):
                g.tensor_mul(wB[:, ss * TC:(ss + 1) * TC], w_t[:],
                             Bb[:, ss * TC:(ss + 1) * TC])
            kb.fence(v, kb.se, kb.ge)
            for ss in range(DST):
                v.tensor_tensor_scan(
                    hh[:, ss * TC:(ss + 1) * TC],
                    dA[:, ss * TC:(ss + 1) * TC],
                    wB[:, ss * TC:(ss + 1) * TC],
                    carries[:, ec * DST + ss:ec * DST + ss + 1],
                    ALU.mult, ALU.add)
            v.tensor_copy(carries[:, ec * DST:(ec + 1) * DST],
                          hh[:, TC - 1::TC])
            kb.fence(g, kb.ve)
            for ss in range(DST):
                g.tensor_mul(dA[:, ss * TC:(ss + 1) * TC],
                             hh[:, ss * TC:(ss + 1) * TC],
                             Cb[:, ss * TC:(ss + 1) * TC])
            kb.fence(v, kb.ge)
            half = DST // 2
            while half >= 1:
                v.tensor_add(dA[:, 0:half * TC], dA[:, 0:half * TC],
                             dA[:, half * TC:2 * half * TC])
                half //= 2
            kb.fence(v, kb.st)
            v.scalar_tensor_tensor(y_st[:, buf * TC:(buf + 1) * TC],
                                   xcb_, Dv[:, ec:ec + 1], dA[:, 0:TC],
                                   ALU.mult, ALU.add)
            kb.fence(nc.sync, kb.ve)
            kb.store(o_y2[ec * 128:(ec + 1) * 128, tc * TC:(tc + 1) * TC],
                     y_st[:, buf * TC:(buf + 1) * TC])
    return kb.finish()


def p2_inmaps(inputs, p1_res):
    A_log = inputs["L0_attn_A_log"]
    Dvec = inputs["L0_attn_D"]
    maps = []
    for c in range(NCORE):
        b, sl = c // 4, c % 4
        dts = np.concatenate([p1_res[4 * b + j]["o_dt"] for j in range(4)], 1)
        xcs = np.concatenate([p1_res[4 * b + j]["o_xconv"] for j in range(4)], 1)
        Bsf = np.concatenate([p1_res[4 * b + j]["o_B"] for j in range(4)], 1)
        Csf = np.concatenate([p1_res[4 * b + j]["o_C"] for j in range(4)], 1)
        rows = slice(sl * 512, (sl + 1) * 512)
        maps.append(dict(
            dts=np.ascontiguousarray(dts[rows]),
            xcs=np.ascontiguousarray(xcs[rows]),
            A_log_s=np.ascontiguousarray(A_log[rows]),
            D_s=chunk_fm(Dvec[rows][:, None]),
            Bsf=np.ascontiguousarray(Bsf),
            Csf=np.ascontiguousarray(Csf),
        ))
    return maps


TH = HALO + TOK
OW = TOK + 1


def tiles_of(total, off=0, wmax=512):
    out, p = [], off
    while p < off + total:
        w = min(wmax, off + total - p)
        out.append((p, w))
        p += w
    return out


def build_attn(debug=False):
    kb = KB()
    nc = kb.nc
    xB = kb.inp("xB", [128, 8 * TH])
    wqkv = kb.inp("wqkv", [128, 8 * 3072])
    wo = kb.inp("wo", [128, 8 * 1024])
    qsv = kb.inp("qsv", [128, 1])
    ksv = kb.inp("ksv", [128, 1])
    csq8 = kb.inp("csq8", [128, TH])
    ssq8 = kb.inp("ssq8", [128, TH])
    csk = kb.inp("csk", [128, TH])
    ssk = kb.inp("ssk", [128, TH])
    maskg = kb.inp("maskg", [128, 4 * 256], BF16)
    mask0 = kb.inp("mask0", [128, 4 * 256], BF16)
    ident = kb.inp("ident", [128, 128], BF16)
    sel33b = kb.inp("sel33b", [128, 33], BF16)
    selB33 = kb.inp("selB33", [33, 128])
    ones1 = kb.inp("ones1", [1, 128])
    onesb = kb.inp("onesb", [128, 1], BF16)
    onesr = kb.inp("onesr", [1, 128], BF16)
    identf = kb.inp("identf", [128, 128])

    o_xA = kb.out("o_xA", [128, 8 * OW])
    o_dbg1 = kb.out("o_dbg1", [128, TH]) if debug else None
    o_dbg2 = kb.out("o_dbg2", [128, TH]) if debug else None

    kb.open()
    v, s, g, t = kb.v, kb.s, kb.g, kb.t

    idt = kb.sb([128, 128], BF16, "idt")
    qsb = kb.sb([128, 1], F32, "qsb")
    ksb = kb.sb([128, 1], F32, "ksb")
    sel33b_t = kb.sb([128, 33], BF16, "sel33")
    selB33_t = kb.sb([33, 128], F32, "selB")
    ones1_t = kb.sb([1, 128], F32, "ones1")
    onesb_t = kb.sb([128, 1], BF16, "onesb")
    onesr_t = kb.sb([1, 128], BF16, "onesr")
    maskg_t = kb.sb([128, 4 * 256], BF16, "mg")
    mask0_t = kb.sb([128, 4 * 256], BF16, "m0")

    xqb = kb.sb([128, 8 * TH], BF16, "xqb")
    x_f = kb.sb([128, TH], F32, "xf")
    t1 = kb.sb([128, TH], F32, "t1")
    qpb = kb.sb([128, TH], BF16, "qpb")
    kpb = kb.sb([128, TH], BF16, "kpb")
    vpb = kb.sb([128, TH], BF16, "vpb")
    V_tm = kb.sb([128, 20 * 128], BF16, "vtm")
    O_all = kb.sb([128, 8 * OW], BF16, "oall")
    hilo = kb.sb([128, 2 * TH], BF16, "hilo")
    rnr = kb.sb([33, TH], F32, "rnr")
    mcol = kb.sb([128, 36], F32, "mcol")
    msb = kb.sb([36, 128], F32, "msb")
    negrow = kb.sb([1, 2 * 2304], F32, "negrow")
    mrow_d = kb.nc.dram_tensor("mrow_d", [36, 128], F32)
    idtf = kb.sb([128, 128], F32, "idtf")
    lrow = kb.sb([1, 256], F32, "lrow")
    lbv = kb.sb([64, 256], F32, "lbv")
    Pt = kb.sb([128, 4 * 256], BF16, "pt")
    wpart = kb.sb([128, 2 * 8 * 128], BF16, "wp")
    tbl = kb.sb([128, 2 * 512], F32, "tbl")
    xbt = kb.sb([128, 2 * 512], F32, "xbt")
    xa_st = kb.sb([128, 2 * 512], F32, "xast")

    psf = [kb.ps([128, 512], F32, "psf") for _ in range(7)]
    psb = kb.ps([128, 1024], BF16, "psb")
    bank_ready = [None] * 8

    def bank_wait(bk):
        if bank_ready[bk] is not None:
            c, val = bank_ready[bk]
            t.wait_ge(c.sem, val)

    kb.load(idt[:], ident)
    kb.load(qsb[:], qsv)
    kb.load(ksb[:], ksv)
    kb.load(sel33b_t[:], sel33b)
    kb.load(selB33_t[:], selB33)
    kb.load(ones1_t[:], ones1)
    kb.load(onesb_t[:], onesb)
    kb.load(onesr_t[:], onesr)
    kb.load(idtf[:], identf)
    kb.load(maskg_t[:], maskg)
    kb.load(mask0_t[:], mask0)
    kb.load(xqb[:], xB, gp=True)
    kb.fence(v, kb.ld)
    v.memset(rnr[:], 1.0)
    v.memset(O_all[:], 0.0)
    v.memset(x_f[:, 0:2], 0.0)
    v.memset(t1[:, 0:2], 0.0)

    qkv_tiles = tiles_of(2560, 2)

    def rhs_slice(k, c0, w):
        off = k * TH + c0 - (1 if k >= 4 else 0)
        return xqb[:, off:off + w]

    def qk_prep(part, hp):
        csT, ssT = (csq8, ssq8) if part == "q" else (csk, ssk)
        scl = qsb if part == "q" else ksb
        dstb = qpb if part == "q" else kpb
        if debug and hp == 7 and part == "k":
            kb.fence(nc.sync, kb.ve)
            kb.store(o_dbg1, x_f[:])
        kb.fence(s, kb.ve)
        s.activation(t1[:], x_f[:], AF.Square)
        kb.fence(v, kb.se)
        v.tensor_copy(hilo[:, 0:TH], t1[:])
        v.tensor_sub(hilo[:, TH:2 * TH], t1[:], hilo[:, 0:TH])
        kb.fence(t, kb.ve)
        for j, (c0, w) in enumerate(qkv_tiles):
            bank_wait(3)
            t.matmul(psf[3][0:33, 0:w], sel33b_t[:], hilo[:, c0:c0 + w],
                     start=True, stop=False)
            mi = t.matmul(psf[3][0:33, 0:w], sel33b_t[:],
                          hilo[:, TH + c0:TH + c0 + w],
                          start=False, stop=True)
            kb.pe.tick(mi)
            kb.fence(v, kb.pe)
            v.tensor_copy(rnr[0:33, c0:c0 + w], psf[3][0:33, 0:w])
            bank_ready[3] = kb.ve.mark()
        v.tensor_scalar_max(rnr[:], rnr[:], 1e-24)
        kb.fence(s, kb.ve)
        s.activation(rnr[:], rnr[:], AF.Ln)
        kb.fence(v, kb.se)
        v.tensor_scalar_mul(rnr[:], rnr[:], -0.5)
        kb.fence(s, kb.ve)
        s.activation(rnr[:], rnr[:], AF.Exp)
        kb.fence(t, kb.se)
        for j, (c0, w) in enumerate(qkv_tiles):
            bank_wait(2)
            mi = t.matmul(psf[2][:, 0:w], selB33_t[:], rnr[:, c0:c0 + w],
                          start=True, stop=True)
            kb.pe.tick(mi)
            kb.fence(v, kb.pe)
            v.scalar_tensor_tensor(t1[:, c0:c0 + w], x_f[:, c0:c0 + w],
                                   scl[:, 0:1], psf[2][:, 0:w],
                                   ALU.mult, ALU.mult)
            bank_ready[2] = kb.ve.mark()
        if debug and hp == 7 and part == "k":
            kb.fence(nc.sync, kb.ve)
            kb.store(o_dbg2, t1[:])
        for (dst0, src0, sgn) in ((0, 32, -1.0), (32, 0, 1.0),
                                  (64, 96, -1.0), (96, 64, 1.0)):
            v.tensor_scalar_mul(x_f[dst0:dst0 + 32, :],
                                t1[src0:src0 + 32, :], sgn)
        for j, (c0, w) in enumerate(qkv_tiles):
            bf = (j % 2) * 512
            kb.fence(nc.sync, kb.ve)
            kb.load(tbl[:, bf:bf + w], csT[:, c0:c0 + w])
            kb.fence(v, kb.ld)
            v.tensor_mul(t1[:, c0:c0 + w], t1[:, c0:c0 + w], tbl[:, bf:bf + w])
            kb.fence(nc.sync, kb.ve)
            kb.load(tbl[:, bf:bf + w], ssT[:, c0:c0 + w])
            kb.fence(v, kb.ld)
            v.tensor_mul(x_f[:, c0:c0 + w], x_f[:, c0:c0 + w],
                         tbl[:, bf:bf + w])
            v.tensor_add(t1[:, c0:c0 + w], t1[:, c0:c0 + w],
                         x_f[:, c0:c0 + w])
        v.tensor_copy(dstb[:], t1[:])
        return

    for hp in range(8):
        for i3, part in enumerate(("q", "k", "v")):
            wslc = wqkv.rearrange("p (k e) -> p k e", k=8)[
                :, :, i3 * 1024 + hp * 128: i3 * 1024 + (hp + 1) * 128]
            kb.fence(nc.gpsimd, kb.pe)
            kb.load(wpart[:, 0:8 * 128].rearrange("p (k e) -> p k e", k=8),
                    wslc, gp=True)
            kb.fence(t, kb.gq, kb.ve, kb.se)
            for j, (c0, w) in enumerate(qkv_tiles):
                bk = j % 2
                bank_wait(bk)
                for k in range(8):
                    mi = t.matmul(psf[bk][:, 0:w],
                                  wpart[:, k * 128:(k + 1) * 128],
                                  rhs_slice(k, c0, w),
                                  start=(k == 0), stop=(k == 7))
                kb.pe.tick(mi)
                if part == "v":
                    kb.fence(s, kb.pe)
                    s.copy(vpb[:, c0:c0 + w], psf[bk][:, 0:w])
                    bank_ready[bk] = kb.se.mark()
                else:
                    kb.fence(v, kb.pe)
                    v.tensor_copy(x_f[:, c0:c0 + w], psf[bk][:, 0:w])
                    bank_ready[bk] = kb.ve.mark()
            if part != "v":
                qk_prep(part, hp)
        kb.fence(t, kb.se)
        for jt in range(20):
            kb.pe.tick(t.transpose(psb[:, 0:128],
                                   vpb[:, 2 + 128 * jt: 2 + 128 * (jt + 1)],
                                   idt[:]))
            kb.fence(v, kb.pe)
            v.tensor_copy(V_tm[:, 128 * jt:128 * (jt + 1)], psb[:, 0:128])
            kb.fence(t, kb.ve)
        for h in (0, 1):
            hb = 64 * h
            for qt in range(18):
                qt0 = 258 + 128 * qt
                k0 = qt0 - 256
                bank_wait(2)
                mi = t.matmul(psf[2][:, 0:384],
                              qpb[hb:hb + 64, qt0:qt0 + 128],
                              kpb[hb:hb + 64, k0:k0 + 384],
                              start=True, stop=True)
                kb.pe.tick(mi)
                kb.fence(v, kb.pe)
                v.reduce_max(mcol[:, h * 18 + qt:h * 18 + qt + 1],
                             psf[2][:, 0:384], axis=AX.X, negate=True)
                bank_ready[2] = kb.ve.mark()
        kb.fence(t, kb.ve)
        bank_wait(2)
        kb.pe.tick(t.transpose(psf[2][0:36, 0:128], mcol[:], idtf[0:128, 0:128]))
        kb.fence(v, kb.pe)
        v.tensor_copy(msb[:], psf[2][0:36, 0:128])
        bank_ready[2] = kb.ve.mark()
        kb.fence(nc.sync, kb.ve, kb.ld)
        kb.store(mrow_d.ap(), msb[:])
        kb.fence(nc.sync, kb.st)
        kb.load(negrow[:], mrow_d.ap().rearrange("a b -> (a b)").unsqueeze(0))
        kb.fence(t, kb.ld)
        for wi in range(9):
            qc0 = 258 + 256 * wi
            kc0 = qc0 - 256
            mask_t = mask0_t if wi == 1 else maskg_t
            for h in (0, 1):
                hb, hr = 64 * h, 32 * h
                for kt in range(4):
                    bk = 4 + kt // 2
                    co = (kt % 2) * 256
                    if kt % 2 == 0:
                        bank_wait(bk)
                    t.matmul(psf[bk][:, co:co + 256],
                             kpb[hb:hb + 64, kc0 + 128 * kt:kc0 + 128 * (kt + 1)],
                             qpb[hb:hb + 64, qc0:qc0 + 256],
                             start=True, stop=False)
                    mi = t.matmul(psf[bk][:, co:co + 256],
                                  ones1_t[0:1, :],
                                  negrow[0:1, h * 2304 + 256 * wi:
                                         h * 2304 + 256 * (wi + 1)],
                                  start=False, stop=True)
                    kb.pe.tick(mi)
                kb.fence(s, kb.pe)
                s.activation(Pt[:, 0:512], psf[4][:], AF.Exp)
                s.activation(Pt[:, 512:1024], psf[5][:], AF.Exp)
                bank_ready[4] = kb.se.mark()
                bank_ready[5] = kb.se.mark()
                kb.fence(g, kb.se)
                g.tensor_mul(Pt[:], Pt[:], mask_t[:])
                kb.fence(t, kb.ge)
                jt0 = (kc0 - 2) // 128
                bank_wait(6)
                bank_wait(3)
                for kt in range(4):
                    t.matmul(psf[6][0:64, 0:256],
                             V_tm[:, 128 * (jt0 + kt) + hb:
                                  128 * (jt0 + kt) + hb + 64],
                             Pt[:, 256 * kt:256 * (kt + 1)],
                             start=(kt == 0), stop=(kt == 3))
                    mi = t.matmul(psf[3][0:1, 0:256], onesb_t[:],
                                  Pt[:, 256 * kt:256 * (kt + 1)],
                                  start=(kt == 0), stop=(kt == 3))
                kb.pe.tick(mi)
                kb.fence(v, kb.pe)
                v.tensor_copy(lrow[:], psf[3][0:1, 0:256])
                bank_ready[3] = kb.ve.mark()
                kb.fence(s, kb.ve)
                s.activation(lrow[:], lrow[:], AF.Ln)
                s.activation(lrow[:], lrow[:], AF.Exp, scale=-1.0)
                kb.fence(t, kb.se)
                bank_wait(2)
                mi = t.matmul(psf[2][0:64, 0:256], ones1_t[0:1, 0:64],
                              lrow[:], start=True, stop=True)
                kb.pe.tick(mi)
                kb.fence(v, kb.pe)
                v.tensor_copy(lbv[:], psf[2][0:64, 0:256])
                bank_ready[2] = kb.ve.mark()
                if wi == 0:
                    v.tensor_mul(O_all[hb:hb + 64, hp * OW:hp * OW + 1],
                                 psf[6][0:64, 255:256], lbv[:, 255:256])
                else:
                    oc0 = hp * OW + 1 + 256 * (wi - 1)
                    v.tensor_mul(O_all[hb:hb + 64, oc0:oc0 + 256],
                                 psf[6][0:64, 0:256], lbv[:])
                bank_ready[6] = kb.ve.mark()

    ow_tiles = tiles_of(OW, 0)
    for oc in range(8):
        wslc = wo.rearrange("p (k e) -> p k e", k=8)[
            :, :, oc * 128:(oc + 1) * 128]
        kb.fence(nc.gpsimd, kb.pe)
        kb.load(wpart[:, 0:8 * 128].rearrange("p (k e) -> p k e", k=8),
                wslc, gp=True)
        kb.fence(t, kb.gq)
        for j, (c0, w) in enumerate(ow_tiles):
            bk = j % 2
            bf = (j % 2) * 512
            kb.fence(nc.sync, kb.ve)
            kb.load(xbt[:, bf:bf + w], xB[:, oc * TH + 513 + c0:
                                          oc * TH + 513 + c0 + w],
                    slow=(w < 8))
            bank_wait(bk)
            for k in range(8):
                mi = t.matmul(psf[bk][:, 0:w],
                              wpart[:, k * 128:(k + 1) * 128],
                              O_all[:, k * OW + c0:k * OW + c0 + w],
                              start=(k == 0), stop=(k == 7))
            kb.pe.tick(mi)
            kb.fence(v, kb.pe, kb.ld, kb.st)
            v.tensor_add(xa_st[:, bf:bf + w], psf[bk][:, 0:w],
                         xbt[:, bf:bf + w])
            bank_ready[bk] = kb.ve.mark()
            kb.fence(nc.sync, kb.ve)
            kb.store(o_xA[:, oc * OW + c0:oc * OW + c0 + w],
                     xa_st[:, bf:bf + w], slow=(w < 8))
    return kb.finish()


def _rotary_tables():
    j = np.arange(32)
    inv_freq = 1.0 / 10000.0 ** (2 * j / 64.0)
    base = (2 * j + 0.4 * 64) / (1.4 * 64)
    tpos = np.arange(-HALO, T, dtype=np.float64)
    fr = np.outer(inv_freq, tpos)
    sc = base[:, None] ** ((tpos[None, :] - T // 2) / 128.0)
    cos = np.cos(fr)
    sin = np.sin(fr)
    d2 = lambda a: np.concatenate([a, a], 0)
    csq8 = d2(8.0 * cos * sc).astype(np.float32)
    ssq8 = d2(8.0 * sin * sc).astype(np.float32)
    csk = d2(cos / sc).astype(np.float32)
    ssk = d2(sin / sc).astype(np.float32)
    dup = lambda a: np.ascontiguousarray(np.tile(a, (2, 1)))
    return tuple(dup(a) for a in (csq8, ssq8, csk, ssk))


_TABLES = None


def rotary_tables():
    global _TABLES
    if _TABLES is None:
        _TABLES = _rotary_tables()
    return _TABLES


def _mask_tiles(zero_prev):
    m = np.zeros((128, 4 * 256), dtype=BF)
    for kt in range(4):
        for p in range(128):
            c = kt * 128 + p
            r = np.arange(256)
            ok = (r <= c) & (c <= r + 256)
            if zero_prev and kt < 2:
                ok = ok & False
            m[p, kt * 256:(kt + 1) * 256] = ok.astype(BF)
    return m


_SEL = None


def attn_consts():
    global _SEL
    if _SEL is None:
        sel33b = np.zeros((128, 33), dtype=BF)
        sel33b[0:64, 0] = 1
        sel33b[64:128, 32] = 1
        selB33 = np.zeros((33, 128), dtype=np.float32)
        selB33[0, 0:64] = 1
        selB33[32, 64:128] = 1
        ones1 = np.ones((1, 128), dtype=np.float32)
        onesb = np.ones((128, 1), dtype=BF)
        onesr = np.ones((1, 128), dtype=BF)
        maskg = _mask_tiles(False)
        mask0 = _mask_tiles(True)
        _SEL = dict(sel33b=sel33b, selB33=selB33, ones1=ones1, onesb=onesb,
                    onesr=onesr, maskg=maskg, mask0=mask0, ident=_IDENT,
                    identf=np.eye(128, dtype=np.float32))
    return _SEL


def slice_tok_cm(x_full, b, g0, lead, width):
    xs = np.zeros((width, D), np.float32)
    lo = g0 - lead
    src0 = max(0, lo)
    src1 = min(T, lo + width)
    xs[src0 - lo:src1 - lo, :] = x_full[b, src0:src1]
    return chunk_fm(np.ascontiguousarray(xs.T))


def attn_inmaps(inputs, layer, xB_full):
    cst = attn_consts()
    csq8, ssq8, csk, ssk = rotary_tables()
    wqkv = chunk_fm(inputs["L0_ffn_Wqkv"][layer])
    wo = chunk_fm(inputs["L0_ffn_Wo"][layer])
    qsv = np.ascontiguousarray(
        np.tile(inputs["L0_ffn_q_scale"][layer], 2)[:, None])
    ksv = np.ascontiguousarray(
        np.tile(inputs["L0_ffn_k_scale"][layer], 2)[:, None])
    maps = []
    for c in range(NCORE):
        b, ch = c // 4, c % 4
        g0 = ch * TOK
        cols = slice(g0 - 514 + HALO, g0 + TOK + HALO)
        maps.append(dict(
            xB=slice_tok_cm(xB_full, b, g0, 514, TH),
            wqkv=wqkv, wo=wo, qsv=qsv, ksv=ksv,
            csq8=np.ascontiguousarray(csq8[:, cols]),
            ssq8=np.ascontiguousarray(ssq8[:, cols]),
            csk=np.ascontiguousarray(csk[:, cols]),
            ssk=np.ascontiguousarray(ssk[:, cols]),
            maskg=cst["maskg"],
            mask0=cst["mask0"] if ch == 0 else cst["maskg"],
            ident=cst["ident"], sel33b=cst["sel33b"], selB33=cst["selB33"],
            ones1=cst["ones1"], onesb=cst["onesb"], onesr=cst["onesr"],
            identf=cst["identf"],
        ))
    return maps


HID = 2816


def build_ffn():
    kb = KB()
    nc = kb.nc
    xA = kb.inp("xA", [128, 8 * OW])
    w1a = kb.inp("w1a", [128, 8 * HID])
    w1g = kb.inp("w1g", [128, 8 * HID])
    w2 = kb.inp("w2", [128, 22 * 1024])
    b1a = kb.inp("b1a", [128, 22])
    b1g = kb.inp("b1g", [128, 22])
    b2v = kb.inp("b2v", [128, 8])
    lng = kb.inp("lng", [128, 8])
    lnb = kb.inp("lnb", [128, 8])
    ones1 = kb.inp("ones1", [1, 128])
    onesbf = kb.inp("onesbf", [128, 1], BF16)

    o_xB = kb.out("o_xB", [128, 8 * TOK])

    kb.open()
    v, s, g, t = kb.v, kb.s, kb.g, kb.t

    xA_f = kb.sb([128, 8 * OW], F32, "xaf")
    xnf = kb.sb([128, 8 * TOK], BF16, "xnf")
    hilo = kb.sb([128, 4 * 512], BF16, "hilo")
    sq = kb.sb([128, 512], F32, "sq")
    rows = kb.sb([1, 4 * TOK], F32, "rows")
    hbuf = kb.sb([128, 22 * 512], BF16, "hbuf")
    gl = kb.sb([128, 512], F32, "gl")
    tmp = kb.sb([128, 512], F32, "tmp")
    wca = kb.sb([128, 8 * 128], BF16, "wca")
    wcg = kb.sb([128, 8 * 128], BF16, "wcg")
    wc2 = kb.sb([128, 1024], BF16, "wc2")
    b1a_t = kb.sb([128, 22], F32, "b1at")
    b1g_t = kb.sb([128, 22], F32, "b1gt")
    b2_t = kb.sb([128, 8], F32, "b2t")
    lng_t = kb.sb([128, 8], F32, "lngt")
    lnb_t = kb.sb([128, 8], F32, "lnbt")
    ones1_t = kb.sb([1, 128], F32, "ones1")
    onesbf_t = kb.sb([128, 1], BF16, "onesbf")
    xst = kb.sb([128, 2 * 512], F32, "xst")

    psf = [kb.ps([128, 512], F32, "psf") for _ in range(8)]
    bank_ready = [None] * 8

    def bank_wait(bk):
        if bank_ready[bk] is not None:
            c, val = bank_ready[bk]
            t.wait_ge(c.sem, val)

    kb.load(xA_f[:], xA)
    kb.load(b1a_t[:], b1a)
    kb.load(b1g_t[:], b1g)
    kb.load(b2_t[:], b2v)
    kb.load(lng_t[:], lng)
    kb.load(lnb_t[:], lnb)
    kb.load(ones1_t[:], ones1)
    kb.load(onesbf_t[:], onesbf)

    def xsh(c, c0, w):
        off = c * OW + c0 + (1 if c < 4 else 0)
        return xA_f[:, off:off + w]

    t_tiles = tiles_of(TOK, 0)
    kb.fence(v, kb.ld)
    kb.fence(s, kb.ld)
    for (c0, w) in t_tiles:
        bank_wait(0)
        bank_wait(1)
        for c in range(8):
            v.tensor_copy(hilo[:, 0:w], xsh(c, c0, w))
            v.tensor_sub(hilo[:, w:2 * w], xsh(c, c0, w), hilo[:, 0:w])
            kb.fence(s, kb.ve)
            s.activation(sq[:, 0:w], xsh(c, c0, w), AF.Square)
            kb.fence(v, kb.se)
            v.tensor_copy(hilo[:, 2 * w:3 * w], sq[:, 0:w])
            v.tensor_sub(hilo[:, 3 * w:4 * w], sq[:, 0:w], hilo[:, 2 * w:3 * w])
            kb.fence(t, kb.ve)
            t.matmul(psf[0][0:1, 0:w], onesbf_t[:], hilo[:, 0:w],
                     start=(c == 0), stop=False)
            mi = t.matmul(psf[0][0:1, 0:w], onesbf_t[:], hilo[:, w:2 * w],
                          start=False, stop=(c == 7))
            kb.pe.tick(mi)
            t.matmul(psf[1][0:1, 0:w], onesbf_t[:], hilo[:, 2 * w:3 * w],
                     start=(c == 0), stop=False)
            mi = t.matmul(psf[1][0:1, 0:w], onesbf_t[:], hilo[:, 3 * w:4 * w],
                          start=False, stop=(c == 7))
            kb.pe.tick(mi)
            if c < 7:
                kb.fence(v, kb.pe)
        kb.fence(v, kb.pe)
        v.tensor_copy(rows[0:1, c0:c0 + w], psf[0][0:1, 0:w])
        v.tensor_copy(rows[0:1, TOK + c0:TOK + c0 + w], psf[1][0:1, 0:w])
        bank_ready[0] = kb.ve.mark()
        bank_ready[1] = kb.ve.mark()
    v.tensor_scalar_mul(rows[0:1, 2 * TOK:3 * TOK], rows[0:1, 0:TOK],
                        -1.0 / D)
    v.tensor_mul(rows[0:1, 0:TOK], rows[0:1, 2 * TOK:3 * TOK],
                 rows[0:1, 2 * TOK:3 * TOK])
    v.scalar_tensor_tensor(rows[0:1, 3 * TOK:4 * TOK],
                           rows[0:1, TOK:2 * TOK], 1.0 / D,
                           rows[0:1, 0:TOK], ALU.mult, ALU.subtract)
    v.tensor_scalar_add(rows[0:1, 3 * TOK:4 * TOK],
                        rows[0:1, 3 * TOK:4 * TOK], 1e-5)
    kb.fence(s, kb.ve)
    s.activation(rows[0:1, 3 * TOK:4 * TOK], rows[0:1, 3 * TOK:4 * TOK],
                 AF.Ln)
    kb.fence(v, kb.se)
    v.tensor_scalar_mul(rows[0:1, 3 * TOK:4 * TOK],
                        rows[0:1, 3 * TOK:4 * TOK], -0.5)
    kb.fence(s, kb.ve)
    s.activation(rows[0:1, 3 * TOK:4 * TOK], rows[0:1, 3 * TOK:4 * TOK],
                 AF.Exp)
    kb.fence(t, kb.se)
    for c in range(8):
        for j, (c0, w) in enumerate(t_tiles):
            bank_wait(2)
            mi = t.matmul(psf[2][:, 0:w], ones1_t[:],
                          rows[0:1, 2 * TOK + c0:2 * TOK + c0 + w],
                          start=True, stop=True)
            kb.pe.tick(mi)
            bank_wait(3)
            mi = t.matmul(psf[3][:, 0:w], ones1_t[:],
                          rows[0:1, 3 * TOK + c0:3 * TOK + c0 + w],
                          start=True, stop=True)
            kb.pe.tick(mi)
            kb.fence(v, kb.pe)
            v.tensor_add(tmp[:, 0:w], xsh(c, c0, w), psf[2][:, 0:w])
            bank_ready[2] = kb.ve.mark()
            v.tensor_mul(tmp[:, 0:w], tmp[:, 0:w], psf[3][:, 0:w])
            bank_ready[3] = kb.ve.mark()
            v.tensor_scalar(xnf[:, c * TOK + c0:c * TOK + c0 + w],
                            tmp[:, 0:w], lng_t[:, c:c + 1],
                            lnb_t[:, c:c + 1], ALU.mult, ALU.add)

    for th in range(4):
        tc0 = th * 512
        for hc in range(22):
            kb.fence(nc.gpsimd, kb.pe)
            kb.load(wca[:].rearrange("p (k e) -> p k e", k=8),
                    w1a.rearrange("p (k e) -> p k e", k=8)
                    [:, :, hc * 128:(hc + 1) * 128], gp=True)
            kb.load(wcg[:].rearrange("p (k e) -> p k e", k=8),
                    w1g.rearrange("p (k e) -> p k e", k=8)
                    [:, :, hc * 128:(hc + 1) * 128], gp=True)
            kb.fence(t, kb.gq, kb.ve)
            bank_wait(0)
            bank_wait(1)
            for k in range(8):
                t.matmul(psf[0][:], wca[:, k * 128:(k + 1) * 128],
                         xnf[:, k * TOK + tc0:k * TOK + tc0 + 512],
                         start=(k == 0), stop=(k == 7))
                mi = t.matmul(psf[1][:], wcg[:, k * 128:(k + 1) * 128],
                              xnf[:, k * TOK + tc0:k * TOK + tc0 + 512],
                              start=(k == 0), stop=(k == 7))
            kb.pe.tick(mi)
            kb.fence(s, kb.pe)
            s.activation(gl[:], psf[1][:], AF.Gelu, bias=b1g_t[:, hc:hc + 1])
            bank_ready[1] = kb.se.mark()
            kb.fence(v, kb.se)
            v.scalar_tensor_tensor(hbuf[:, hc * 512:(hc + 1) * 512],
                                   psf[0][:], b1a_t[:, hc:hc + 1], gl[:],
                                   ALU.add, ALU.mult)
            bank_ready[0] = kb.ve.mark()
        for bk in range(8):
            bank_wait(bk)
        for hc in range(22):
            kb.fence(nc.gpsimd, kb.pe)
            kb.load(wc2[:], w2[:, hc * 1024:(hc + 1) * 1024], gp=True)
            kb.fence(t, kb.gq)
            for oc in range(8):
                mi = t.matmul(psf[oc][:], wc2[:, oc * 128:(oc + 1) * 128],
                              hbuf[:, hc * 512:(hc + 1) * 512],
                              start=(hc == 0), stop=(hc == 21))
            kb.pe.tick(mi)
        for oc in range(8):
            kb.fence(v, kb.pe, kb.st)
            v.scalar_tensor_tensor(xst[:, (oc % 2) * 512:(oc % 2) * 512 + 512],
                                   psf[oc][:], b2_t[:, oc:oc + 1],
                                   xA_f[:, oc * OW + 1 + tc0:
                                        oc * OW + 1 + tc0 + 512],
                                   ALU.add, ALU.add)
            bank_ready[oc] = kb.ve.mark()
            kb.fence(nc.sync, kb.ve)
            kb.store(o_xB[:, oc * TOK + tc0:oc * TOK + tc0 + 512],
                     xst[:, (oc % 2) * 512:(oc % 2) * 512 + 512])
    return kb.finish()


def ffn_inmaps(inputs, layer, xA_cms):
    W1 = inputs["L0_ffn_W1"][layer]
    b1 = inputs["L0_ffn_b1"][layer]
    W2 = inputs["L0_ffn_W2"][layer]
    w1a = np.zeros((D, HID), np.float32)
    w1g = np.zeros((D, HID), np.float32)
    w1a[:, 0:2730] = W1[:, 0:2730]
    w1g[:, 0:2730] = W1[:, 2730:5460]
    b1a = np.zeros((HID, 1), np.float32)
    b1g = np.zeros((HID, 1), np.float32)
    b1a[0:2730, 0] = b1[0:2730]
    b1g[0:2730, 0] = b1[2730:5460]
    w2p = np.zeros((HID, 1024), np.float32)
    w2p[0:2730] = W2
    cst = attn_consts()
    shared = dict(
        w1a=chunk_fm(w1a), w1g=chunk_fm(w1g), w2=chunk_fm(w2p),
        b1a=chunk_fm(b1a), b1g=chunk_fm(b1g),
        b2v=chunk_fm(inputs["L0_ffn_b2"][layer][:, None]),
        lng=chunk_fm(inputs["L0_ffn_ln_g"][layer][:, None]),
        lnb=chunk_fm(inputs["L0_ffn_ln_b"][layer][:, None]),
        ones1=cst["ones1"],
        onesbf=np.ones((128, 1), dtype=BF),
    )
    return [dict(xA=xA_cms[c], **shared) for c in range(NCORE)]


def build_p8a(debug=False):
    kb = KB()
    nc = kb.nc
    xn0 = kb.inp("xn0", [128, 8 * TOK])
    sz = kb.inp("sz", [128, 16 * TOK])
    y2 = kb.inp("y2", [128, 16 * TOK])
    xorig = kb.inp("xorig", [128, 8 * TOK])
    xB6 = kb.inp("xB6", [128, 8 * TOK])
    w_op = kb.inp("w_op", [128, 16 * 1024])
    w_cfc = kb.inp("w_cfc", [128, 8 * EI])
    w_cpj = kb.inp("w_cpj", [128, 16 * 1024])
    ident = kb.inp("ident", [128, 128], BF16)
    onesbf = kb.inp("onesbf", [128, 1], BF16)
    ones1 = kb.inp("ones1", [1, 128])

    o_x1 = kb.out("o_x1", [128, 8 * TOK])
    o_dbg1 = kb.out("o_dbg1", [128, 8 * TOK]) if debug else None
    o_dbg2 = kb.out("o_dbg2", [128, 8 * TOK], BF16) if debug else None

    kb.open()
    v, s, g, t = kb.v, kb.s, kb.g, kb.t
    idt = kb.sb([128, 128], BF16, "idt")
    onesb_t = kb.sb([128, 1], BF16, "onesb")
    ones1_t = kb.sb([1, 128], F32, "ones1")
    x1i = kb.sb([128, 8 * TOK], F32, "x1i")
    yqb = kb.sb([128, 16 * TOK], BF16, "yqb")
    hrelu = kb.sb([128, 16 * 512], BF16, "hrelu")
    wbuf = kb.sb([128, 16 * 128], BF16, "wbuf")
    stage = kb.sb([128, 2 * 512], F32, "stage")
    bstage = kb.sb([128, 2 * 512], BF16, "bstage")
    hilo = kb.sb([128, 4 * 512], BF16, "hilo")
    sq = kb.sb([128, 512], F32, "sq")
    rows = kb.sb([1, TOK], F32, "rows")
    strm = kb.sb([128, 2 * 512], F32, "strm")
    strm2 = kb.sb([128, 2 * 512], F32, "strm2")

    psf = [kb.ps([128, 512], F32, "psf") for _ in range(7)]
    psb = kb.ps([128, 1024], BF16, "psb")
    bank_ready = [None] * 8

    def bank_wait(bk):
        if bank_ready[bk] is not None:
            c, val = bank_ready[bk]
            t.wait_ge(c.sem, val)

    kb.load(idt[:], ident)
    kb.load(onesb_t[:], onesbf)
    kb.load(ones1_t[:], ones1)
    t_tiles = tiles_of(TOK, 0)

    def wslice(wcm, K, oc, dst):
        kb.fence(nc.gpsimd, kb.pe)
        kb.load(dst[:, 0:K * 128].rearrange("p (k e) -> p k e", k=K),
                wcm.rearrange("p (k e) -> p k e", k=K)
                [:, :, oc * 128:(oc + 1) * 128], gp=True)
        kb.fence(t, kb.gq)

    kb.fence(v, kb.ld)
    for c in range(16):
        for j, (c0, w) in enumerate(t_tiles):
            bf = (j % 2) * 512
            kb.fence(nc.sync, kb.ge)
            kb.load(strm[:, bf:bf + w], y2[:, c * TOK + c0:c * TOK + c0 + w])
            kb.load(strm2[:, bf:bf + w], sz[:, c * TOK + c0:c * TOK + c0 + w])
            kb.fence(g, kb.ld)
            g.tensor_mul(yqb[:, c * TOK + c0:c * TOK + c0 + w],
                         strm[:, bf:bf + w], strm2[:, bf:bf + w])

    for oc in range(8):
        wslice(w_op, 16, oc, wbuf)
        for j, (c0, w) in enumerate(t_tiles):
            bk = j % 2
            bank_wait(bk)
            kb.fence(t, kb.ge) if (oc == 0 and j == 0) else None
            for k in range(16):
                mi = t.matmul(psf[bk][:, 0:w],
                              wbuf[:, k * 128:(k + 1) * 128],
                              yqb[:, k * TOK + c0:k * TOK + c0 + w],
                              start=(k == 0), stop=(k == 15))
            kb.pe.tick(mi)
            kb.fence(nc.sync, kb.ve)
            kb.load(strm[:, 0:w], xn0[:, oc * TOK + c0:oc * TOK + c0 + w])
            kb.fence(v, kb.pe, kb.ld)
            v.tensor_add(x1i[:, oc * TOK + c0:oc * TOK + c0 + w],
                         psf[bk][:, 0:w], strm[:, 0:w])
            bank_ready[bk] = kb.ve.mark()

    if debug:
        kb.fence(nc.sync, kb.ve)
        kb.store(o_dbg1, x1i[:])
    for (c0, w) in t_tiles:
        bank_wait(0)
        for c in range(8):
            kb.fence(s, kb.ve)
            s.activation(sq[:, 0:w], x1i[:, c * TOK + c0:c * TOK + c0 + w],
                         AF.Square)
            kb.fence(v, kb.se)
            v.tensor_copy(hilo[:, 0:w], sq[:, 0:w])
            v.tensor_sub(hilo[:, w:2 * w], sq[:, 0:w], hilo[:, 0:w])
            kb.fence(t, kb.ve)
            t.matmul(psf[0][0:1, 0:w], onesb_t[:], hilo[:, 0:w],
                     start=(c == 0), stop=False)
            mi = t.matmul(psf[0][0:1, 0:w], onesb_t[:], hilo[:, w:2 * w],
                          start=False, stop=(c == 7))
            kb.pe.tick(mi)
            if c < 7:
                kb.fence(v, kb.pe)
        kb.fence(v, kb.pe)
        v.tensor_copy(rows[0:1, c0:c0 + w], psf[0][0:1, 0:w])
        bank_ready[0] = kb.ve.mark()
    v.tensor_scalar_mul(rows[0:1, 0:TOK], rows[0:1, 0:TOK], 1.0 / D)
    v.tensor_scalar_add(rows[0:1, 0:TOK], rows[0:1, 0:TOK], F32EPS)
    kb.fence(s, kb.ve)
    s.activation(rows[0:1, 0:TOK], rows[0:1, 0:TOK], AF.Ln)
    kb.fence(v, kb.se)
    v.tensor_scalar_mul(rows[0:1, 0:TOK], rows[0:1, 0:TOK], -0.5)
    kb.fence(s, kb.ve)
    s.activation(rows[0:1, 0:TOK], rows[0:1, 0:TOK], AF.Exp)
    kb.fence(t, kb.se)
    for c in range(8):
        for j, (c0, w) in enumerate(t_tiles):
            bank_wait(2)
            mi = t.matmul(psf[2][:, 0:w], ones1_t[:], rows[0:1, c0:c0 + w],
                          start=True, stop=True)
            kb.pe.tick(mi)
            kb.fence(v, kb.pe)
            v.tensor_mul(yqb[:, c * TOK + c0:c * TOK + c0 + w],
                         x1i[:, c * TOK + c0:c * TOK + c0 + w],
                         psf[2][:, 0:w])
            bank_ready[2] = kb.ve.mark()

    if debug:
        kb.fence(nc.sync, kb.ve)
        kb.store(o_dbg2, yqb[:, 0:8 * TOK])
    for th in range(4):
        tc0 = th * 512
        for hc in range(16):
            wslice(w_cfc, 8, hc, wbuf)
            bank_wait(hc % 2)
            for k in range(8):
                mi = t.matmul(psf[hc % 2][:], wbuf[:, k * 128:(k + 1) * 128],
                              yqb[:, k * TOK + tc0:k * TOK + tc0 + 512],
                              start=(k == 0), stop=(k == 7))
            kb.pe.tick(mi)
            kb.fence(v, kb.pe, kb.se)
            v.tensor_scalar_max(sq[:], psf[hc % 2][:], 0.0)
            bank_ready[hc % 2] = kb.ve.mark()
            kb.fence(s, kb.ve)
            s.activation(hrelu[:, hc * 512:(hc + 1) * 512], sq[:], AF.Square)
        kb.fence(t, kb.se)
        for ocg in range(2):
            for bk in range(2, 6):
                bank_wait(bk)
            for hc in range(16):
                kb.fence(nc.gpsimd, kb.pe)
                kb.load(wbuf[:, 0:4 * 128].rearrange("p (k e) -> p k e", k=4),
                        w_cpj.rearrange("p (k e) -> p k e", k=16)
                        [:, hc, ocg * 512:(ocg + 1) * 512]
                        .rearrange("p (a e) -> p a e", a=4), gp=True)
                kb.fence(t, kb.gq)
                for oc4 in range(4):
                    mi = t.matmul(psf[2 + oc4][:],
                                  wbuf[:, oc4 * 128:(oc4 + 1) * 128],
                                  hrelu[:, hc * 512:(hc + 1) * 512],
                                  start=(hc == 0), stop=(hc == 15))
                kb.pe.tick(mi)
            for oc4 in range(4):
                oc = ocg * 4 + oc4
                kb.fence(nc.sync, kb.ve)
                kb.load(strm[:, 0:512],
                        xB6[:, oc * TOK + tc0:oc * TOK + tc0 + 512])
                kb.load(strm2[:, 0:512],
                        xorig[:, oc * TOK + tc0:oc * TOK + tc0 + 512])
                kb.fence(v, kb.pe, kb.ld, kb.st)
                v.tensor_add(stage[:, 0:512], psf[2 + oc4][:], strm[:, 0:512])
                bank_ready[2 + oc4] = kb.ve.mark()
                v.tensor_add(stage[:, 0:512], stage[:, 0:512], strm2[:, 0:512])
                v.tensor_add(x1i[:, oc * TOK + tc0:oc * TOK + tc0 + 512],
                             x1i[:, oc * TOK + tc0:oc * TOK + tc0 + 512],
                             stage[:, 0:512])
                kb.fence(nc.sync, kb.ve)
                kb.store(o_x1[:, oc * TOK + tc0:oc * TOK + tc0 + 512],
                         x1i[:, oc * TOK + tc0:oc * TOK + tc0 + 512])

    return kb.finish()


def build_p8b():
    kb = KB()
    nc = kb.nc
    x1in = kb.inp("x1in", [128, 8 * TOK])
    w_qkv1 = kb.inp("w_qkv1", [128, 8 * 3072])
    ident = kb.inp("ident", [128, 128], BF16)
    onesbf = kb.inp("onesbf", [128, 1], BF16)
    ones1 = kb.inp("ones1", [1, 128])

    o_xn1 = kb.out("o_xn1", [128, 8 * TOK], BF16)
    o_q = kb.out("o_q", [128, 8 * TOK], BF16)
    o_k = kb.out("o_k", [128, 8 * TOK], BF16)
    o_vtm = kb.out("o_vtm", [128, 16 * 1024], BF16)
    o_kmax = kb.out("o_kmax", [1, 16])

    kb.open()
    v, s, g, t = kb.v, kb.s, kb.g, kb.t
    idt = kb.sb([128, 128], BF16, "idt")
    onesb_t = kb.sb([128, 1], BF16, "onesb")
    ones1_t = kb.sb([1, 128], F32, "ones1")
    x1i = kb.sb([128, 8 * TOK], F32, "x1i")
    xn1b = kb.sb([128, 8 * TOK], BF16, "xn1b")
    wbuf = kb.sb([128, 16 * 128], BF16, "wbuf")
    stage = kb.sb([128, 2 * 512], F32, "stage")
    bstage = kb.sb([128, 2 * 512], BF16, "bstage")
    hilo = kb.sb([128, 4 * 512], BF16, "hilo")
    sq = kb.sb([128, 512], F32, "sq")
    rows = kb.sb([1, 4 * TOK], F32, "rows")
    kmaxr = kb.sb([1, 16], F32, "kmaxr")
    krow = kb.sb([1, 512], F32, "krow")
    psf = [kb.ps([128, 512], F32, "psf") for _ in range(7)]
    psb = kb.ps([128, 1024], BF16, "psb")
    bank_ready = [None] * 8

    def bank_wait(bk):
        if bank_ready[bk] is not None:
            c, val = bank_ready[bk]
            t.wait_ge(c.sem, val)

    kb.load(idt[:], ident)
    kb.load(onesb_t[:], onesbf)
    kb.load(ones1_t[:], ones1)
    kb.load(x1i[:], x1in)
    t_tiles = tiles_of(TOK, 0)
    kb.fence(v, kb.ld)
    kb.fence(s, kb.ld)
    kb.fence(t, kb.ld)

    for (c0, w) in t_tiles:
        bank_wait(0)
        bank_wait(1)
        for c in range(8):
            xs_ap = x1i[:, c * TOK + c0:c * TOK + c0 + w]
            v.tensor_copy(hilo[:, 0:w], xs_ap)
            v.tensor_sub(hilo[:, w:2 * w], xs_ap, hilo[:, 0:w])
            kb.fence(s, kb.ve)
            s.activation(sq[:, 0:w], xs_ap, AF.Square)
            kb.fence(v, kb.se)
            v.tensor_copy(hilo[:, 2 * w:3 * w], sq[:, 0:w])
            v.tensor_sub(hilo[:, 3 * w:4 * w], sq[:, 0:w], hilo[:, 2 * w:3 * w])
            kb.fence(t, kb.ve)
            t.matmul(psf[0][0:1, 0:w], onesb_t[:], hilo[:, 0:w],
                     start=(c == 0), stop=False)
            mi = t.matmul(psf[0][0:1, 0:w], onesb_t[:], hilo[:, w:2 * w],
                          start=False, stop=(c == 7))
            kb.pe.tick(mi)
            t.matmul(psf[1][0:1, 0:w], onesb_t[:], hilo[:, 2 * w:3 * w],
                     start=(c == 0), stop=False)
            mi = t.matmul(psf[1][0:1, 0:w], onesb_t[:], hilo[:, 3 * w:4 * w],
                          start=False, stop=(c == 7))
            kb.pe.tick(mi)
            kb.fence(v, kb.pe)
        v.tensor_copy(rows[0:1, c0:c0 + w], psf[0][0:1, 0:w])
        v.tensor_copy(rows[0:1, TOK + c0:TOK + c0 + w], psf[1][0:1, 0:w])
        bank_ready[0] = kb.ve.mark()
        bank_ready[1] = kb.ve.mark()
    v.tensor_scalar_mul(rows[0:1, 2 * TOK:3 * TOK], rows[0:1, 0:TOK],
                        -1.0 / D)
    v.tensor_mul(rows[0:1, 0:TOK], rows[0:1, 2 * TOK:3 * TOK],
                 rows[0:1, 2 * TOK:3 * TOK])
    v.scalar_tensor_tensor(rows[0:1, 3 * TOK:4 * TOK],
                           rows[0:1, TOK:2 * TOK], 1.0 / D,
                           rows[0:1, 0:TOK], ALU.mult, ALU.subtract)
    v.tensor_scalar_add(rows[0:1, 3 * TOK:4 * TOK],
                        rows[0:1, 3 * TOK:4 * TOK], 1e-5)
    kb.fence(s, kb.ve)
    s.activation(rows[0:1, 3 * TOK:4 * TOK], rows[0:1, 3 * TOK:4 * TOK], AF.Ln)
    kb.fence(v, kb.se)
    v.tensor_scalar_mul(rows[0:1, 3 * TOK:4 * TOK],
                        rows[0:1, 3 * TOK:4 * TOK], -0.5)
    kb.fence(s, kb.ve)
    s.activation(rows[0:1, 3 * TOK:4 * TOK], rows[0:1, 3 * TOK:4 * TOK],
                 AF.Exp)
    kb.fence(t, kb.se)
    for c in range(8):
        for j, (c0, w) in enumerate(t_tiles):
            bank_wait(2)
            mi = t.matmul(psf[2][:, 0:w], ones1_t[:],
                          rows[0:1, 2 * TOK + c0:2 * TOK + c0 + w],
                          start=True, stop=True)
            kb.pe.tick(mi)
            bank_wait(3)
            mi = t.matmul(psf[3][:, 0:w], ones1_t[:],
                          rows[0:1, 3 * TOK + c0:3 * TOK + c0 + w],
                          start=True, stop=True)
            kb.pe.tick(mi)
            kb.fence(v, kb.pe)
            v.tensor_add(stage[:, 0:w], x1i[:, c * TOK + c0:c * TOK + c0 + w],
                         psf[2][:, 0:w])
            bank_ready[2] = kb.ve.mark()
            v.tensor_mul(xn1b[:, c * TOK + c0:c * TOK + c0 + w],
                         stage[:, 0:w], psf[3][:, 0:w])
            bank_ready[3] = kb.ve.mark()
        kb.fence(nc.sync, kb.ve)
        kb.store(o_xn1[:, c * TOK:(c + 1) * TOK], xn1b[:, c * TOK:(c + 1) * TOK])

    v.memset(kmaxr[:], 0.0)
    for part, i3 in (("q", 0), ("k", 1), ("v", 2)):
        for oc in range(8):
            kb.fence(nc.gpsimd, kb.pe)
            kb.load(wbuf[:, 0:8 * 128].rearrange("p (k e) -> p k e", k=8),
                    w_qkv1.rearrange("p (k e) -> p k e", k=8)
                    [:, :, i3 * 1024 + oc * 128:i3 * 1024 + (oc + 1) * 128],
                    gp=True)
            kb.fence(t, kb.gq)
            for j, (c0, w) in enumerate(t_tiles):
                bk = j % 2
                bank_wait(bk)
                for k in range(8):
                    mi = t.matmul(psf[bk][:, 0:w],
                                  wbuf[:, k * 128:(k + 1) * 128],
                                  xn1b[:, k * TOK + c0:k * TOK + c0 + w],
                                  start=(k == 0), stop=(k == 7))
                kb.pe.tick(mi)
                kb.fence(s, kb.pe, kb.st, kb.ve)
                s.copy(bstage[:, (j % 2) * 512:(j % 2) * 512 + w],
                       psf[bk][:, 0:w])
                bank_ready[bk] = kb.se.mark()
                dst = {"q": o_q, "k": o_k, "v": None}[part]
                if part != "v":
                    kb.fence(nc.sync, kb.se)
                    kb.store(dst[:, oc * TOK + c0:oc * TOK + c0 + w],
                             bstage[:, (j % 2) * 512:(j % 2) * 512 + w])
                if part == "k":
                    kb.fence(v, kb.se, kb.pe)
                    v.tensor_mul(hilo[:, 0:w],
                                 bstage[:, (j % 2) * 512:(j % 2) * 512 + w],
                                 bstage[:, (j % 2) * 512:(j % 2) * 512 + w])
                    kb.fence(t, kb.ve)
                    bank_wait(4)
                    mi = t.matmul(psf[4][0:1, 0:w], onesb_t[0:64, :],
                                  hilo[0:64, 0:w], start=True, stop=True)
                    kb.pe.tick(mi)
                    bank_wait(5)
                    mi = t.matmul(psf[5][0:1, 0:w], onesb_t[64:128, :],
                                  hilo[64:128, 0:w], start=True, stop=True)
                    kb.pe.tick(mi)
                    kb.fence(v, kb.pe)
                    v.reduce_max(krow[0:1, 0:1], psf[4][0:1, 0:w], axis=AX.X)
                    bank_ready[4] = kb.ve.mark()
                    v.tensor_tensor(kmaxr[0:1, 2 * oc:2 * oc + 1],
                                    kmaxr[0:1, 2 * oc:2 * oc + 1],
                                    krow[0:1, 0:1], ALU.max)
                    v.reduce_max(krow[0:1, 0:1], psf[5][0:1, 0:w], axis=AX.X)
                    bank_ready[5] = kb.ve.mark()
                    v.tensor_tensor(kmaxr[0:1, 2 * oc + 1:2 * oc + 2],
                                    kmaxr[0:1, 2 * oc + 1:2 * oc + 2],
                                    krow[0:1, 0:1], ALU.max)
                if part == "v":
                    for jb in range(w // 128):
                        kb.fence(t, kb.se, kb.ve)
                        kb.pe.tick(t.transpose(
                            psb[:, 0:128],
                            bstage[:, (j % 2) * 512 + jb * 128:
                                   (j % 2) * 512 + (jb + 1) * 128]
                            .bitcast(BF16) if False else
                            bstage[:, (j % 2) * 512 + jb * 128:
                                   (j % 2) * 512 + (jb + 1) * 128],
                            idt[:]))
                        kb.fence(v, kb.pe, kb.st)
                        tt = (c0 + jb * 128) // 128
                        v.tensor_copy(hilo[:, 0:128], psb[:, 0:128])
                        kb.fence(nc.sync, kb.ve)
                        kb.store(o_vtm[:, tt * 1024 + oc * 128:
                                       tt * 1024 + (oc + 1) * 128],
                                 hilo[:, 0:128])
    kb.fence(s, kb.ve)
    s.activation(kmaxr[:], kmaxr[:], AF.Sqrt)
    kb.fence(nc.sync, kb.se)
    kb.store(o_kmax, kmaxr[:])
    return kb.finish()


def p8a_inmaps(inputs, xn0_full, sz_cores, y2_full, xB6_cores):
    x = inputs["L0_attn_x"]
    w_op = chunk_fm(np.ascontiguousarray(inputs["L0_attn_out_proj_w"].T))
    w_cfc = chunk_fm(np.ascontiguousarray(inputs["L0_attn_c_fc_w"].T))
    w_cpj = chunk_fm(np.ascontiguousarray(inputs["L0_attn_c_proj_w"].T))
    cst = attn_consts()
    maps = []
    for c in range(NCORE):
        b, ch = c // 4, c % 4
        g0 = ch * TOK
        maps.append(dict(
            xn0=slice_tok_cm(xn0_full, b, g0, 0, TOK),
            sz=chunk_fm(sz_cores[c]),
            y2=chunk_fm(np.ascontiguousarray(y2_full[b][:, g0:g0 + TOK])),
            xorig=slice_tok_cm(x, b, g0, 0, TOK),
            xB6=xB6_cores[c],
            w_op=w_op, w_cfc=w_cfc, w_cpj=w_cpj,
            ident=cst["ident"], onesbf=np.ones((128, 1), dtype=BF),
            ones1=cst["ones1"],
        ))
    return maps


def p8b_inmaps(inputs, x1_cores):
    w_qkv1 = chunk_fm(inputs["L1_attn_w_qkv"])
    cst = attn_consts()
    return [dict(x1in=x1_cores[c], w_qkv1=w_qkv1, ident=cst["ident"],
                 onesbf=np.ones((128, 1), dtype=BF), ones1=cst["ones1"])
            for c in range(NCORE)]



def build_p9():
    kb = KB()
    nc = kb.nc
    x1in = kb.inp("x1in", [128, 8 * TOK])
    xn1 = kb.inp("xn1", [128, 8 * TOK], BF16)
    qin = kb.inp("qin", [128, 8 * TOK], BF16)
    kfull = kb.inp("kfull", [16, 64 * T], BF16)
    vtmf = kb.inp("vtmf", [16, T * 64], BF16)
    kpart = kb.inp("kpart", [1, 64])
    w1 = kb.inp("w1", [128, 8 * 4096])
    w2 = kb.inp("w2", [128, 8 * 4096])
    h1 = kb.inp("h1", [128, 32 * 4096])
    h2 = kb.inp("h2", [128, 32 * 4096])
    w3 = kb.inp("w3", [128, 32 * 1024])
    s1v = kb.inp("s1v", [128, 32])
    s2v = kb.inp("s2v", [128, 32])
    h1s = kb.inp("h1s", [128, 32])
    h2s = kb.inp("h2s", [128, 32])
    w3s = kb.inp("w3s", [128, 8])
    onesbf = kb.inp("onesbf", [128, 1], BF16)
    ones1 = kb.inp("ones1", [1, 128])

    o_out = kb.out("o_out", [128, 8 * TOK])

    kb.open()
    v, s, g, t = kb.v, kb.s, kb.g, kb.t
    onesb_t = kb.sb([128, 1], BF16, "onesb")
    ones1_t = kb.sb([1, 128], F32, "ones1")
    xn1b = kb.sb([128, 8 * TOK], BF16, "xn1b")
    O_all = kb.sb([128, 8 * TOK], BF16, "oall")
    kh = kb.sb([128, T], BF16, "kh")
    vh = kb.sb([128, 64 * 64], BF16, "vh")
    qb32 = kb.sb([128, TOK], BF16, "qb32")
    qsq = kb.sb([128, TOK], BF16, "qsq")
    Pt = kb.sb([128, 512], BF16, "pt")
    srow = kb.sb([1, TOK], F32, "srow")
    negm = kb.sb([1, TOK], F32, "negm")
    khat = kb.sb([1, 16], F32, "khat")
    khn = kb.sb([1, 16], F32, "khn")
    kpt = kb.sb([1, 64], F32, "kpt")
    lrow = kb.sb([1, 512], F32, "lrow")
    lbv = kb.sb([64, 512], F32, "lbv")
    hq1 = kb.sb([128, 32 * 512], BF16, "hq1")
    hq2 = kb.sb([128, 32 * 512], BF16, "hq2")
    gl = kb.sb([128, 512], F32, "gl")
    gl2 = kb.sb([128, 512], F32, "gl2")
    wbuf = kb.sb([128, 32 * 128], BF16, "wbuf")
    wbufb = kb.sb([128, 8 * 128], BF16, "wbufb")
    sc1 = kb.sb([128, 32], F32, "sc1")
    sc2 = kb.sb([128, 32], F32, "sc2")
    sch1 = kb.sb([128, 32], F32, "sch1")
    sch2 = kb.sb([128, 32], F32, "sch2")
    scw3 = kb.sb([128, 8], F32, "scw3")
    stage = kb.sb([128, 2 * 512], F32, "stage")
    strm = kb.sb([128, 512], F32, "strm")

    psf = [kb.ps([128, 512], F32, "psf") for _ in range(6)]
    bank_ready = [None] * 6

    def bank_wait(bk):
        if bank_ready[bk] is not None:
            c, val = bank_ready[bk]
            t.wait_ge(c.sem, val)

    kb.load(onesb_t[:], onesbf)
    kb.load(ones1_t[:], ones1)
    kb.load(xn1b[:], xn1)
    kb.load(kpt[:], kpart)
    kb.load(sc1[:], s1v)
    kb.load(sc2[:], s2v)
    kb.load(sch1[:], h1s)
    kb.load(sch2[:], h2s)
    kb.load(scw3[:], w3s)
    t_tiles = tiles_of(TOK, 0)

    kb.fence(v, kb.ld)
    for h in range(16):
        v.reduce_max(khat[0:1, h:h + 1], kpt[0:1, h:64:16], axis=AX.X)
    v.tensor_scalar_mul(khn[:], khat[:], -1.0 / 32.0)

    qld = kb.sb([128, TOK], BF16, "qld")
    for h in range(16):
        hb, oc = 64 * (h % 2), h // 2
        buf = 0
        kb.fence(nc.sync, kb.pe)
        kb.load(kh[hb:hb + 64, 0:T], kfull[h:h + 1, :]
                .rearrange("h (d t) -> (h d) t", d=64))
        kb.load(vh[:, 0:4096].rearrange("p (a d) -> p a d", d=64),
                vtmf[h:h + 1, :].rearrange("h (a p d) -> (h p) a d",
                                           p=128, d=64))
        if h % 2 == 0:
            kb.fence(nc.sync, kb.ve, kb.se)
            kb.load(qld[:], qin[:, oc * TOK:(oc + 1) * TOK])
            kb.fence(v, kb.pe, kb.ld)
            v.tensor_scalar_mul(qb32[:], qld[:], 1.0 / 32.0)
            kb.fence(s, kb.ve, kb.pe)
            s.activation(qsq[:], qld[:], AF.Square)
        kb.fence(t, kb.se)
        for j, (c0, w) in enumerate(t_tiles):
            bank_wait(4)
            mi = t.matmul(psf[4][0:1, 0:w], onesb_t[hb:hb + 64, :],
                          qsq[hb:hb + 64, c0:c0 + w], start=True, stop=True)
            kb.pe.tick(mi)
            kb.fence(v, kb.pe)
            v.tensor_copy(srow[0:1, c0:c0 + w], psf[4][0:1, 0:w])
            bank_ready[4] = kb.ve.mark()
        kb.fence(s, kb.ve)
        s.activation(srow[:], srow[:], AF.Sqrt)
        kb.fence(v, kb.se)
        v.tensor_scalar_mul(negm[:], srow[:], khn[0:1, h:h + 1])
        kb.fence(t, kb.ve, kb.ld)
        for qt in range(4):
            q0 = qt * 512
            bank_wait(2)
            bank_wait(3)
            for kt in range(64):
                bk = kt % 2
                bank_wait(bk)
                t.matmul(psf[bk][:], kh[hb:hb + 64, 128 * kt:128 * (kt + 1)],
                         qb32[hb:hb + 64, q0:q0 + 512],
                         start=True, stop=False)
                mi = t.matmul(psf[bk][:], ones1_t[:],
                              negm[0:1, q0:q0 + 512], start=False, stop=True)
                kb.pe.tick(mi)
                kb.fence(s, kb.pe)
                s.activation(Pt[:], psf[bk][:], AF.Exp)
                bank_ready[bk] = kb.se.mark()
                kb.fence(t, kb.se)
                t.matmul(psf[2][0:64, :], vh[:, 64 * kt:64 * (kt + 1)],
                         Pt[:], start=(kt == 0), stop=(kt == 63))
                mi = t.matmul(psf[3][0:1, :], onesb_t[:], Pt[:],
                              start=(kt == 0), stop=(kt == 63))
                kb.pe.tick(mi)
            kb.fence(v, kb.pe)
            v.tensor_copy(lrow[:], psf[3][0:1, :])
            bank_ready[3] = kb.ve.mark()
            kb.fence(s, kb.ve)
            s.activation(lrow[:], lrow[:], AF.Ln)
            s.activation(lrow[:], lrow[:], AF.Exp, scale=-1.0)
            kb.fence(t, kb.se)
            bank_wait(5)
            mi = t.matmul(psf[5][0:64, :], ones1_t[0:1, 0:64], lrow[:],
                          start=True, stop=True)
            kb.pe.tick(mi)
            kb.fence(v, kb.pe)
            v.tensor_copy(lbv[:], psf[5][0:64, :])
            bank_ready[5] = kb.ve.mark()
            v.tensor_mul(O_all[hb:hb + 64, oc * TOK + q0:oc * TOK + q0 + 512],
                         psf[2][0:64, :], lbv[:])
            bank_ready[2] = kb.ve.mark()

    def wload(wcm, K, oc, dst, cols=128):
        kb.fence(nc.gpsimd, kb.pe)
        kb.load(dst[:, 0:K * cols].rearrange("p (k e) -> p k e", k=K),
                wcm.rearrange("p (k e) -> p k e", k=K)
                [:, :, oc * cols:(oc + 1) * cols], gp=True)
        kb.fence(t, kb.gq)

    for th in range(4):
        tc0 = th * 512
        for oc in range(32):
            wload(w1, 8, oc, wbufb)
            bank_wait(0)
            for k in range(8):
                mi = t.matmul(psf[0][:], wbufb[:, k * 128:(k + 1) * 128],
                              xn1b[:, k * TOK + tc0:k * TOK + tc0 + 512],
                              start=(k == 0), stop=(k == 7))
            kb.pe.tick(mi)
            kb.fence(s, kb.pe)
            s.activation(gl[:], psf[0][:], AF.Silu, scale=sc1[:, oc:oc + 1])
            bank_ready[0] = kb.se.mark()
            wload(w2, 8, oc, wbufb)
            bank_wait(1)
            for k in range(8):
                mi = t.matmul(psf[1][:], wbufb[:, k * 128:(k + 1) * 128],
                              xn1b[:, k * TOK + tc0:k * TOK + tc0 + 512],
                              start=(k == 0), stop=(k == 7))
            kb.pe.tick(mi)
            kb.fence(v, kb.pe, kb.se)
            v.tensor_scalar(gl2[:], psf[1][:], sc2[:, oc:oc + 1], None,
                            ALU.mult)
            bank_ready[1] = kb.ve.mark()
            v.tensor_mul(hq1[:, oc * 512:(oc + 1) * 512], gl[:], gl2[:])
        for (wcm, ssc, src_b, dst_b) in ((h1, sch1, hq1, hq2),
                                         (h2, sch2, hq2, hq1)):
            for oc in range(32):
                wload(wcm, 32, oc, wbuf)
                bank_wait(0)
                for k in range(32):
                    mi = t.matmul(psf[0][:], wbuf[:, k * 128:(k + 1) * 128],
                                  src_b[:, k * 512:(k + 1) * 512],
                                  start=(k == 0), stop=(k == 31))
                kb.pe.tick(mi)
                kb.fence(s, kb.pe)
                s.activation(dst_b[:, oc * 512:(oc + 1) * 512], psf[0][:],
                             AF.Gelu, scale=ssc[:, oc:oc + 1])
                bank_ready[0] = kb.se.mark()
        for oc in range(8):
            wload(w3, 32, oc, wbuf)
            bank_wait(0)
            for k in range(32):
                mi = t.matmul(psf[0][:], wbuf[:, k * 128:(k + 1) * 128],
                              hq1[:, k * 512:(k + 1) * 512],
                              start=(k == 0), stop=(k == 31))
            kb.pe.tick(mi)
            kb.fence(nc.sync, kb.ve)
            kb.load(strm[:, 0:512], x1in[:, oc * TOK + tc0:
                                         oc * TOK + tc0 + 512])
            kb.fence(v, kb.pe, kb.ld, kb.st)
            v.scalar_tensor_tensor(stage[:, 0:512], psf[0][:],
                                   scw3[:, oc:oc + 1], strm[:, 0:512],
                                   ALU.mult, ALU.add)
            bank_ready[0] = kb.ve.mark()
            v.tensor_add(stage[:, 512:1024], stage[:, 0:512],
                         O_all[:, oc * TOK + tc0:oc * TOK + tc0 + 512])
            kb.fence(nc.sync, kb.ve)
            kb.store(o_out[:, oc * TOK + tc0:oc * TOK + tc0 + 512],
                     stage[:, 512:1024])
    return kb.finish()


def p9_inmaps(inputs, x1_cores, xn1_cores, q_cores, k_cores, vtm_cores,
              kmax_cores):
    w1 = chunk_fm(np.ascontiguousarray(inputs["L1_ffn_w1_w"].T))
    w2 = chunk_fm(np.ascontiguousarray(inputs["L1_ffn_w2_w"].T))
    h1 = chunk_fm(np.ascontiguousarray(inputs["L1_ffn_h1_w"].T))
    h2 = chunk_fm(np.ascontiguousarray(inputs["L1_ffn_h2_w"].T))
    w3 = chunk_fm(np.ascontiguousarray(inputs["L1_ffn_w3_w"].T))
    s1v = chunk_fm(inputs["L1_ffn_w1_s"][:, None])
    s2v = chunk_fm(inputs["L1_ffn_w2_s"][:, None])
    h1s = chunk_fm(inputs["L1_ffn_h1_s"][:, None])
    h2s = chunk_fm(inputs["L1_ffn_h2_s"][:, None])
    w3s = chunk_fm(inputs["L1_ffn_w3_s"][:, None])
    cst = attn_consts()
    maps = []
    for bb in range(2):
        Kfm = np.concatenate(
            [from_cm_np(k_cores[4 * bb + j], TOK) for j in range(4)], axis=1)
        kf = np.ascontiguousarray(
            Kfm.reshape(16, 64, T)).reshape(16, 64 * T)
        Vtm = np.concatenate(
            [vtm_cores[4 * bb + j].reshape(128, 16, 1024)
             .transpose(1, 0, 2).reshape(TOK, 1024)
             for j in range(4)], axis=0)
        vtmf = np.ascontiguousarray(
            Vtm.reshape(T, 16, 64).transpose(1, 0, 2)).reshape(16, T * 64)
        kpart = np.zeros((1, 64), np.float32)
        for j in range(4):
            kpart[0, 16 * j:16 * (j + 1)] = kmax_cores[4 * bb + j][0]
        for j in range(4):
            c = 4 * bb + j
            maps.append(dict(
                x1in=x1_cores[c], xn1=xn1_cores[c], qin=q_cores[c],
                kfull=kf, vtmf=vtmf, kpart=kpart,
                w1=w1, w2=w2, h1=h1, h2=h2, w3=w3,
                s1v=s1v, s2v=s2v, h1s=h1s, h2s=h2s, w3s=w3s,
                onesbf=np.ones((128, 1), dtype=BF), ones1=cst["ones1"],
            ))
    assert len(maps) == NCORE
    return maps


def from_cm_np(a, cols):
    return np.ascontiguousarray(
        a.reshape(128, 8, cols).transpose(1, 0, 2).reshape(1024, cols))



_PROGS = {}


def _prog(name, builder):
    if name not in _PROGS:
        _PROGS[name] = builder()
    return _PROGS[name]


def _run(name, builder, maps):
    nc = _prog(name, builder)
    res = run_bass_kernel_spmd(nc, maps, core_ids=list(range(NCORE)))
    return res.results


def kernel(**inputs):
    inputs = {k: np.asarray(v) for k, v in inputs.items()}
    r1 = _run("p1", build_p1, p1_inmaps(inputs))
    xn0_full = np.stack(
        [np.concatenate([r1[4 * b + j]["o_xn0"] for j in range(4)], 0)
         for b in range(2)])
    r2 = _run("p2", build_p2, p2_inmaps(inputs, r1))
    y2_full = [np.concatenate([r2[4 * b + j]["o_y2"] for j in range(4)], 0)
               for b in range(2)]
    xB_full = xn0_full
    for layer in range(6):
        ra = _run("attn", build_attn, attn_inmaps(inputs, layer, xB_full))
        xA_cms = [ra[c]["o_xA"] for c in range(NCORE)]
        rf = _run("ffn", build_ffn, ffn_inmaps(inputs, layer, xA_cms))
        xB_full = np.stack(
            [np.concatenate(
                [from_cm_np(rf[4 * b + j]["o_xB"], TOK).T for j in range(4)],
                0) for b in range(2)])
    xB6_cores = [chunk_fm(np.ascontiguousarray(
        xB_full[c // 4, (c % 4) * TOK:(c % 4 + 1) * TOK].T))
        for c in range(NCORE)]
    r8a = _run("p8a", build_p8a,
               p8a_inmaps(inputs, xn0_full,
                          [r1[c]["o_sz"] for c in range(NCORE)],
                          y2_full, xB6_cores))
    x1_cores = [r8a[c]["o_x1"] for c in range(NCORE)]
    r8b = _run("p8b", build_p8b, p8b_inmaps(inputs, x1_cores))
    r9 = _run("p9", build_p9,
              p9_inmaps(inputs, x1_cores,
                        [r8b[c]["o_xn1"] for c in range(NCORE)],
                        [r8b[c]["o_q"] for c in range(NCORE)],
                        [r8b[c]["o_k"] for c in range(NCORE)],
                        [r8b[c]["o_vtm"] for c in range(NCORE)],
                        [r8b[c]["o_kmax"] for c in range(NCORE)]))
    out = np.empty((2, T, D), np.float32)
    for c in range(NCORE):
        b, ch = c // 4, c % 4
        out[b, ch * TOK:(ch + 1) * TOK] = from_cm_np(r9[c]["o_out"], TOK).T
    return out


# version 02
# speedup vs baseline: 1.2942x; 1.2942x over previous
import math
from contextlib import ExitStack

import numpy as np
import ml_dtypes
import concourse.bass as bass
import concourse.mybir as mybir
from concourse.bass_utils import run_bass_kernel_spmd

F32 = mybir.dt.float32
BF16 = mybir.dt.bfloat16
AF = mybir.ActivationFunctionType
ALU = mybir.AluOpType
AX = mybir.AxisListType

NCORE = 8
B = 2
T = 8192
TOK = 2048
D = 1024
EI = 2048
DST = 16
DTR = 64
HALO = 514
F32EPS = float(np.finfo(np.float32).eps)
BF = ml_dtypes.bfloat16


class Ctr:

    def __init__(self, sem):
        self.sem = sem
        self.v = 0

    def tick(self, instr, n=1):
        instr.then_inc(self.sem, n)
        self.v += n
        return instr

    def wait(self, eng):
        p = getattr(self, "_proxy", None)
        if p is not None:
            p.flush()
        eng.wait_ge(self.sem, self.v)

    def mark(self):
        p = getattr(self, "_proxy", None)
        if p is not None:
            p.flush()
        return (self, self.v)


_OPS = {
    "tensor_copy", "tensor_tensor", "tensor_add", "tensor_sub", "tensor_mul",
    "tensor_max", "tensor_scalar", "tensor_single_scalar", "tensor_scalar_add",
    "tensor_scalar_sub", "tensor_scalar_mul", "tensor_scalar_max",
    "tensor_scalar_min", "tensor_relu", "tensor_reduce", "reduce_max",
    "reduce_sum", "scalar_tensor_tensor", "tensor_tensor_scan", "memset",
    "memzero", "reciprocal", "select", "transpose", "bn_stats", "bn_aggr",
    "iota", "activation", "square", "sqrt", "mul", "copy", "add", "sign",
    "engine_nop", "stream_shuffle", "pool", "pool_avg", "pool_max",
    "copy_predicated",
}


class EngProxy:

    SMALL = 512

    def __init__(self, eng, ctr):
        self._eng = eng
        self._ctr = ctr
        self._last = None
        self._need_wait = False

    def flush(self):
        if self._last is not None:
            self._ctr.tick(self._last)
            self._last = None

    def mark(self):
        self.flush()
        return self._ctr.mark()

    def __getattr__(self, name):
        attr = getattr(self._eng, name)
        if name not in _OPS:
            return attr

        def wrap(*a, **k):
            if self._need_wait:
                self._eng.wait_ge(self._ctr.sem, self._ctr.v)
                self._need_wait = False
            instr = attr(*a, **k)
            try:
                fs = a[0].free_size()
            except Exception:
                fs = 0
            if fs < self.SMALL:
                self.flush()
                self._ctr.tick(instr)
                self._need_wait = True
            else:
                if self._last is not None:
                    self._ctr.tick(self._last)
                self._last = instr
            return instr

        return wrap

    def wait_ge(self, sem, v):
        return self._eng.wait_ge(sem, v)

    def wait(self, mark):
        ctr, v = mark if isinstance(mark, tuple) else (mark, mark.v)
        return self._eng.wait_ge(ctr.sem, v)


class KB:

    def __init__(self):
        self.nc = bass.Bass("TRN2", target_bir_lowering=False,
                            detect_race_conditions=False)
        self.ctx = ExitStack()
        self._n = 0

    def name(self, p):
        self._n += 1
        return f"{p}{self._n}"

    def inp(self, name, shape, dt=F32):
        return self.nc.dram_tensor(name, shape, dt, kind="ExternalInput").ap()

    def out(self, name, shape, dt=F32):
        return self.nc.dram_tensor(name, shape, dt, kind="ExternalOutput").ap()

    def sb(self, shape, dt=F32, tag="sb"):
        return self.ctx.enter_context(
            self.nc.sbuf_tensor(self.name(tag), shape, dt))

    def ps(self, shape, dt=F32, tag="ps"):
        return self.ctx.enter_context(
            self.nc.psum_tensor(self.name(tag), shape, dt))

    def ctr(self, tag="sem"):
        return Ctr(self.ctx.enter_context(self.nc.semaphore(self.name(tag))))

    def open(self):
        self.blk = self.ctx.enter_context(self.nc.Block())
        self.ld = self.ctr("ld")
        self.gq = self.ctr("gq")
        self.st = self.ctr("st")
        self.pe = self.ctr("pe")
        self.ve = self.ctr("ve")
        self.se = self.ctr("se")
        self.ge = self.ctr("ge")
        self.v = EngProxy(self.nc.vector, self.ve)
        self.s = EngProxy(self.nc.scalar, self.se)
        self.g = EngProxy(self.nc.gpsimd, self.ge)
        self.t = self.nc.tensor
        self._proxy_of = {id(self.ve): self.v, id(self.se): self.s,
                          id(self.ge): self.g}
        self.ve._proxy = self.v
        self.se._proxy = self.s
        self.ge._proxy = self.g
        return self

    def finish(self):
        self.nc.sync.wait_ge(self.st.sem, self.st.v)
        self.ctx.close()
        return self.nc

    def load(self, dst_ap, src_ap, gp=False, slow=False):
        q = self.nc.gpsimd if gp else self.nc.sync
        ctr = self.gq if gp else self.ld
        kw = dict(allow_slow_non_contiguous=True) if slow else {}
        ctr.tick(q.dma_start(out=dst_ap, in_=src_ap, **kw), 16)

    def store(self, dst_ap, src_ap, slow=False):
        kw = dict(allow_slow_non_contiguous=True) if slow else {}
        self.st.tick(self.nc.sync.dma_start(out=dst_ap, in_=src_ap, **kw), 16)

    def fence(self, consumer, *ctrs):
        eng = consumer._eng if isinstance(consumer, EngProxy) else consumer
        for c in ctrs:
            p = self._proxy_of.get(id(c))
            if p is not None:
                p.flush()
            eng.wait_ge(c.sem, c.v)



def build_p1():
    kb = KB()
    nc = kb.nc
    xs = kb.inp("xs", [3 + TOK, D])
    w_in0 = kb.inp("w_in0", [128, 8 * EI])
    w_in1 = kb.inp("w_in1", [128, 8 * EI])
    conv_w = kb.inp("conv_w", [128, 16 * 4])
    conv_b = kb.inp("conv_b", [128, 16])
    w_dtl = kb.inp("w_dtl", [128, 16 * DTR])
    w_Bp = kb.inp("w_Bp", [128, 16 * DST])
    w_Cp = kb.inp("w_Cp", [128, 16 * DST])
    w_dtp = kb.inp("w_dtp", [DTR, EI])
    dtp_b = kb.inp("dtp_b", [128, 16])
    ident = kb.inp("ident", [128, 128], BF16)

    o_xn0 = kb.out("o_xn0", [TOK, D])
    o_sz = kb.out("o_sz", [EI, TOK])
    o_xconv = kb.out("o_xconv", [EI, TOK])
    o_dt = kb.out("o_dt", [EI, TOK])
    o_B = kb.out("o_B", [DST, TOK])
    o_C = kb.out("o_C", [DST, TOK])

    NT = TOK // 128
    T2 = 3 + TOK
    kb.open()
    v, s, t = kb.v, kb.s, kb.t

    idt = kb.sb([128, 128], BF16, "idt")
    rn_fm = kb.sb([128, 8 * T2], BF16, "rnfm")
    wh = kb.sb([128, 8 * EI], BF16, "wh")
    xcb = kb.sb([128, 16 * TOK], BF16, "xcb")
    cw = kb.sb([128, 16 * 4], F32, "cw")
    cb = kb.sb([128, 16], F32, "cb")
    dtpb = kb.sb([128, 16], F32, "dtpb")
    wb_dtl = kb.sb([128, 16 * DTR], BF16, "wdtl")
    wb_B = kb.sb([128, 16 * DST], BF16, "wB")
    wb_C = kb.sb([128, 16 * DST], BF16, "wC")
    wb_dtp = kb.sb([DTR, EI], BF16, "wdtp")
    dtl_b = kb.sb([DTR, TOK], BF16, "dtl")

    kb.load(idt[:], ident)
    kb.load(cw[:], conv_w)
    kb.load(cb[:], conv_b)
    kb.load(dtpb[:], dtp_b)
    kb.load(wb_dtl[:], w_dtl, gp=True)
    kb.load(wb_B[:], w_Bp, gp=True)
    kb.load(wb_C[:], w_Cp, gp=True)
    kb.load(wb_dtp[:], w_dtp, gp=True)

    x_tm = kb.sb([128, D], F32, "xtm")
    sq = kb.sb([128, D], F32, "sq")
    s1 = kb.sb([128, 1], F32, "s1")
    s2 = kb.sb([128, 1], F32, "s2")
    var = kb.sb([128, 1], F32, "var")
    rstd = kb.sb([128, 1], F32, "rstd")
    rrms = kb.sb([128, 1], F32, "rrms")
    xn_tm = kb.sb([128, D], F32, "xntm")
    rn_tm = kb.sb([128, D], BF16, "rntm")
    ps_tr = kb.ps([128, 128], BF16, "pstr")

    def ln_rms_tile(rows, x_ap, col0):
        kb.fence(v, kb.ld, kb.st)
        kb.fence(s, kb.ld, kb.st)
        v.reduce_sum(s1[0:rows], x_ap, axis=AX.X)
        s.activation(sq[0:rows], x_ap, AF.Square)
        kb.fence(v, kb.se)
        v.reduce_sum(s2[0:rows], sq[0:rows], axis=AX.X)
        v.tensor_scalar_mul(s1[0:rows], s1[0:rows], -1.0 / D)
        v.tensor_mul(var[0:rows], s1[0:rows], s1[0:rows])
        v.scalar_tensor_tensor(var[0:rows], s2[0:rows], 1.0 / D, var[0:rows],
                               ALU.mult, ALU.subtract)
        v.tensor_scalar_add(var[0:rows], var[0:rows], 1e-5)
        kb.fence(s, kb.ve)
        s.activation(rstd[0:rows], var[0:rows], AF.Sqrt)
        kb.fence(v, kb.se)
        v.reciprocal(rstd[0:rows], rstd[0:rows])
        v.tensor_scalar(xn_tm[0:rows], x_ap, s1[0:rows], rstd[0:rows],
                        ALU.add, ALU.mult)
        v.tensor_scalar_add(rrms[0:rows], var[0:rows], -1e-5)
        v.tensor_mul(rrms[0:rows], rrms[0:rows], rstd[0:rows])
        v.tensor_mul(rrms[0:rows], rrms[0:rows], rstd[0:rows])
        v.tensor_scalar_add(rrms[0:rows], rrms[0:rows], F32EPS)
        kb.fence(s, kb.ve)
        s.activation(rrms[0:rows], rrms[0:rows], AF.Ln)
        kb.fence(v, kb.se)
        v.tensor_scalar_mul(rrms[0:rows], rrms[0:rows], -0.5)
        kb.fence(s, kb.ve)
        s.activation(rrms[0:rows], rrms[0:rows], AF.Exp)
        kb.fence(v, kb.se)
        v.tensor_scalar_mul(rn_tm[0:rows], xn_tm[0:rows], rrms[0:rows])
        kb.fence(t, kb.ve)
        for c in range(8):
            kb.pe.tick(t.transpose(ps_tr[0:128, 0:rows],
                                   rn_tm[0:rows, 128 * c:128 * (c + 1)],
                                   idt[0:rows, 0:rows]))
            kb.fence(v, kb.pe)
            v.tensor_copy(rn_fm[:, c * T2 + col0: c * T2 + col0 + rows],
                          ps_tr[0:128, 0:rows])
            kb.fence(t, kb.ve)

    xh = kb.sb([3, D], F32, "xh")
    kb.load(xh[:], xs[0:3, :])
    ln_rms_tile(3, xh[:], 0)
    for i in range(NT):
        kb.fence(nc.sync, kb.ve, kb.se)
        kb.load(x_tm[:], xs[3 + 128 * i: 3 + 128 * (i + 1), :])
        ln_rms_tile(128, x_tm[:], 3 + 128 * i)
        kb.fence(nc.sync, kb.ve)
        kb.store(o_xn0[128 * i:128 * (i + 1), :], xn_tm[0:128, :])

    ps_mm = [kb.ps([128, 512], F32, "pmm") for _ in range(6)]
    bank_ready = [None] * 6
    xin_c = kb.sb([128, T2], F32, "xinc")
    conv_t = kb.sb([128, TOK], F32, "convt")
    z_t = conv_t

    def tiles_for(total, off):
        out, p = [], off
        while p < off + total:
            w = min(512, off + total - p)
            out.append((p, w))
            p += w
        return out

    kb.load(wh[:], w_in0, gp=True)

    def bank_wait(bk):
        if bank_ready[bk] is not None:
            c, val = bank_ready[bk]
            t.wait_ge(c.sem, val)

    def in_proj_chunk(o, tiles, dst, dst_off, evac_eng):
        for j, (c0, w) in enumerate(tiles):
            bk = (o * len(tiles) + j) % 4
            bank_wait(bk)
            for k in range(8):
                mi = t.matmul(ps_mm[bk][:, 0:w],
                              wh[:, k * EI + 128 * o: k * EI + 128 * (o + 1)],
                              rn_fm[:, k * T2 + c0: k * T2 + c0 + w],
                              start=(k == 0), stop=(k == 7))
            kb.pe.tick(mi)
            kb.fence(evac_eng, kb.pe)
            evac_eng.tensor_copy(dst[:, c0 + dst_off:c0 + dst_off + w],
                                 ps_mm[bk][:, 0:w])
            bank_ready[bk] = evac_eng._ctr.mark()

    xin_tiles = tiles_for(T2, 0)
    kb.fence(t, kb.ve, kb.gq)
    for o in range(16):
        in_proj_chunk(o, xin_tiles, xin_c, 0, v)
        cw_o = [cw[:, o * 4 + k: o * 4 + k + 1] for k in range(4)]
        v.tensor_scalar(conv_t[:], xin_c[:, 0:TOK], cw_o[0], cb[:, o:o + 1],
                        ALU.mult, ALU.add)
        for k in (1, 2, 3):
            v.scalar_tensor_tensor(conv_t[:], xin_c[:, k:k + TOK],
                                   cw_o[k], conv_t[:], ALU.mult, ALU.add)
        kb.fence(s, kb.ve, kb.st)
        s.activation(xin_c[:, 0:TOK], conv_t[:], AF.Silu)
        s.activation(xcb[:, o * TOK:(o + 1) * TOK], conv_t[:], AF.Silu)
        kb.fence(nc.sync, kb.se)
        kb.store(o_xconv[128 * o:128 * (o + 1), :], xin_c[:, 0:TOK])
        kb.fence(v, kb.st)

    kb.fence(nc.gpsimd, kb.pe)
    kb.load(wh[:], w_in1, gp=True)
    z_tiles = tiles_for(TOK, 3)
    kb.fence(t, kb.gq)
    for o in range(16):
        in_proj_chunk(o, z_tiles, z_t, -3, v)
        kb.fence(s, kb.ve)
        s.activation(z_t[:], z_t[:], AF.Silu)
        kb.fence(nc.sync, kb.se)
        kb.store(o_sz[128 * o:128 * (o + 1), :], z_t[:])
        kb.fence(v, kb.st)
        kb.fence(s, kb.st)

    bc_stage = kb.sb([DST, 2 * 512], F32, "bcst")
    kb.fence(t, kb.se, kb.gq)
    for j, (c0, w) in enumerate(tiles_for(TOK, 0)):
        for (dst_out, wmat, rows) in ((None, wb_dtl, DTR), (o_B, wb_B, DST),
                                      (o_C, wb_C, DST)):
            bk = {DTR: j % 2, DST: 4 + j % 2}[rows]
            bank_wait(bk)
            cols = {DTR: DTR, DST: DST}[rows]
            for k in range(16):
                mi = t.matmul(ps_mm[bk][0:rows, 0:w],
                              wmat[:, k * cols:(k + 1) * cols],
                              xcb[:, k * TOK + c0: k * TOK + c0 + w],
                              start=(k == 0), stop=(k == 15))
            kb.pe.tick(mi)
            kb.fence(v, kb.pe)
            if dst_out is None:
                v.tensor_copy(dtl_b[0:rows, c0:c0 + w], ps_mm[bk][0:rows, 0:w])
            else:
                slot = 0 if dst_out is o_B else 1
                kb.fence(v, kb.st)
                v.tensor_copy(bc_stage[0:DST, 512 * slot:512 * slot + w],
                              ps_mm[bk][0:rows, 0:w])
                kb.fence(nc.sync, kb.ve)
                kb.store(dst_out[:, c0:c0 + w],
                         bc_stage[0:DST, 512 * slot:512 * slot + w])
            bank_ready[bk] = kb.ve.mark()

    t_relu = kb.sb([128, 512], F32, "trelu")
    t_sp = kb.sb([128, 512], F32, "tsp")
    dt_stage = kb.sb([128, 2 * 512], F32, "dtstage")
    kb.fence(t, kb.ve)
    for o in range(16):
        for j, (c0, w) in enumerate(tiles_for(TOK, 0)):
            bk = 2 + (o * 4 + j) % 2
            slot = (o * 4 + j) % 2
            bank_wait(bk)
            mi = t.matmul(ps_mm[bk][:, 0:w], wb_dtp[:, 128 * o:128 * (o + 1)],
                          dtl_b[:, c0:c0 + w], start=True, stop=True)
            kb.pe.tick(mi)
            kb.fence(v, kb.pe)
            b_o = dtpb[:, o:o + 1]
            v.tensor_scalar(t_relu[:, 0:w], ps_mm[bk][:, 0:w], b_o, 0.0,
                            ALU.add, ALU.max)
            kb.fence(s, kb.ve)
            s.activation(t_sp[:, 0:w], ps_mm[bk][:, 0:w], AF.Abs, bias=b_o)
            bank_ready[bk] = kb.se.mark()
            s.activation(t_sp[:, 0:w], t_sp[:, 0:w], AF.Exp, scale=-1.0)
            s.activation(t_sp[:, 0:w], t_sp[:, 0:w], AF.Ln, bias=1.0)
            kb.fence(v, kb.se, kb.st)
            v.tensor_add(dt_stage[:, 512 * slot:512 * slot + w],
                         t_relu[:, 0:w], t_sp[:, 0:w])
            kb.fence(nc.sync, kb.ve)
            kb.store(o_dt[128 * o:128 * (o + 1), c0:c0 + w],
                     dt_stage[:, 512 * slot:512 * slot + w])
    return kb.finish()


def chunk_fm(a, p=128):
    K, c = a.shape
    return np.ascontiguousarray(
        a.reshape(K // p, p, c).transpose(1, 0, 2).reshape(p, -1))


_IDENT = np.eye(128).astype(BF)


def p1_inmaps(inputs):
    x = inputs["L0_attn_x"]
    w_inT = inputs["L0_attn_in_proj_w"].T
    w_in0 = chunk_fm(np.ascontiguousarray(w_inT[:, 0:EI]))
    w_in1 = chunk_fm(np.ascontiguousarray(w_inT[:, EI:2 * EI]))
    conv_w = chunk_fm(inputs["L0_attn_conv_w"])
    conv_b = chunk_fm(inputs["L0_attn_conv_b"][:, None])
    xp = inputs["L0_attn_x_proj_w"].T
    w_dtl = chunk_fm(np.ascontiguousarray(xp[:, 0:DTR]))
    w_Bp = chunk_fm(np.ascontiguousarray(xp[:, DTR:DTR + DST]))
    w_Cp = chunk_fm(np.ascontiguousarray(xp[:, DTR + DST:DTR + 2 * DST]))
    w_dtp = np.ascontiguousarray(inputs["L0_attn_dt_proj_w"].T)
    dtp_b = chunk_fm(inputs["L0_attn_dt_proj_b"][:, None])
    shared = dict(w_in0=w_in0, w_in1=w_in1, conv_w=conv_w, conv_b=conv_b,
                  w_dtl=w_dtl, w_Bp=w_Bp, w_Cp=w_Cp, w_dtp=w_dtp,
                  dtp_b=dtp_b, ident=_IDENT)
    maps = []
    for c in range(NCORE):
        b, ch = c // 4, c % 4
        g0 = ch * TOK
        xs = np.zeros((3 + TOK, D), np.float32)
        xs[3:] = x[b, g0:g0 + TOK]
        if g0 >= 3:
            xs[0:3] = x[b, g0 - 3:g0]
        maps.append(dict(xs=np.ascontiguousarray(xs), **shared))
    return maps


def kernel(**inputs):
    raise NotImplementedError("under construction")


def build_p2():
    kb = KB()
    nc = kb.nc
    dts = kb.inp("dts", [512, T])
    xcs = kb.inp("xcs", [512, T])
    A_log_s = kb.inp("A_log_s", [512, DST])
    D_s = kb.inp("D_s", [128, 4])
    Bsf = kb.inp("Bsf", [DST, T])
    Csf = kb.inp("Csf", [DST, T])
    o_y2 = kb.out("o_y2", [512, T])

    TC = 512
    NTC = T // TC
    kb.open()
    v, s, g, t = kb.v, kb.s, kb.g, kb.t

    A_f = kb.sb([128, 4 * DST], F32, "Af")
    Dv = kb.sb([128, 4], F32, "Dv")
    carries = kb.sb([128, 4 * DST], F32, "carry")
    dA = kb.sb([128, DST * TC], F32, "dA")
    wB = kb.sb([128, DST * TC], F32, "wB")
    hh = kb.sb([128, DST * TC], F32, "hh")
    Bb = kb.sb([128, DST * TC], F32, "Bb")
    Cb = kb.sb([128, DST * TC], F32, "Cb")
    dt_t = kb.sb([128, 2 * TC], F32, "dtt")
    xc_t = kb.sb([128, 2 * TC], F32, "xct")
    w_t = kb.sb([128, TC], F32, "wt")
    y_st = kb.sb([128, 2 * TC], F32, "yst")

    kb.load(Dv[:], D_s)
    al = kb.sb([128, 4 * DST], F32, "al")
    for ec in range(4):
        kb.load(al[:, ec * DST:(ec + 1) * DST],
                A_log_s[ec * 128:(ec + 1) * 128, :])
    kb.fence(s, kb.ld)
    s.activation(A_f[:], al[:], AF.Exp)
    kb.fence(v, kb.se)
    v.tensor_scalar_mul(A_f[:], A_f[:], -1.0)
    v.memset(carries[:], 0.0)

    for tc in range(NTC):
        kb.fence(nc.sync, kb.ge)
        for ss in range(DST):
            kb.load(Bb[:, ss * TC:(ss + 1) * TC],
                    Bsf[ss:ss + 1, tc * TC:(tc + 1) * TC]
                    .partition_broadcast(128).squeeze(1))
            kb.load(Cb[:, ss * TC:(ss + 1) * TC],
                    Csf[ss:ss + 1, tc * TC:(tc + 1) * TC]
                    .partition_broadcast(128).squeeze(1))
        for ec in range(4):
            buf = ec % 2
            kb.fence(nc.sync, kb.se, kb.ge, kb.ve)
            kb.load(dt_t[:, buf * TC:(buf + 1) * TC],
                    dts[ec * 128:(ec + 1) * 128, tc * TC:(tc + 1) * TC])
            kb.load(xc_t[:, buf * TC:(buf + 1) * TC],
                    xcs[ec * 128:(ec + 1) * 128, tc * TC:(tc + 1) * TC])
            dtb = dt_t[:, buf * TC:(buf + 1) * TC]
            xcb_ = xc_t[:, buf * TC:(buf + 1) * TC]
            kb.fence(s, kb.ld, kb.ve, kb.ge)
            for ss in range(DST):
                s.activation(dA[:, ss * TC:(ss + 1) * TC], dtb, AF.Exp,
                             scale=A_f[:, ec * DST + ss:ec * DST + ss + 1])
            kb.fence(g, kb.ld, kb.ve)
            g.tensor_mul(w_t[:], dtb, xcb_)
            for ss in range(DST):
                g.tensor_mul(wB[:, ss * TC:(ss + 1) * TC], w_t[:],
                             Bb[:, ss * TC:(ss + 1) * TC])
            kb.fence(v, kb.se, kb.ge)
            for ss in range(DST):
                v.tensor_tensor_scan(
                    hh[:, ss * TC:(ss + 1) * TC],
                    dA[:, ss * TC:(ss + 1) * TC],
                    wB[:, ss * TC:(ss + 1) * TC],
                    carries[:, ec * DST + ss:ec * DST + ss + 1],
                    ALU.mult, ALU.add)
            v.tensor_copy(carries[:, ec * DST:(ec + 1) * DST],
                          hh[:, TC - 1::TC])
            kb.fence(g, kb.ve)
            for ss in range(DST):
                g.tensor_mul(dA[:, ss * TC:(ss + 1) * TC],
                             hh[:, ss * TC:(ss + 1) * TC],
                             Cb[:, ss * TC:(ss + 1) * TC])
            kb.fence(v, kb.ge)
            half = DST // 2
            while half >= 1:
                v.tensor_add(dA[:, 0:half * TC], dA[:, 0:half * TC],
                             dA[:, half * TC:2 * half * TC])
                half //= 2
            kb.fence(v, kb.st)
            v.scalar_tensor_tensor(y_st[:, buf * TC:(buf + 1) * TC],
                                   xcb_, Dv[:, ec:ec + 1], dA[:, 0:TC],
                                   ALU.mult, ALU.add)
            kb.fence(nc.sync, kb.ve)
            kb.store(o_y2[ec * 128:(ec + 1) * 128, tc * TC:(tc + 1) * TC],
                     y_st[:, buf * TC:(buf + 1) * TC])
    return kb.finish()


def p2_inmaps(inputs, p1_res):
    A_log = inputs["L0_attn_A_log"]
    Dvec = inputs["L0_attn_D"]
    maps = []
    for c in range(NCORE):
        b, sl = c // 4, c % 4
        dts = np.concatenate([p1_res[4 * b + j]["o_dt"] for j in range(4)], 1)
        xcs = np.concatenate([p1_res[4 * b + j]["o_xconv"] for j in range(4)], 1)
        Bsf = np.concatenate([p1_res[4 * b + j]["o_B"] for j in range(4)], 1)
        Csf = np.concatenate([p1_res[4 * b + j]["o_C"] for j in range(4)], 1)
        rows = slice(sl * 512, (sl + 1) * 512)
        maps.append(dict(
            dts=np.ascontiguousarray(dts[rows]),
            xcs=np.ascontiguousarray(xcs[rows]),
            A_log_s=np.ascontiguousarray(A_log[rows]),
            D_s=chunk_fm(Dvec[rows][:, None]),
            Bsf=np.ascontiguousarray(Bsf),
            Csf=np.ascontiguousarray(Csf),
        ))
    return maps


TH = HALO + TOK
OW = TOK + 1


def tiles_of(total, off=0, wmax=512):
    out, p = [], off
    while p < off + total:
        w = min(wmax, off + total - p)
        out.append((p, w))
        p += w
    return out


def build_attn(debug=False):
    kb = KB()
    nc = kb.nc
    xB = kb.inp("xB", [128, 8 * TH])
    wqkv = kb.inp("wqkv", [128, 8 * 3072])
    wo = kb.inp("wo", [128, 8 * 1024])
    qsv = kb.inp("qsv", [128, 1])
    ksv = kb.inp("ksv", [128, 1])
    csq8 = kb.inp("csq8", [128, TH])
    ssq8 = kb.inp("ssq8", [128, TH])
    csk = kb.inp("csk", [128, TH])
    ssk = kb.inp("ssk", [128, TH])
    maskg = kb.inp("maskg", [128, 4 * 256], BF16)
    mask0 = kb.inp("mask0", [128, 4 * 256], BF16)
    ident = kb.inp("ident", [128, 128], BF16)
    sel33b = kb.inp("sel33b", [128, 33], BF16)
    selB33 = kb.inp("selB33", [33, 128])
    ones1 = kb.inp("ones1", [1, 128])
    onesb = kb.inp("onesb", [128, 1], BF16)
    onesr = kb.inp("onesr", [1, 128], BF16)
    identf = kb.inp("identf", [128, 128])

    o_xA = kb.out("o_xA", [128, 8 * OW])
    o_dbg1 = kb.out("o_dbg1", [128, TH]) if debug else None
    o_dbg2 = kb.out("o_dbg2", [128, TH]) if debug else None

    kb.open()
    v, s, g, t = kb.v, kb.s, kb.g, kb.t

    idt = kb.sb([128, 128], BF16, "idt")
    qsb = kb.sb([128, 1], F32, "qsb")
    ksb = kb.sb([128, 1], F32, "ksb")
    sel33b_t = kb.sb([128, 33], BF16, "sel33")
    selB33_t = kb.sb([33, 128], F32, "selB")
    ones1_t = kb.sb([1, 128], F32, "ones1")
    onesb_t = kb.sb([128, 1], BF16, "onesb")
    onesr_t = kb.sb([1, 128], BF16, "onesr")
    maskg_t = kb.sb([128, 4 * 256], BF16, "mg")
    mask0_t = kb.sb([128, 4 * 256], BF16, "m0")

    xqb = kb.sb([128, 8 * TH], BF16, "xqb")
    x_f = kb.sb([128, TH], F32, "xf")
    t1 = kb.sb([128, TH], F32, "t1")
    qpb = kb.sb([128, TH], BF16, "qpb")
    kpb = kb.sb([128, TH], BF16, "kpb")
    vpb = kb.sb([128, TH], BF16, "vpb")
    V_tm = kb.sb([128, 20 * 128], BF16, "vtm")
    O_all = kb.sb([128, 8 * OW], BF16, "oall")
    hilo = kb.sb([128, 2 * TH], BF16, "hilo")
    rnr = kb.sb([33, TH], F32, "rnr")
    mcol = kb.sb([128, 36], F32, "mcol")
    msb = kb.sb([36, 128], F32, "msb")
    negrow = kb.sb([1, 2 * 2304], F32, "negrow")
    mrow_d = kb.nc.dram_tensor("mrow_d", [36, 128], F32)
    idtf = kb.sb([128, 128], F32, "idtf")
    lrow = kb.sb([1, 256], F32, "lrow")
    lbv = kb.sb([64, 256], F32, "lbv")
    Pt = kb.sb([128, 4 * 256], BF16, "pt")
    wpart = kb.sb([128, 2 * 8 * 128], BF16, "wp")
    tbl = kb.sb([128, 2 * 512], F32, "tbl")
    xbt = kb.sb([128, 2 * 512], F32, "xbt")
    xa_st = kb.sb([128, 2 * 512], F32, "xast")

    psf = [kb.ps([128, 512], F32, "psf") for _ in range(7)]
    psb = kb.ps([128, 1024], BF16, "psb")
    bank_ready = [None] * 8

    def bank_wait(bk):
        if bank_ready[bk] is not None:
            c, val = bank_ready[bk]
            t.wait_ge(c.sem, val)

    kb.load(idt[:], ident)
    kb.load(qsb[:], qsv)
    kb.load(ksb[:], ksv)
    kb.load(sel33b_t[:], sel33b)
    kb.load(selB33_t[:], selB33)
    kb.load(ones1_t[:], ones1)
    kb.load(onesb_t[:], onesb)
    kb.load(onesr_t[:], onesr)
    kb.load(idtf[:], identf)
    kb.load(maskg_t[:], maskg)
    kb.load(mask0_t[:], mask0)
    kb.load(xqb[:], xB, gp=True)
    kb.fence(v, kb.ld)
    v.memset(rnr[:], 1.0)
    v.memset(O_all[:], 0.0)
    v.memset(x_f[:, 0:2], 0.0)
    v.memset(t1[:, 0:2], 0.0)

    qkv_tiles = tiles_of(2560, 2)

    def rhs_slice(k, c0, w):
        off = k * TH + c0 - (1 if k >= 4 else 0)
        return xqb[:, off:off + w]

    def qk_prep(part, hp):
        csT, ssT = (csq8, ssq8) if part == "q" else (csk, ssk)
        scl = qsb if part == "q" else ksb
        dstb = qpb if part == "q" else kpb
        if debug and hp == 7 and part == "k":
            kb.fence(nc.sync, kb.ve)
            kb.store(o_dbg1, x_f[:])
        kb.fence(s, kb.ve)
        s.activation(t1[:], x_f[:], AF.Square)
        kb.fence(v, kb.se)
        v.tensor_copy(hilo[:, 0:TH], t1[:])
        v.tensor_sub(hilo[:, TH:2 * TH], t1[:], hilo[:, 0:TH])
        kb.fence(t, kb.ve)
        for j, (c0, w) in enumerate(qkv_tiles):
            bank_wait(3)
            t.matmul(psf[3][0:33, 0:w], sel33b_t[:], hilo[:, c0:c0 + w],
                     start=True, stop=False)
            mi = t.matmul(psf[3][0:33, 0:w], sel33b_t[:],
                          hilo[:, TH + c0:TH + c0 + w],
                          start=False, stop=True)
            kb.pe.tick(mi)
            kb.fence(v, kb.pe)
            v.tensor_copy(rnr[0:33, c0:c0 + w], psf[3][0:33, 0:w])
            bank_ready[3] = kb.ve.mark()
        v.tensor_scalar_max(rnr[:], rnr[:], 1e-24)
        kb.fence(s, kb.ve)
        s.activation(rnr[:], rnr[:], AF.Ln)
        kb.fence(v, kb.se)
        v.tensor_scalar_mul(rnr[:], rnr[:], -0.5)
        kb.fence(s, kb.ve)
        s.activation(rnr[:], rnr[:], AF.Exp)
        kb.fence(t, kb.se)
        for j, (c0, w) in enumerate(qkv_tiles):
            bank_wait(2)
            mi = t.matmul(psf[2][:, 0:w], selB33_t[:], rnr[:, c0:c0 + w],
                          start=True, stop=True)
            kb.pe.tick(mi)
            kb.fence(v, kb.pe)
            v.scalar_tensor_tensor(t1[:, c0:c0 + w], x_f[:, c0:c0 + w],
                                   scl[:, 0:1], psf[2][:, 0:w],
                                   ALU.mult, ALU.mult)
            bank_ready[2] = kb.ve.mark()
        if debug and hp == 7 and part == "k":
            kb.fence(nc.sync, kb.ve)
            kb.store(o_dbg2, t1[:])
        for (dst0, src0, sgn) in ((0, 32, -1.0), (32, 0, 1.0),
                                  (64, 96, -1.0), (96, 64, 1.0)):
            v.tensor_scalar_mul(x_f[dst0:dst0 + 32, :],
                                t1[src0:src0 + 32, :], sgn)
        for j, (c0, w) in enumerate(qkv_tiles):
            bf = (j % 2) * 512
            kb.fence(nc.sync, kb.ve)
            kb.load(tbl[:, bf:bf + w], csT[:, c0:c0 + w])
            kb.fence(v, kb.ld)
            v.tensor_mul(t1[:, c0:c0 + w], t1[:, c0:c0 + w], tbl[:, bf:bf + w])
            kb.fence(nc.sync, kb.ve)
            kb.load(tbl[:, bf:bf + w], ssT[:, c0:c0 + w])
            kb.fence(v, kb.ld)
            v.tensor_mul(x_f[:, c0:c0 + w], x_f[:, c0:c0 + w],
                         tbl[:, bf:bf + w])
            v.tensor_add(t1[:, c0:c0 + w], t1[:, c0:c0 + w],
                         x_f[:, c0:c0 + w])
        v.tensor_copy(dstb[:], t1[:])
        return

    for hp in range(8):
        for i3, part in enumerate(("q", "k", "v")):
            wslc = wqkv.rearrange("p (k e) -> p k e", k=8)[
                :, :, i3 * 1024 + hp * 128: i3 * 1024 + (hp + 1) * 128]
            kb.fence(nc.gpsimd, kb.pe)
            kb.load(wpart[:, 0:8 * 128].rearrange("p (k e) -> p k e", k=8),
                    wslc, gp=True)
            kb.fence(t, kb.gq, kb.ve, kb.se)
            for j, (c0, w) in enumerate(qkv_tiles):
                bk = j % 2
                bank_wait(bk)
                for k in range(8):
                    mi = t.matmul(psf[bk][:, 0:w],
                                  wpart[:, k * 128:(k + 1) * 128],
                                  rhs_slice(k, c0, w),
                                  start=(k == 0), stop=(k == 7))
                kb.pe.tick(mi)
                if part == "v":
                    kb.fence(s, kb.pe)
                    s.copy(vpb[:, c0:c0 + w], psf[bk][:, 0:w])
                    bank_ready[bk] = kb.se.mark()
                else:
                    kb.fence(v, kb.pe)
                    v.tensor_copy(x_f[:, c0:c0 + w], psf[bk][:, 0:w])
                    bank_ready[bk] = kb.ve.mark()
            if part != "v":
                qk_prep(part, hp)
        kb.fence(t, kb.se)
        for jt in range(20):
            kb.pe.tick(t.transpose(psb[:, 0:128],
                                   vpb[:, 2 + 128 * jt: 2 + 128 * (jt + 1)],
                                   idt[:]))
            kb.fence(v, kb.pe)
            v.tensor_copy(V_tm[:, 128 * jt:128 * (jt + 1)], psb[:, 0:128])
            kb.fence(t, kb.ve)
        for h in (0, 1):
            hb = 64 * h
            for qt in range(18):
                qt0 = 258 + 128 * qt
                k0 = qt0 - 256
                bank_wait(2)
                mi = t.matmul(psf[2][:, 0:384],
                              qpb[hb:hb + 64, qt0:qt0 + 128],
                              kpb[hb:hb + 64, k0:k0 + 384],
                              start=True, stop=True)
                kb.pe.tick(mi)
                kb.fence(v, kb.pe)
                v.reduce_max(mcol[:, h * 18 + qt:h * 18 + qt + 1],
                             psf[2][:, 0:384], axis=AX.X, negate=True)
                bank_ready[2] = kb.ve.mark()
        kb.fence(t, kb.ve)
        bank_wait(2)
        kb.pe.tick(t.transpose(psf[2][0:36, 0:128], mcol[:], idtf[0:128, 0:128]))
        kb.fence(v, kb.pe)
        v.tensor_copy(msb[:], psf[2][0:36, 0:128])
        bank_ready[2] = kb.ve.mark()
        kb.fence(nc.sync, kb.ve, kb.ld)
        kb.store(mrow_d.ap(), msb[:])
        kb.fence(nc.sync, kb.st)
        kb.load(negrow[:], mrow_d.ap().rearrange("a b -> (a b)").unsqueeze(0))
        kb.fence(t, kb.ld)
        for wi in range(9):
            qc0 = 258 + 256 * wi
            kc0 = qc0 - 256
            mask_t = mask0_t if wi == 1 else maskg_t
            for h in (0, 1):
                hb, hr = 64 * h, 32 * h
                for kt in range(4):
                    bk = 4 + kt // 2
                    co = (kt % 2) * 256
                    if kt % 2 == 0:
                        bank_wait(bk)
                    t.matmul(psf[bk][:, co:co + 256],
                             kpb[hb:hb + 64, kc0 + 128 * kt:kc0 + 128 * (kt + 1)],
                             qpb[hb:hb + 64, qc0:qc0 + 256],
                             start=True, stop=False)
                    mi = t.matmul(psf[bk][:, co:co + 256],
                                  ones1_t[0:1, :],
                                  negrow[0:1, h * 2304 + 256 * wi:
                                         h * 2304 + 256 * (wi + 1)],
                                  start=False, stop=True)
                    kb.pe.tick(mi)
                kb.fence(s, kb.pe)
                s.activation(Pt[:, 0:512], psf[4][:], AF.Exp)
                s.activation(Pt[:, 512:1024], psf[5][:], AF.Exp)
                bank_ready[4] = kb.se.mark()
                bank_ready[5] = kb.se.mark()
                kb.fence(g, kb.se)
                g.tensor_mul(Pt[:], Pt[:], mask_t[:])
                kb.fence(t, kb.ge)
                jt0 = (kc0 - 2) // 128
                bank_wait(6)
                bank_wait(3)
                for kt in range(4):
                    t.matmul(psf[6][0:64, 0:256],
                             V_tm[:, 128 * (jt0 + kt) + hb:
                                  128 * (jt0 + kt) + hb + 64],
                             Pt[:, 256 * kt:256 * (kt + 1)],
                             start=(kt == 0), stop=(kt == 3))
                    mi = t.matmul(psf[3][0:1, 0:256], onesb_t[:],
                                  Pt[:, 256 * kt:256 * (kt + 1)],
                                  start=(kt == 0), stop=(kt == 3))
                kb.pe.tick(mi)
                kb.fence(v, kb.pe)
                v.tensor_copy(lrow[:], psf[3][0:1, 0:256])
                bank_ready[3] = kb.ve.mark()
                kb.fence(s, kb.ve)
                s.activation(lrow[:], lrow[:], AF.Ln)
                s.activation(lrow[:], lrow[:], AF.Exp, scale=-1.0)
                kb.fence(t, kb.se)
                bank_wait(2)
                mi = t.matmul(psf[2][0:64, 0:256], ones1_t[0:1, 0:64],
                              lrow[:], start=True, stop=True)
                kb.pe.tick(mi)
                kb.fence(v, kb.pe)
                v.tensor_copy(lbv[:], psf[2][0:64, 0:256])
                bank_ready[2] = kb.ve.mark()
                if wi == 0:
                    v.tensor_mul(O_all[hb:hb + 64, hp * OW:hp * OW + 1],
                                 psf[6][0:64, 255:256], lbv[:, 255:256])
                else:
                    oc0 = hp * OW + 1 + 256 * (wi - 1)
                    v.tensor_mul(O_all[hb:hb + 64, oc0:oc0 + 256],
                                 psf[6][0:64, 0:256], lbv[:])
                bank_ready[6] = kb.ve.mark()

    ow_tiles = tiles_of(OW, 0)
    for oc in range(8):
        wslc = wo.rearrange("p (k e) -> p k e", k=8)[
            :, :, oc * 128:(oc + 1) * 128]
        kb.fence(nc.gpsimd, kb.pe)
        kb.load(wpart[:, 0:8 * 128].rearrange("p (k e) -> p k e", k=8),
                wslc, gp=True)
        kb.fence(t, kb.gq)
        for j, (c0, w) in enumerate(ow_tiles):
            bk = j % 2
            bf = (j % 2) * 512
            kb.fence(nc.sync, kb.ve)
            kb.load(xbt[:, bf:bf + w], xB[:, oc * TH + 513 + c0:
                                          oc * TH + 513 + c0 + w],
                    slow=(w < 8))
            bank_wait(bk)
            for k in range(8):
                mi = t.matmul(psf[bk][:, 0:w],
                              wpart[:, k * 128:(k + 1) * 128],
                              O_all[:, k * OW + c0:k * OW + c0 + w],
                              start=(k == 0), stop=(k == 7))
            kb.pe.tick(mi)
            kb.fence(v, kb.pe, kb.ld, kb.st)
            v.tensor_add(xa_st[:, bf:bf + w], psf[bk][:, 0:w],
                         xbt[:, bf:bf + w])
            bank_ready[bk] = kb.ve.mark()
            kb.fence(nc.sync, kb.ve)
            kb.store(o_xA[:, oc * OW + c0:oc * OW + c0 + w],
                     xa_st[:, bf:bf + w], slow=(w < 8))
    return kb.finish()


def _rotary_tables():
    j = np.arange(32)
    inv_freq = 1.0 / 10000.0 ** (2 * j / 64.0)
    base = (2 * j + 0.4 * 64) / (1.4 * 64)
    tpos = np.arange(-HALO, T, dtype=np.float64)
    fr = np.outer(inv_freq, tpos)
    sc = base[:, None] ** ((tpos[None, :] - T // 2) / 128.0)
    cos = np.cos(fr)
    sin = np.sin(fr)
    d2 = lambda a: np.concatenate([a, a], 0)
    csq8 = d2(8.0 * cos * sc).astype(np.float32)
    ssq8 = d2(8.0 * sin * sc).astype(np.float32)
    csk = d2(cos / sc).astype(np.float32)
    ssk = d2(sin / sc).astype(np.float32)
    dup = lambda a: np.ascontiguousarray(np.tile(a, (2, 1)))
    return tuple(dup(a) for a in (csq8, ssq8, csk, ssk))


_TABLES = None


def rotary_tables():
    global _TABLES
    if _TABLES is None:
        _TABLES = _rotary_tables()
    return _TABLES


def _mask_tiles(zero_prev):
    m = np.zeros((128, 4 * 256), dtype=BF)
    for kt in range(4):
        for p in range(128):
            c = kt * 128 + p
            r = np.arange(256)
            ok = (r <= c) & (c <= r + 256)
            if zero_prev and kt < 2:
                ok = ok & False
            m[p, kt * 256:(kt + 1) * 256] = ok.astype(BF)
    return m


_SEL = None


def attn_consts():
    global _SEL
    if _SEL is None:
        sel33b = np.zeros((128, 33), dtype=BF)
        sel33b[0:64, 0] = 1
        sel33b[64:128, 32] = 1
        selB33 = np.zeros((33, 128), dtype=np.float32)
        selB33[0, 0:64] = 1
        selB33[32, 64:128] = 1
        ones1 = np.ones((1, 128), dtype=np.float32)
        onesb = np.ones((128, 1), dtype=BF)
        onesr = np.ones((1, 128), dtype=BF)
        maskg = _mask_tiles(False)
        mask0 = _mask_tiles(True)
        _SEL = dict(sel33b=sel33b, selB33=selB33, ones1=ones1, onesb=onesb,
                    onesr=onesr, maskg=maskg, mask0=mask0, ident=_IDENT,
                    identf=np.eye(128, dtype=np.float32))
    return _SEL


def slice_tok_cm(x_full, b, g0, lead, width):
    xs = np.zeros((width, D), np.float32)
    lo = g0 - lead
    src0 = max(0, lo)
    src1 = min(T, lo + width)
    xs[src0 - lo:src1 - lo, :] = x_full[b, src0:src1]
    return chunk_fm(np.ascontiguousarray(xs.T))


def attn_inmaps(inputs, layer, xB_full):
    cst = attn_consts()
    csq8, ssq8, csk, ssk = rotary_tables()
    wqkv = chunk_fm(inputs["L0_ffn_Wqkv"][layer])
    wo = chunk_fm(inputs["L0_ffn_Wo"][layer])
    qsv = np.ascontiguousarray(
        np.tile(inputs["L0_ffn_q_scale"][layer], 2)[:, None])
    ksv = np.ascontiguousarray(
        np.tile(inputs["L0_ffn_k_scale"][layer], 2)[:, None])
    maps = []
    for c in range(NCORE):
        b, ch = c // 4, c % 4
        g0 = ch * TOK
        cols = slice(g0 - 514 + HALO, g0 + TOK + HALO)
        maps.append(dict(
            xB=slice_tok_cm(xB_full, b, g0, 514, TH),
            wqkv=wqkv, wo=wo, qsv=qsv, ksv=ksv,
            csq8=np.ascontiguousarray(csq8[:, cols]),
            ssq8=np.ascontiguousarray(ssq8[:, cols]),
            csk=np.ascontiguousarray(csk[:, cols]),
            ssk=np.ascontiguousarray(ssk[:, cols]),
            maskg=cst["maskg"],
            mask0=cst["mask0"] if ch == 0 else cst["maskg"],
            ident=cst["ident"], sel33b=cst["sel33b"], selB33=cst["selB33"],
            ones1=cst["ones1"], onesb=cst["onesb"], onesr=cst["onesr"],
            identf=cst["identf"],
        ))
    return maps


HID = 2816


def build_ffn():
    kb = KB()
    nc = kb.nc
    xA = kb.inp("xA", [128, 8 * OW])
    w1a = kb.inp("w1a", [128, 8 * HID])
    w1g = kb.inp("w1g", [128, 8 * HID])
    w2 = kb.inp("w2", [128, 22 * 1024])
    b1a = kb.inp("b1a", [128, 22])
    b1g = kb.inp("b1g", [128, 22])
    b2v = kb.inp("b2v", [128, 8])
    lng = kb.inp("lng", [128, 8])
    lnb = kb.inp("lnb", [128, 8])
    ones1 = kb.inp("ones1", [1, 128])
    onesbf = kb.inp("onesbf", [128, 1], BF16)

    o_xB = kb.out("o_xB", [128, 8 * TOK])

    kb.open()
    v, s, g, t = kb.v, kb.s, kb.g, kb.t

    xA_f = kb.sb([128, 8 * OW], F32, "xaf")
    xnf = kb.sb([128, 8 * TOK], BF16, "xnf")
    hilo = kb.sb([128, 4 * 512], BF16, "hilo")
    sq = kb.sb([128, 512], F32, "sq")
    rows = kb.sb([1, 4 * TOK], F32, "rows")
    hbuf = kb.sb([128, 22 * 512], BF16, "hbuf")
    gl = kb.sb([128, 512], F32, "gl")
    tmp = kb.sb([128, 512], F32, "tmp")
    wca = kb.sb([128, 8 * 128], BF16, "wca")
    wcg = kb.sb([128, 8 * 128], BF16, "wcg")
    wc2 = kb.sb([128, 1024], BF16, "wc2")
    b1a_t = kb.sb([128, 22], F32, "b1at")
    b1g_t = kb.sb([128, 22], F32, "b1gt")
    b2_t = kb.sb([128, 8], F32, "b2t")
    lng_t = kb.sb([128, 8], F32, "lngt")
    lnb_t = kb.sb([128, 8], F32, "lnbt")
    ones1_t = kb.sb([1, 128], F32, "ones1")
    onesbf_t = kb.sb([128, 1], BF16, "onesbf")
    xst = kb.sb([128, 2 * 512], F32, "xst")

    psf = [kb.ps([128, 512], F32, "psf") for _ in range(8)]
    bank_ready = [None] * 8

    def bank_wait(bk):
        if bank_ready[bk] is not None:
            c, val = bank_ready[bk]
            t.wait_ge(c.sem, val)

    kb.load(xA_f[:], xA)
    kb.load(b1a_t[:], b1a)
    kb.load(b1g_t[:], b1g)
    kb.load(b2_t[:], b2v)
    kb.load(lng_t[:], lng)
    kb.load(lnb_t[:], lnb)
    kb.load(ones1_t[:], ones1)
    kb.load(onesbf_t[:], onesbf)

    def xsh(c, c0, w):
        off = c * OW + c0 + (1 if c < 4 else 0)
        return xA_f[:, off:off + w]

    t_tiles = tiles_of(TOK, 0)
    kb.fence(v, kb.ld)
    kb.fence(s, kb.ld)
    for (c0, w) in t_tiles:
        bank_wait(0)
        bank_wait(1)
        for c in range(8):
            v.tensor_copy(hilo[:, 0:w], xsh(c, c0, w))
            v.tensor_sub(hilo[:, w:2 * w], xsh(c, c0, w), hilo[:, 0:w])
            kb.fence(s, kb.ve)
            s.activation(sq[:, 0:w], xsh(c, c0, w), AF.Square)
            kb.fence(v, kb.se)
            v.tensor_copy(hilo[:, 2 * w:3 * w], sq[:, 0:w])
            v.tensor_sub(hilo[:, 3 * w:4 * w], sq[:, 0:w], hilo[:, 2 * w:3 * w])
            kb.fence(t, kb.ve)
            t.matmul(psf[0][0:1, 0:w], onesbf_t[:], hilo[:, 0:w],
                     start=(c == 0), stop=False)
            mi = t.matmul(psf[0][0:1, 0:w], onesbf_t[:], hilo[:, w:2 * w],
                          start=False, stop=(c == 7))
            kb.pe.tick(mi)
            t.matmul(psf[1][0:1, 0:w], onesbf_t[:], hilo[:, 2 * w:3 * w],
                     start=(c == 0), stop=False)
            mi = t.matmul(psf[1][0:1, 0:w], onesbf_t[:], hilo[:, 3 * w:4 * w],
                          start=False, stop=(c == 7))
            kb.pe.tick(mi)
            if c < 7:
                kb.fence(v, kb.pe)
        kb.fence(v, kb.pe)
        v.tensor_copy(rows[0:1, c0:c0 + w], psf[0][0:1, 0:w])
        v.tensor_copy(rows[0:1, TOK + c0:TOK + c0 + w], psf[1][0:1, 0:w])
        bank_ready[0] = kb.ve.mark()
        bank_ready[1] = kb.ve.mark()
    v.tensor_scalar_mul(rows[0:1, 2 * TOK:3 * TOK], rows[0:1, 0:TOK],
                        -1.0 / D)
    v.tensor_mul(rows[0:1, 0:TOK], rows[0:1, 2 * TOK:3 * TOK],
                 rows[0:1, 2 * TOK:3 * TOK])
    v.scalar_tensor_tensor(rows[0:1, 3 * TOK:4 * TOK],
                           rows[0:1, TOK:2 * TOK], 1.0 / D,
                           rows[0:1, 0:TOK], ALU.mult, ALU.subtract)
    v.tensor_scalar_add(rows[0:1, 3 * TOK:4 * TOK],
                        rows[0:1, 3 * TOK:4 * TOK], 1e-5)
    kb.fence(s, kb.ve)
    s.activation(rows[0:1, 3 * TOK:4 * TOK], rows[0:1, 3 * TOK:4 * TOK],
                 AF.Ln)
    kb.fence(v, kb.se)
    v.tensor_scalar_mul(rows[0:1, 3 * TOK:4 * TOK],
                        rows[0:1, 3 * TOK:4 * TOK], -0.5)
    kb.fence(s, kb.ve)
    s.activation(rows[0:1, 3 * TOK:4 * TOK], rows[0:1, 3 * TOK:4 * TOK],
                 AF.Exp)
    kb.fence(t, kb.se)
    for c in range(8):
        for j, (c0, w) in enumerate(t_tiles):
            bank_wait(2)
            mi = t.matmul(psf[2][:, 0:w], ones1_t[:],
                          rows[0:1, 2 * TOK + c0:2 * TOK + c0 + w],
                          start=True, stop=True)
            kb.pe.tick(mi)
            bank_wait(3)
            mi = t.matmul(psf[3][:, 0:w], ones1_t[:],
                          rows[0:1, 3 * TOK + c0:3 * TOK + c0 + w],
                          start=True, stop=True)
            kb.pe.tick(mi)
            kb.fence(v, kb.pe)
            v.tensor_add(tmp[:, 0:w], xsh(c, c0, w), psf[2][:, 0:w])
            bank_ready[2] = kb.ve.mark()
            v.tensor_mul(tmp[:, 0:w], tmp[:, 0:w], psf[3][:, 0:w])
            bank_ready[3] = kb.ve.mark()
            v.tensor_scalar(xnf[:, c * TOK + c0:c * TOK + c0 + w],
                            tmp[:, 0:w], lng_t[:, c:c + 1],
                            lnb_t[:, c:c + 1], ALU.mult, ALU.add)

    for th in range(4):
        tc0 = th * 512
        for hc in range(22):
            kb.fence(nc.gpsimd, kb.pe)
            kb.load(wca[:].rearrange("p (k e) -> p k e", k=8),
                    w1a.rearrange("p (k e) -> p k e", k=8)
                    [:, :, hc * 128:(hc + 1) * 128], gp=True)
            kb.load(wcg[:].rearrange("p (k e) -> p k e", k=8),
                    w1g.rearrange("p (k e) -> p k e", k=8)
                    [:, :, hc * 128:(hc + 1) * 128], gp=True)
            kb.fence(t, kb.gq, kb.ve)
            bank_wait(0)
            bank_wait(1)
            for k in range(8):
                t.matmul(psf[0][:], wca[:, k * 128:(k + 1) * 128],
                         xnf[:, k * TOK + tc0:k * TOK + tc0 + 512],
                         start=(k == 0), stop=(k == 7))
                mi = t.matmul(psf[1][:], wcg[:, k * 128:(k + 1) * 128],
                              xnf[:, k * TOK + tc0:k * TOK + tc0 + 512],
                              start=(k == 0), stop=(k == 7))
            kb.pe.tick(mi)
            kb.fence(s, kb.pe)
            s.activation(gl[:], psf[1][:], AF.Gelu, bias=b1g_t[:, hc:hc + 1])
            bank_ready[1] = kb.se.mark()
            kb.fence(v, kb.se)
            v.scalar_tensor_tensor(hbuf[:, hc * 512:(hc + 1) * 512],
                                   psf[0][:], b1a_t[:, hc:hc + 1], gl[:],
                                   ALU.add, ALU.mult)
            bank_ready[0] = kb.ve.mark()
        for bk in range(8):
            bank_wait(bk)
        for hc in range(22):
            kb.fence(nc.gpsimd, kb.pe)
            kb.load(wc2[:], w2[:, hc * 1024:(hc + 1) * 1024], gp=True)
            kb.fence(t, kb.gq)
            for oc in range(8):
                mi = t.matmul(psf[oc][:], wc2[:, oc * 128:(oc + 1) * 128],
                              hbuf[:, hc * 512:(hc + 1) * 512],
                              start=(hc == 0), stop=(hc == 21))
            kb.pe.tick(mi)
        for oc in range(8):
            kb.fence(v, kb.pe, kb.st)
            v.scalar_tensor_tensor(xst[:, (oc % 2) * 512:(oc % 2) * 512 + 512],
                                   psf[oc][:], b2_t[:, oc:oc + 1],
                                   xA_f[:, oc * OW + 1 + tc0:
                                        oc * OW + 1 + tc0 + 512],
                                   ALU.add, ALU.add)
            bank_ready[oc] = kb.ve.mark()
            kb.fence(nc.sync, kb.ve)
            kb.store(o_xB[:, oc * TOK + tc0:oc * TOK + tc0 + 512],
                     xst[:, (oc % 2) * 512:(oc % 2) * 512 + 512])
    return kb.finish()


def ffn_inmaps(inputs, layer, xA_cms):
    W1 = inputs["L0_ffn_W1"][layer]
    b1 = inputs["L0_ffn_b1"][layer]
    W2 = inputs["L0_ffn_W2"][layer]
    w1a = np.zeros((D, HID), np.float32)
    w1g = np.zeros((D, HID), np.float32)
    w1a[:, 0:2730] = W1[:, 0:2730]
    w1g[:, 0:2730] = W1[:, 2730:5460]
    b1a = np.zeros((HID, 1), np.float32)
    b1g = np.zeros((HID, 1), np.float32)
    b1a[0:2730, 0] = b1[0:2730]
    b1g[0:2730, 0] = b1[2730:5460]
    w2p = np.zeros((HID, 1024), np.float32)
    w2p[0:2730] = W2
    cst = attn_consts()
    shared = dict(
        w1a=chunk_fm(w1a), w1g=chunk_fm(w1g), w2=chunk_fm(w2p),
        b1a=chunk_fm(b1a), b1g=chunk_fm(b1g),
        b2v=chunk_fm(inputs["L0_ffn_b2"][layer][:, None]),
        lng=chunk_fm(inputs["L0_ffn_ln_g"][layer][:, None]),
        lnb=chunk_fm(inputs["L0_ffn_ln_b"][layer][:, None]),
        ones1=cst["ones1"],
        onesbf=np.ones((128, 1), dtype=BF),
    )
    return [dict(xA=xA_cms[c], **shared) for c in range(NCORE)]


def build_p8a(debug=False):
    kb = KB()
    nc = kb.nc
    xn0 = kb.inp("xn0", [128, 8 * TOK])
    sz = kb.inp("sz", [128, 16 * TOK])
    y2 = kb.inp("y2", [128, 16 * TOK])
    xorig = kb.inp("xorig", [128, 8 * TOK])
    xB6 = kb.inp("xB6", [128, 8 * TOK])
    w_op = kb.inp("w_op", [128, 16 * 1024])
    w_cfc = kb.inp("w_cfc", [128, 8 * EI])
    w_cpj = kb.inp("w_cpj", [128, 16 * 1024])
    ident = kb.inp("ident", [128, 128], BF16)
    onesbf = kb.inp("onesbf", [128, 1], BF16)
    ones1 = kb.inp("ones1", [1, 128])

    o_x1 = kb.out("o_x1", [128, 8 * TOK])
    o_dbg1 = kb.out("o_dbg1", [128, 8 * TOK]) if debug else None
    o_dbg2 = kb.out("o_dbg2", [128, 8 * TOK], BF16) if debug else None

    kb.open()
    v, s, g, t = kb.v, kb.s, kb.g, kb.t
    idt = kb.sb([128, 128], BF16, "idt")
    onesb_t = kb.sb([128, 1], BF16, "onesb")
    ones1_t = kb.sb([1, 128], F32, "ones1")
    x1i = kb.sb([128, 8 * TOK], F32, "x1i")
    yqb = kb.sb([128, 16 * TOK], BF16, "yqb")
    hrelu = kb.sb([128, 16 * 512], BF16, "hrelu")
    wbuf = kb.sb([128, 16 * 128], BF16, "wbuf")
    stage = kb.sb([128, 2 * 512], F32, "stage")
    bstage = kb.sb([128, 2 * 512], BF16, "bstage")
    hilo = kb.sb([128, 4 * 512], BF16, "hilo")
    sq = kb.sb([128, 512], F32, "sq")
    rows = kb.sb([1, TOK], F32, "rows")
    strm = kb.sb([128, 2 * 512], F32, "strm")
    strm2 = kb.sb([128, 2 * 512], F32, "strm2")

    psf = [kb.ps([128, 512], F32, "psf") for _ in range(7)]
    psb = kb.ps([128, 1024], BF16, "psb")
    bank_ready = [None] * 8

    def bank_wait(bk):
        if bank_ready[bk] is not None:
            c, val = bank_ready[bk]
            t.wait_ge(c.sem, val)

    kb.load(idt[:], ident)
    kb.load(onesb_t[:], onesbf)
    kb.load(ones1_t[:], ones1)
    t_tiles = tiles_of(TOK, 0)

    def wslice(wcm, K, oc, dst):
        kb.fence(nc.gpsimd, kb.pe)
        kb.load(dst[:, 0:K * 128].rearrange("p (k e) -> p k e", k=K),
                wcm.rearrange("p (k e) -> p k e", k=K)
                [:, :, oc * 128:(oc + 1) * 128], gp=True)
        kb.fence(t, kb.gq)

    kb.fence(v, kb.ld)
    for c in range(16):
        for j, (c0, w) in enumerate(t_tiles):
            bf = (j % 2) * 512
            kb.fence(nc.sync, kb.ge)
            kb.load(strm[:, bf:bf + w], y2[:, c * TOK + c0:c * TOK + c0 + w])
            kb.load(strm2[:, bf:bf + w], sz[:, c * TOK + c0:c * TOK + c0 + w])
            kb.fence(g, kb.ld)
            g.tensor_mul(yqb[:, c * TOK + c0:c * TOK + c0 + w],
                         strm[:, bf:bf + w], strm2[:, bf:bf + w])

    for oc in range(8):
        wslice(w_op, 16, oc, wbuf)
        for j, (c0, w) in enumerate(t_tiles):
            bk = j % 2
            bank_wait(bk)
            kb.fence(t, kb.ge) if (oc == 0 and j == 0) else None
            for k in range(16):
                mi = t.matmul(psf[bk][:, 0:w],
                              wbuf[:, k * 128:(k + 1) * 128],
                              yqb[:, k * TOK + c0:k * TOK + c0 + w],
                              start=(k == 0), stop=(k == 15))
            kb.pe.tick(mi)
            kb.fence(nc.sync, kb.ve)
            kb.load(strm[:, 0:w], xn0[:, oc * TOK + c0:oc * TOK + c0 + w])
            kb.fence(v, kb.pe, kb.ld)
            v.tensor_add(x1i[:, oc * TOK + c0:oc * TOK + c0 + w],
                         psf[bk][:, 0:w], strm[:, 0:w])
            bank_ready[bk] = kb.ve.mark()

    if debug:
        kb.fence(nc.sync, kb.ve)
        kb.store(o_dbg1, x1i[:])
    for (c0, w) in t_tiles:
        bank_wait(0)
        for c in range(8):
            kb.fence(s, kb.ve)
            s.activation(sq[:, 0:w], x1i[:, c * TOK + c0:c * TOK + c0 + w],
                         AF.Square)
            kb.fence(v, kb.se)
            v.tensor_copy(hilo[:, 0:w], sq[:, 0:w])
            v.tensor_sub(hilo[:, w:2 * w], sq[:, 0:w], hilo[:, 0:w])
            kb.fence(t, kb.ve)
            t.matmul(psf[0][0:1, 0:w], onesb_t[:], hilo[:, 0:w],
                     start=(c == 0), stop=False)
            mi = t.matmul(psf[0][0:1, 0:w], onesb_t[:], hilo[:, w:2 * w],
                          start=False, stop=(c == 7))
            kb.pe.tick(mi)
            if c < 7:
                kb.fence(v, kb.pe)
        kb.fence(v, kb.pe)
        v.tensor_copy(rows[0:1, c0:c0 + w], psf[0][0:1, 0:w])
        bank_ready[0] = kb.ve.mark()
    v.tensor_scalar_mul(rows[0:1, 0:TOK], rows[0:1, 0:TOK], 1.0 / D)
    v.tensor_scalar_add(rows[0:1, 0:TOK], rows[0:1, 0:TOK], F32EPS)
    kb.fence(s, kb.ve)
    s.activation(rows[0:1, 0:TOK], rows[0:1, 0:TOK], AF.Ln)
    kb.fence(v, kb.se)
    v.tensor_scalar_mul(rows[0:1, 0:TOK], rows[0:1, 0:TOK], -0.5)
    kb.fence(s, kb.ve)
    s.activation(rows[0:1, 0:TOK], rows[0:1, 0:TOK], AF.Exp)
    kb.fence(t, kb.se)
    for c in range(8):
        for j, (c0, w) in enumerate(t_tiles):
            bank_wait(2)
            mi = t.matmul(psf[2][:, 0:w], ones1_t[:], rows[0:1, c0:c0 + w],
                          start=True, stop=True)
            kb.pe.tick(mi)
            kb.fence(v, kb.pe)
            v.tensor_mul(yqb[:, c * TOK + c0:c * TOK + c0 + w],
                         x1i[:, c * TOK + c0:c * TOK + c0 + w],
                         psf[2][:, 0:w])
            bank_ready[2] = kb.ve.mark()

    if debug:
        kb.fence(nc.sync, kb.ve)
        kb.store(o_dbg2, yqb[:, 0:8 * TOK])
    for th in range(4):
        tc0 = th * 512
        for hc in range(16):
            wslice(w_cfc, 8, hc, wbuf)
            bank_wait(hc % 2)
            for k in range(8):
                mi = t.matmul(psf[hc % 2][:], wbuf[:, k * 128:(k + 1) * 128],
                              yqb[:, k * TOK + tc0:k * TOK + tc0 + 512],
                              start=(k == 0), stop=(k == 7))
            kb.pe.tick(mi)
            kb.fence(v, kb.pe, kb.se)
            v.tensor_scalar_max(sq[:], psf[hc % 2][:], 0.0)
            bank_ready[hc % 2] = kb.ve.mark()
            kb.fence(s, kb.ve)
            s.activation(hrelu[:, hc * 512:(hc + 1) * 512], sq[:], AF.Square)
        kb.fence(t, kb.se)
        for ocg in range(2):
            for bk in range(2, 6):
                bank_wait(bk)
            for hc in range(16):
                kb.fence(nc.gpsimd, kb.pe)
                kb.load(wbuf[:, 0:4 * 128].rearrange("p (k e) -> p k e", k=4),
                        w_cpj.rearrange("p (k e) -> p k e", k=16)
                        [:, hc, ocg * 512:(ocg + 1) * 512]
                        .rearrange("p (a e) -> p a e", a=4), gp=True)
                kb.fence(t, kb.gq)
                for oc4 in range(4):
                    mi = t.matmul(psf[2 + oc4][:],
                                  wbuf[:, oc4 * 128:(oc4 + 1) * 128],
                                  hrelu[:, hc * 512:(hc + 1) * 512],
                                  start=(hc == 0), stop=(hc == 15))
                kb.pe.tick(mi)
            for oc4 in range(4):
                oc = ocg * 4 + oc4
                kb.fence(nc.sync, kb.ve)
                kb.load(strm[:, 0:512],
                        xB6[:, oc * TOK + tc0:oc * TOK + tc0 + 512])
                kb.load(strm2[:, 0:512],
                        xorig[:, oc * TOK + tc0:oc * TOK + tc0 + 512])
                kb.fence(v, kb.pe, kb.ld, kb.st)
                v.tensor_add(stage[:, 0:512], psf[2 + oc4][:], strm[:, 0:512])
                bank_ready[2 + oc4] = kb.ve.mark()
                v.tensor_add(stage[:, 0:512], stage[:, 0:512], strm2[:, 0:512])
                v.tensor_add(x1i[:, oc * TOK + tc0:oc * TOK + tc0 + 512],
                             x1i[:, oc * TOK + tc0:oc * TOK + tc0 + 512],
                             stage[:, 0:512])
                kb.fence(nc.sync, kb.ve)
                kb.store(o_x1[:, oc * TOK + tc0:oc * TOK + tc0 + 512],
                         x1i[:, oc * TOK + tc0:oc * TOK + tc0 + 512])

    return kb.finish()


def build_p8b():
    kb = KB()
    nc = kb.nc
    x1in = kb.inp("x1in", [128, 8 * TOK])
    w_qkv1 = kb.inp("w_qkv1", [128, 8 * 3072])
    ident = kb.inp("ident", [128, 128], BF16)
    onesbf = kb.inp("onesbf", [128, 1], BF16)
    ones1 = kb.inp("ones1", [1, 128])

    o_xn1 = kb.out("o_xn1", [128, 8 * TOK], BF16)
    o_q = kb.out("o_q", [128, 8 * TOK], BF16)
    o_k = kb.out("o_k", [128, 8 * TOK], BF16)
    o_vtm = kb.out("o_vtm", [128, 16 * 1024], BF16)
    o_kmax = kb.out("o_kmax", [1, 16])

    kb.open()
    v, s, g, t = kb.v, kb.s, kb.g, kb.t
    idt = kb.sb([128, 128], BF16, "idt")
    onesb_t = kb.sb([128, 1], BF16, "onesb")
    ones1_t = kb.sb([1, 128], F32, "ones1")
    x1i = kb.sb([128, 8 * TOK], F32, "x1i")
    xn1b = kb.sb([128, 8 * TOK], BF16, "xn1b")
    wbuf = kb.sb([128, 16 * 128], BF16, "wbuf")
    stage = kb.sb([128, 2 * 512], F32, "stage")
    bstage = kb.sb([128, 2 * 512], BF16, "bstage")
    hilo = kb.sb([128, 4 * 512], BF16, "hilo")
    sq = kb.sb([128, 512], F32, "sq")
    rows = kb.sb([1, 4 * TOK], F32, "rows")
    kmaxr = kb.sb([1, 16], F32, "kmaxr")
    krow = kb.sb([1, 512], F32, "krow")
    psf = [kb.ps([128, 512], F32, "psf") for _ in range(7)]
    psb = kb.ps([128, 1024], BF16, "psb")
    bank_ready = [None] * 8

    def bank_wait(bk):
        if bank_ready[bk] is not None:
            c, val = bank_ready[bk]
            t.wait_ge(c.sem, val)

    kb.load(idt[:], ident)
    kb.load(onesb_t[:], onesbf)
    kb.load(ones1_t[:], ones1)
    kb.load(x1i[:], x1in)
    t_tiles = tiles_of(TOK, 0)
    kb.fence(v, kb.ld)
    kb.fence(s, kb.ld)
    kb.fence(t, kb.ld)

    for (c0, w) in t_tiles:
        bank_wait(0)
        bank_wait(1)
        for c in range(8):
            xs_ap = x1i[:, c * TOK + c0:c * TOK + c0 + w]
            v.tensor_copy(hilo[:, 0:w], xs_ap)
            v.tensor_sub(hilo[:, w:2 * w], xs_ap, hilo[:, 0:w])
            kb.fence(s, kb.ve)
            s.activation(sq[:, 0:w], xs_ap, AF.Square)
            kb.fence(v, kb.se)
            v.tensor_copy(hilo[:, 2 * w:3 * w], sq[:, 0:w])
            v.tensor_sub(hilo[:, 3 * w:4 * w], sq[:, 0:w], hilo[:, 2 * w:3 * w])
            kb.fence(t, kb.ve)
            t.matmul(psf[0][0:1, 0:w], onesb_t[:], hilo[:, 0:w],
                     start=(c == 0), stop=False)
            mi = t.matmul(psf[0][0:1, 0:w], onesb_t[:], hilo[:, w:2 * w],
                          start=False, stop=(c == 7))
            kb.pe.tick(mi)
            t.matmul(psf[1][0:1, 0:w], onesb_t[:], hilo[:, 2 * w:3 * w],
                     start=(c == 0), stop=False)
            mi = t.matmul(psf[1][0:1, 0:w], onesb_t[:], hilo[:, 3 * w:4 * w],
                          start=False, stop=(c == 7))
            kb.pe.tick(mi)
            kb.fence(v, kb.pe)
        v.tensor_copy(rows[0:1, c0:c0 + w], psf[0][0:1, 0:w])
        v.tensor_copy(rows[0:1, TOK + c0:TOK + c0 + w], psf[1][0:1, 0:w])
        bank_ready[0] = kb.ve.mark()
        bank_ready[1] = kb.ve.mark()
    v.tensor_scalar_mul(rows[0:1, 2 * TOK:3 * TOK], rows[0:1, 0:TOK],
                        -1.0 / D)
    v.tensor_mul(rows[0:1, 0:TOK], rows[0:1, 2 * TOK:3 * TOK],
                 rows[0:1, 2 * TOK:3 * TOK])
    v.scalar_tensor_tensor(rows[0:1, 3 * TOK:4 * TOK],
                           rows[0:1, TOK:2 * TOK], 1.0 / D,
                           rows[0:1, 0:TOK], ALU.mult, ALU.subtract)
    v.tensor_scalar_add(rows[0:1, 3 * TOK:4 * TOK],
                        rows[0:1, 3 * TOK:4 * TOK], 1e-5)
    kb.fence(s, kb.ve)
    s.activation(rows[0:1, 3 * TOK:4 * TOK], rows[0:1, 3 * TOK:4 * TOK], AF.Ln)
    kb.fence(v, kb.se)
    v.tensor_scalar_mul(rows[0:1, 3 * TOK:4 * TOK],
                        rows[0:1, 3 * TOK:4 * TOK], -0.5)
    kb.fence(s, kb.ve)
    s.activation(rows[0:1, 3 * TOK:4 * TOK], rows[0:1, 3 * TOK:4 * TOK],
                 AF.Exp)
    kb.fence(t, kb.se)
    for c in range(8):
        for j, (c0, w) in enumerate(t_tiles):
            bank_wait(2)
            mi = t.matmul(psf[2][:, 0:w], ones1_t[:],
                          rows[0:1, 2 * TOK + c0:2 * TOK + c0 + w],
                          start=True, stop=True)
            kb.pe.tick(mi)
            bank_wait(3)
            mi = t.matmul(psf[3][:, 0:w], ones1_t[:],
                          rows[0:1, 3 * TOK + c0:3 * TOK + c0 + w],
                          start=True, stop=True)
            kb.pe.tick(mi)
            kb.fence(v, kb.pe)
            v.tensor_add(stage[:, 0:w], x1i[:, c * TOK + c0:c * TOK + c0 + w],
                         psf[2][:, 0:w])
            bank_ready[2] = kb.ve.mark()
            v.tensor_mul(xn1b[:, c * TOK + c0:c * TOK + c0 + w],
                         stage[:, 0:w], psf[3][:, 0:w])
            bank_ready[3] = kb.ve.mark()
        kb.fence(nc.sync, kb.ve)
        kb.store(o_xn1[:, c * TOK:(c + 1) * TOK], xn1b[:, c * TOK:(c + 1) * TOK])

    v.memset(kmaxr[:], 0.0)
    for part, i3 in (("q", 0), ("k", 1), ("v", 2)):
        for oc in range(8):
            kb.fence(nc.gpsimd, kb.pe)
            kb.load(wbuf[:, 0:8 * 128].rearrange("p (k e) -> p k e", k=8),
                    w_qkv1.rearrange("p (k e) -> p k e", k=8)
                    [:, :, i3 * 1024 + oc * 128:i3 * 1024 + (oc + 1) * 128],
                    gp=True)
            kb.fence(t, kb.gq)
            for j, (c0, w) in enumerate(t_tiles):
                bk = j % 2
                bank_wait(bk)
                for k in range(8):
                    mi = t.matmul(psf[bk][:, 0:w],
                                  wbuf[:, k * 128:(k + 1) * 128],
                                  xn1b[:, k * TOK + c0:k * TOK + c0 + w],
                                  start=(k == 0), stop=(k == 7))
                kb.pe.tick(mi)
                kb.fence(s, kb.pe, kb.st, kb.ve)
                s.copy(bstage[:, (j % 2) * 512:(j % 2) * 512 + w],
                       psf[bk][:, 0:w])
                bank_ready[bk] = kb.se.mark()
                dst = {"q": o_q, "k": o_k, "v": None}[part]
                if part != "v":
                    kb.fence(nc.sync, kb.se)
                    kb.store(dst[:, oc * TOK + c0:oc * TOK + c0 + w],
                             bstage[:, (j % 2) * 512:(j % 2) * 512 + w])
                if part == "k":
                    kb.fence(v, kb.se, kb.pe)
                    v.tensor_mul(hilo[:, 0:w],
                                 bstage[:, (j % 2) * 512:(j % 2) * 512 + w],
                                 bstage[:, (j % 2) * 512:(j % 2) * 512 + w])
                    kb.fence(t, kb.ve)
                    bank_wait(4)
                    mi = t.matmul(psf[4][0:1, 0:w], onesb_t[0:64, :],
                                  hilo[0:64, 0:w], start=True, stop=True)
                    kb.pe.tick(mi)
                    bank_wait(5)
                    mi = t.matmul(psf[5][0:1, 0:w], onesb_t[64:128, :],
                                  hilo[64:128, 0:w], start=True, stop=True)
                    kb.pe.tick(mi)
                    kb.fence(v, kb.pe)
                    v.reduce_max(krow[0:1, 0:1], psf[4][0:1, 0:w], axis=AX.X)
                    bank_ready[4] = kb.ve.mark()
                    v.tensor_tensor(kmaxr[0:1, 2 * oc:2 * oc + 1],
                                    kmaxr[0:1, 2 * oc:2 * oc + 1],
                                    krow[0:1, 0:1], ALU.max)
                    v.reduce_max(krow[0:1, 0:1], psf[5][0:1, 0:w], axis=AX.X)
                    bank_ready[5] = kb.ve.mark()
                    v.tensor_tensor(kmaxr[0:1, 2 * oc + 1:2 * oc + 2],
                                    kmaxr[0:1, 2 * oc + 1:2 * oc + 2],
                                    krow[0:1, 0:1], ALU.max)
                if part == "v":
                    for jb in range(w // 128):
                        kb.fence(t, kb.se, kb.ve)
                        kb.pe.tick(t.transpose(
                            psb[:, 0:128],
                            bstage[:, (j % 2) * 512 + jb * 128:
                                   (j % 2) * 512 + (jb + 1) * 128]
                            .bitcast(BF16) if False else
                            bstage[:, (j % 2) * 512 + jb * 128:
                                   (j % 2) * 512 + (jb + 1) * 128],
                            idt[:]))
                        kb.fence(v, kb.pe, kb.st)
                        tt = (c0 + jb * 128) // 128
                        v.tensor_copy(hilo[:, 0:128], psb[:, 0:128])
                        kb.fence(nc.sync, kb.ve)
                        kb.store(o_vtm[:, tt * 1024 + oc * 128:
                                       tt * 1024 + (oc + 1) * 128],
                                 hilo[:, 0:128])
    kb.fence(s, kb.ve)
    s.activation(kmaxr[:], kmaxr[:], AF.Sqrt)
    kb.fence(nc.sync, kb.se)
    kb.store(o_kmax, kmaxr[:])
    return kb.finish()


def p8a_inmaps(inputs, xn0_full, sz_cores, y2_full, xB6_cores):
    x = inputs["L0_attn_x"]
    w_op = chunk_fm(np.ascontiguousarray(inputs["L0_attn_out_proj_w"].T))
    w_cfc = chunk_fm(np.ascontiguousarray(inputs["L0_attn_c_fc_w"].T))
    w_cpj = chunk_fm(np.ascontiguousarray(inputs["L0_attn_c_proj_w"].T))
    cst = attn_consts()
    maps = []
    for c in range(NCORE):
        b, ch = c // 4, c % 4
        g0 = ch * TOK
        maps.append(dict(
            xn0=slice_tok_cm(xn0_full, b, g0, 0, TOK),
            sz=chunk_fm(sz_cores[c]),
            y2=chunk_fm(np.ascontiguousarray(y2_full[b][:, g0:g0 + TOK])),
            xorig=slice_tok_cm(x, b, g0, 0, TOK),
            xB6=xB6_cores[c],
            w_op=w_op, w_cfc=w_cfc, w_cpj=w_cpj,
            ident=cst["ident"], onesbf=np.ones((128, 1), dtype=BF),
            ones1=cst["ones1"],
        ))
    return maps


def p8b_inmaps(inputs, x1_cores):
    w_qkv1 = chunk_fm(inputs["L1_attn_w_qkv"])
    cst = attn_consts()
    return [dict(x1in=x1_cores[c], w_qkv1=w_qkv1, ident=cst["ident"],
                 onesbf=np.ones((128, 1), dtype=BF), ones1=cst["ones1"])
            for c in range(NCORE)]



def build_p9():
    kb = KB()
    nc = kb.nc
    x1in = kb.inp("x1in", [128, 8 * TOK])
    xn1 = kb.inp("xn1", [128, 8 * TOK], BF16)
    qin = kb.inp("qin", [128, 8 * TOK], BF16)
    kfull = kb.inp("kfull", [16, 64 * T], BF16)
    vtmf = kb.inp("vtmf", [16, T * 64], BF16)
    kpart = kb.inp("kpart", [1, 64])
    w1 = kb.inp("w1", [128, 8 * 4096])
    w2 = kb.inp("w2", [128, 8 * 4096])
    h1 = kb.inp("h1", [128, 32 * 4096])
    h2 = kb.inp("h2", [128, 32 * 4096])
    w3 = kb.inp("w3", [128, 32 * 1024])
    s1v = kb.inp("s1v", [128, 32])
    s2v = kb.inp("s2v", [128, 32])
    h1s = kb.inp("h1s", [128, 32])
    h2s = kb.inp("h2s", [128, 32])
    w3s = kb.inp("w3s", [128, 8])
    onesbf = kb.inp("onesbf", [128, 1], BF16)
    ones1 = kb.inp("ones1", [1, 128])

    o_out = kb.out("o_out", [128, 8 * TOK])

    kb.open()
    v, s, g, t = kb.v, kb.s, kb.g, kb.t
    onesb_t = kb.sb([128, 1], BF16, "onesb")
    ones1_t = kb.sb([1, 128], F32, "ones1")
    xn1b = kb.sb([128, 8 * TOK], BF16, "xn1b")
    O_all = kb.sb([128, 8 * TOK], BF16, "oall")
    kh = kb.sb([65, T], BF16, "kh")
    vh = kb.sb([128, 64 * 65], BF16, "vh")
    qh = kb.sb([65, TOK], BF16, "qh")
    qsq = kb.sb([64, TOK], BF16, "qsq")
    Pt = kb.sb([128, 2 * 1024], BF16, "pt")
    srow = kb.sb([1, TOK], F32, "srow")
    negm = kb.sb([1, TOK], F32, "negm")
    khat = kb.sb([1, 16], F32, "khat")
    khn = kb.sb([1, 16], F32, "khn")
    kpt = kb.sb([1, 64], F32, "kpt")
    lrow = kb.sb([1, 512], F32, "lrow")
    lbv = kb.sb([64, 512], F32, "lbv")
    hq1 = kb.sb([128, 32 * 512], BF16, "hq1")
    hq2 = kb.sb([128, 32 * 512], BF16, "hq2")
    gl = kb.sb([128, 512], F32, "gl")
    wbuf = kb.sb([128, 2 * 32 * 128], BF16, "wbuf")
    sc1 = kb.sb([128, 32], F32, "sc1")
    sc2 = kb.sb([128, 32], F32, "sc2")
    sch1 = kb.sb([128, 32], F32, "sch1")
    sch2 = kb.sb([128, 32], F32, "sch2")
    scw3 = kb.sb([128, 8], F32, "scw3")
    stage = kb.sb([128, 512], F32, "stage")
    strm = kb.sb([128, 512], F32, "strm")

    psS = [kb.ps([128, 1024], F32, "psS") for _ in range(2)]
    psO = kb.ps([128, 512], F32, "psO")
    psX = [kb.ps([128, 512], F32, "psX") for _ in range(2)]
    bank_ready = {}

    def bank_wait(key):
        if key in bank_ready:
            c, val = bank_ready[key]
            t.wait_ge(c.sem, val)

    kb.load(onesb_t[:], onesbf)
    kb.load(ones1_t[:], ones1)
    kb.load(xn1b[:], xn1)
    kb.load(kpt[:], kpart)
    kb.load(sc1[:], s1v)
    kb.load(sc2[:], s2v)
    kb.load(sch1[:], h1s)
    kb.load(sch2[:], h2s)
    kb.load(scw3[:], w3s)
    t_tiles = tiles_of(TOK, 0)

    kb.fence(v, kb.ld)
    for h in range(16):
        v.reduce_max(khat[0:1, h:h + 1], kpt[0:1, h:64:16], axis=AX.X)
    v.tensor_scalar_mul(khn[:], khat[:], -1.0)
    v.memset(kh[64:65, :], 1.0)
    v.memset(vh[:, 0:64 * 65], 1.0)

    for h in range(16):
        hb, oc = 64 * (h % 2), h // 2
        kb.fence(nc.sync, kb.pe, kb.ve)
        kb.load(kh[0:64, 0:T], kfull[h:h + 1, :]
                .rearrange("h (d t) -> (h d) t", d=64))
        kb.load(vh[:, 0:64 * 65].rearrange("p (a d) -> p a d", d=65)
                [:, :, 0:64],
                vtmf[h:h + 1, :].rearrange("h (a p d) -> (h p) a d",
                                           p=128, d=64))
        kb.fence(nc.sync, kb.ve, kb.pe, kb.se)
        kb.load(qh[0:64, :], qin[hb:hb + 64, oc * TOK:(oc + 1) * TOK])
        kb.fence(v, kb.ld)
        v.tensor_scalar_mul(qh[0:64, :], qh[0:64, :], 1.0 / 32.0)
        kb.fence(s, kb.ve)
        s.activation(qsq[:], qh[0:64, :], AF.Square)
        kb.fence(t, kb.se)
        for j2, (c0, w) in enumerate(t_tiles):
            bank_wait(("x", j2 % 2))
            mi = t.matmul(psX[j2 % 2][0:1, 0:w], onesb_t[0:64, :],
                          qsq[0:64, c0:c0 + w], start=True, stop=True)
            kb.pe.tick(mi)
            kb.fence(v, kb.pe)
            v.tensor_copy(srow[0:1, c0:c0 + w], psX[j2 % 2][0:1, 0:w])
            bank_ready[("x", j2 % 2)] = kb.ve.mark()
        kb.fence(s, kb.ve)
        s.activation(srow[:], srow[:], AF.Sqrt)
        kb.fence(v, kb.se)
        v.tensor_scalar_mul(negm[:], srow[:], khn[0:1, h:h + 1])
        kb.fence(v, kb.se)
        v.tensor_copy(qh[64:65, :], negm[0:1, :])
        kb.fence(t, kb.ve, kb.ld)
        for qt in range(4):
            q0 = qt * 512
            bank_wait(("O",))

            def emit_S(kp):
                bk = kp % 2
                bank_wait(("S", bk))
                for half in range(2):
                    kt = 2 * kp + half
                    mi = t.matmul(psS[bk][:, half * 512:(half + 1) * 512],
                                  kh[0:65, 128 * kt:128 * (kt + 1)],
                                  qh[0:65, q0:q0 + 512],
                                  start=True, stop=True)
                kb.pe.tick(mi)
                bank_ready[("Smm", bk)] = kb.pe.mark()

            emit_S(0)
            for kp in range(32):
                bk = kp % 2
                c0m, v0m = bank_ready[("Smm", bk)]
                s._eng.wait_ge(c0m.sem, v0m)
                kb.se.tick(s._eng.activation(
                    Pt[:, bk * 1024:(bk + 1) * 1024], psS[bk][:], AF.Exp))
                bank_ready[("S", bk)] = kb.se.mark()
                exp_mark = kb.se.mark()
                if kp + 1 < 32:
                    emit_S(kp + 1)
                cm, vm = exp_mark
                t.wait_ge(cm.sem, vm)
                for half in range(2):
                    kt = 2 * kp + half
                    mi = t.matmul(psO[0:65, :],
                                  vh[:, 65 * kt:65 * kt + 65],
                                  Pt[:, bk * 1024 + half * 512:
                                     bk * 1024 + (half + 1) * 512],
                                  start=(kt == 0), stop=(kt == 63))
                kb.pe.tick(mi)
            kb.fence(v, kb.pe)
            v.tensor_copy(lrow[:], psO[64:65, :])
            kb.fence(s, kb.ve)
            s.activation(lrow[:], lrow[:], AF.Ln)
            s.activation(lrow[:], lrow[:], AF.Exp, scale=-1.0)
            kb.fence(t, kb.se)
            bank_wait(("x", 0))
            mi = t.matmul(psX[0][0:64, :], ones1_t[0:1, 0:64], lrow[:],
                          start=True, stop=True)
            kb.pe.tick(mi)
            kb.fence(v, kb.pe)
            v.tensor_copy(lbv[:], psX[0][0:64, :])
            bank_ready[("x", 0)] = kb.ve.mark()
            v.tensor_mul(O_all[hb:hb + 64, oc * TOK + q0:oc * TOK + q0 + 512],
                         psO[0:64, :], lbv[:])
            bank_ready[("O",)] = kb.ve.mark()

    WB = 32 * 128

    def wload(wcm, K, oc, buf, cols=128):
        kb.load(wbuf[:, buf * WB:buf * WB + K * cols]
                .rearrange("p (k e) -> p k e", k=K),
                wcm.rearrange("p (k e) -> p k e", k=K)
                [:, :, oc * cols:(oc + 1) * cols], gp=True)

    wq_mark = [None, None]

    def wfence(buf):
        if wq_mark[buf] is not None:
            c, val = wq_mark[buf]
            nc.gpsimd.wait_ge(c.sem, val)

    for th in range(4):
        tc0 = th * 512
        plan = [(w1, 8, oc) for oc in range(32)]
        planB = [(w2, 8, oc) for oc in range(32)]
        wfence(0)
        wload(w1, 8, 0, 0)
        for oc in range(32):
            nxt = (oc + 1) % 32
            buf = oc % 2
            wfence(1 - buf)
            wload(w2, 8, oc, 1 - buf)
            kb.fence(t, kb.gq, kb.ve)
            bank_wait(("x", 0))
            for k in range(8):
                mi = t.matmul(psX[0][:],
                              wbuf[:, buf * WB + k * 128:buf * WB + (k + 1) * 128],
                              xn1b[:, k * TOK + tc0:k * TOK + tc0 + 512],
                              start=(k == 0), stop=(k == 7))
            kb.pe.tick(mi)
            wq_mark[buf] = kb.pe.mark()
            kb.fence(s, kb.pe)
            s.activation(gl[:], psX[0][:], AF.Silu, scale=sc1[:, oc:oc + 1])
            bank_ready[("x", 0)] = kb.se.mark()
            kb.fence(t, kb.gq)
            bank_wait(("x", 1))
            for k in range(8):
                mi = t.matmul(psX[1][:],
                              wbuf[:, (1 - buf) * WB + k * 128:
                                   (1 - buf) * WB + (k + 1) * 128],
                              xn1b[:, k * TOK + tc0:k * TOK + tc0 + 512],
                              start=(k == 0), stop=(k == 7))
            kb.pe.tick(mi)
            wq_mark[1 - buf] = kb.pe.mark()
            kb.fence(v, kb.pe, kb.se)
            v.scalar_tensor_tensor(hq1[:, oc * 512:(oc + 1) * 512],
                                   psX[1][:], sc2[:, oc:oc + 1], gl[:],
                                   ALU.mult, ALU.mult)
            bank_ready[("x", 1)] = kb.ve.mark()
            if oc < 31:
                wfence(buf)
                wload(w1, 8, oc + 1, buf)
        for (wcm, ssc, src_b, dst_b) in ((h1, sch1, hq1, hq2),
                                         (h2, sch2, hq2, hq1)):
            wfence(0)
            wload(wcm, 32, 0, 0)
            for oc in range(32):
                buf = oc % 2
                if oc < 31:
                    wfence(1 - buf)
                    wload(wcm, 32, oc + 1, 1 - buf)
                kb.fence(t, kb.gq)
                bank_wait(("x", oc % 2))
                for k in range(32):
                    mi = t.matmul(psX[oc % 2][:],
                                  wbuf[:, buf * WB + k * 128:
                                       buf * WB + (k + 1) * 128],
                                  src_b[:, k * 512:(k + 1) * 512],
                                  start=(k == 0), stop=(k == 31))
                kb.pe.tick(mi)
                wq_mark[buf] = kb.pe.mark()
                kb.fence(s, kb.pe)
                s.activation(dst_b[:, oc * 512:(oc + 1) * 512],
                             psX[oc % 2][:],
                             AF.Gelu, scale=ssc[:, oc:oc + 1])
                bank_ready[("x", oc % 2)] = kb.se.mark()
        wfence(0)
        wload(w3, 32, 0, 0)
        for oc2 in range(8):
            buf = oc2 % 2
            if oc2 < 7:
                wfence(1 - buf)
                wload(w3, 32, oc2 + 1, 1 - buf)
            kb.fence(t, kb.gq, kb.se)
            bank_wait(("x", 0))
            for k in range(32):
                mi = t.matmul(psX[0][:],
                              wbuf[:, buf * WB + k * 128:
                                   buf * WB + (k + 1) * 128],
                              hq1[:, k * 512:(k + 1) * 512],
                              start=(k == 0), stop=(k == 31))
            kb.pe.tick(mi)
            wq_mark[buf] = kb.pe.mark()
            kb.fence(nc.sync, kb.ve)
            kb.load(strm[:, 0:512], x1in[:, oc2 * TOK + tc0:
                                         oc2 * TOK + tc0 + 512])
            kb.fence(v, kb.pe, kb.ld, kb.st)
            v.scalar_tensor_tensor(stage[:, 0:512], psX[0][:],
                                   scw3[:, oc2:oc2 + 1], strm[:, 0:512],
                                   ALU.mult, ALU.add)
            bank_ready[("x", 0)] = kb.ve.mark()
            v.tensor_add(stage[:, 0:512], stage[:, 0:512],
                         O_all[:, oc2 * TOK + tc0:oc2 * TOK + tc0 + 512])
            kb.fence(nc.sync, kb.ve)
            kb.store(o_out[:, oc2 * TOK + tc0:oc2 * TOK + tc0 + 512],
                     stage[:, 0:512])
    return kb.finish()


def p9_inmaps(inputs, x1_cores, xn1_cores, q_cores, k_cores, vtm_cores,
              kmax_cores):
    w1 = chunk_fm(np.ascontiguousarray(inputs["L1_ffn_w1_w"].T))
    w2 = chunk_fm(np.ascontiguousarray(inputs["L1_ffn_w2_w"].T))
    h1 = chunk_fm(np.ascontiguousarray(inputs["L1_ffn_h1_w"].T))
    h2 = chunk_fm(np.ascontiguousarray(inputs["L1_ffn_h2_w"].T))
    w3 = chunk_fm(np.ascontiguousarray(inputs["L1_ffn_w3_w"].T))
    s1v = chunk_fm(inputs["L1_ffn_w1_s"][:, None])
    s2v = chunk_fm(inputs["L1_ffn_w2_s"][:, None])
    h1s = chunk_fm(inputs["L1_ffn_h1_s"][:, None])
    h2s = chunk_fm(inputs["L1_ffn_h2_s"][:, None])
    w3s = chunk_fm(inputs["L1_ffn_w3_s"][:, None])
    cst = attn_consts()
    maps = []
    for bb in range(2):
        Kfm = np.concatenate(
            [from_cm_np(k_cores[4 * bb + j], TOK) for j in range(4)], axis=1)
        kf = np.ascontiguousarray(
            Kfm.reshape(16, 64, T)).reshape(16, 64 * T)
        Vtm = np.concatenate(
            [vtm_cores[4 * bb + j].reshape(128, 16, 1024)
             .transpose(1, 0, 2).reshape(TOK, 1024)
             for j in range(4)], axis=0)
        vtmf = np.ascontiguousarray(
            Vtm.reshape(T, 16, 64).transpose(1, 0, 2)).reshape(16, T * 64)
        kpart = np.zeros((1, 64), np.float32)
        for j in range(4):
            kpart[0, 16 * j:16 * (j + 1)] = kmax_cores[4 * bb + j][0]
        for j in range(4):
            c = 4 * bb + j
            maps.append(dict(
                x1in=x1_cores[c], xn1=xn1_cores[c], qin=q_cores[c],
                kfull=kf, vtmf=vtmf, kpart=kpart,
                w1=w1, w2=w2, h1=h1, h2=h2, w3=w3,
                s1v=s1v, s2v=s2v, h1s=h1s, h2s=h2s, w3s=w3s,
                onesbf=np.ones((128, 1), dtype=BF), ones1=cst["ones1"],
            ))
    assert len(maps) == NCORE
    return maps


def from_cm_np(a, cols):
    return np.ascontiguousarray(
        a.reshape(128, 8, cols).transpose(1, 0, 2).reshape(1024, cols))



_PROGS = {}


def _prog(name, builder):
    if name not in _PROGS:
        _PROGS[name] = builder()
    return _PROGS[name]


def _run(name, builder, maps):
    nc = _prog(name, builder)
    res = run_bass_kernel_spmd(nc, maps, core_ids=list(range(NCORE)))
    return res.results


def kernel(**inputs):
    inputs = {k: np.asarray(v) for k, v in inputs.items()}
    r1 = _run("p1", build_p1, p1_inmaps(inputs))
    xn0_full = np.stack(
        [np.concatenate([r1[4 * b + j]["o_xn0"] for j in range(4)], 0)
         for b in range(2)])
    r2 = _run("p2", build_p2, p2_inmaps(inputs, r1))
    y2_full = [np.concatenate([r2[4 * b + j]["o_y2"] for j in range(4)], 0)
               for b in range(2)]
    xB_full = xn0_full
    for layer in range(6):
        ra = _run("attn", build_attn, attn_inmaps(inputs, layer, xB_full))
        xA_cms = [ra[c]["o_xA"] for c in range(NCORE)]
        rf = _run("ffn", build_ffn, ffn_inmaps(inputs, layer, xA_cms))
        xB_full = np.stack(
            [np.concatenate(
                [from_cm_np(rf[4 * b + j]["o_xB"], TOK).T for j in range(4)],
                0) for b in range(2)])
    xB6_cores = [chunk_fm(np.ascontiguousarray(
        xB_full[c // 4, (c % 4) * TOK:(c % 4 + 1) * TOK].T))
        for c in range(NCORE)]
    r8a = _run("p8a", build_p8a,
               p8a_inmaps(inputs, xn0_full,
                          [r1[c]["o_sz"] for c in range(NCORE)],
                          y2_full, xB6_cores))
    x1_cores = [r8a[c]["o_x1"] for c in range(NCORE)]
    r8b = _run("p8b", build_p8b, p8b_inmaps(inputs, x1_cores))
    r9 = _run("p9", build_p9,
              p9_inmaps(inputs, x1_cores,
                        [r8b[c]["o_xn1"] for c in range(NCORE)],
                        [r8b[c]["o_q"] for c in range(NCORE)],
                        [r8b[c]["o_k"] for c in range(NCORE)],
                        [r8b[c]["o_vtm"] for c in range(NCORE)],
                        [r8b[c]["o_kmax"] for c in range(NCORE)]))
    out = np.empty((2, T, D), np.float32)
    for c in range(NCORE):
        b, ch = c // 4, c % 4
        out[b, ch * TOK:(ch + 1) * TOK] = from_cm_np(r9[c]["o_out"], TOK).T
    return out


# version 03
# speedup vs baseline: 1.6511x; 1.2758x over previous
import math
from contextlib import ExitStack

import numpy as np
import ml_dtypes
import concourse.bass as bass
import concourse.mybir as mybir
from concourse.bass_utils import run_bass_kernel_spmd

F32 = mybir.dt.float32
BF16 = mybir.dt.bfloat16
AF = mybir.ActivationFunctionType
ALU = mybir.AluOpType
AX = mybir.AxisListType

NCORE = 8
B = 2
T = 8192
TOK = 2048
D = 1024
EI = 2048
DST = 16
DTR = 64
HALO = 514
F32EPS = float(np.finfo(np.float32).eps)
BF = ml_dtypes.bfloat16


class Ctr:

    def __init__(self, sem):
        self.sem = sem
        self.v = 0

    def tick(self, instr, n=1):
        instr.then_inc(self.sem, n)
        self.v += n
        return instr

    def wait(self, eng):
        p = getattr(self, "_proxy", None)
        if p is not None:
            p.flush()
        eng.wait_ge(self.sem, self.v)

    def mark(self):
        p = getattr(self, "_proxy", None)
        if p is not None:
            p.flush()
        return (self, self.v)


_OPS = {
    "tensor_copy", "tensor_tensor", "tensor_add", "tensor_sub", "tensor_mul",
    "tensor_max", "tensor_scalar", "tensor_single_scalar", "tensor_scalar_add",
    "tensor_scalar_sub", "tensor_scalar_mul", "tensor_scalar_max",
    "tensor_scalar_min", "tensor_relu", "tensor_reduce", "reduce_max",
    "reduce_sum", "scalar_tensor_tensor", "tensor_tensor_scan", "memset",
    "memzero", "reciprocal", "select", "transpose", "bn_stats", "bn_aggr",
    "iota", "activation", "square", "sqrt", "mul", "copy", "add", "sign",
    "engine_nop", "stream_shuffle", "pool", "pool_avg", "pool_max",
    "copy_predicated",
}


class EngProxy:

    SMALL = 512

    def __init__(self, eng, ctr):
        self._eng = eng
        self._ctr = ctr
        self._last = None
        self._need_wait = False

    def flush(self):
        if self._last is not None:
            self._ctr.tick(self._last)
            self._last = None

    def mark(self):
        self.flush()
        return self._ctr.mark()

    def __getattr__(self, name):
        attr = getattr(self._eng, name)
        if name not in _OPS:
            return attr

        def wrap(*a, **k):
            if self._need_wait:
                self._eng.wait_ge(self._ctr.sem, self._ctr.v)
                self._need_wait = False
            instr = attr(*a, **k)
            try:
                fs = a[0].free_size()
            except Exception:
                fs = 0
            if fs < self.SMALL:
                self.flush()
                self._ctr.tick(instr)
                self._need_wait = True
            else:
                if self._last is not None:
                    self._ctr.tick(self._last)
                self._last = instr
            return instr

        return wrap

    def wait_ge(self, sem, v):
        return self._eng.wait_ge(sem, v)

    def wait(self, mark):
        ctr, v = mark if isinstance(mark, tuple) else (mark, mark.v)
        return self._eng.wait_ge(ctr.sem, v)


class KB:

    def __init__(self):
        self.nc = bass.Bass("TRN2", target_bir_lowering=False,
                            detect_race_conditions=False)
        self.ctx = ExitStack()
        self._n = 0

    def name(self, p):
        self._n += 1
        return f"{p}{self._n}"

    def inp(self, name, shape, dt=F32):
        return self.nc.dram_tensor(name, shape, dt, kind="ExternalInput").ap()

    def out(self, name, shape, dt=F32):
        return self.nc.dram_tensor(name, shape, dt, kind="ExternalOutput").ap()

    def sb(self, shape, dt=F32, tag="sb"):
        return self.ctx.enter_context(
            self.nc.sbuf_tensor(self.name(tag), shape, dt))

    def ps(self, shape, dt=F32, tag="ps"):
        return self.ctx.enter_context(
            self.nc.psum_tensor(self.name(tag), shape, dt))

    def ctr(self, tag="sem"):
        return Ctr(self.ctx.enter_context(self.nc.semaphore(self.name(tag))))

    def open(self):
        self.blk = self.ctx.enter_context(self.nc.Block())
        self.ld = self.ctr("ld")
        self.gq = self.ctr("gq")
        self.st = self.ctr("st")
        self.pe = self.ctr("pe")
        self.ve = self.ctr("ve")
        self.se = self.ctr("se")
        self.ge = self.ctr("ge")
        self.v = EngProxy(self.nc.vector, self.ve)
        self.s = EngProxy(self.nc.scalar, self.se)
        self.g = EngProxy(self.nc.gpsimd, self.ge)
        self.t = self.nc.tensor
        self._proxy_of = {id(self.ve): self.v, id(self.se): self.s,
                          id(self.ge): self.g}
        self.ve._proxy = self.v
        self.se._proxy = self.s
        self.ge._proxy = self.g
        return self

    def finish(self):
        self.nc.sync.wait_ge(self.st.sem, self.st.v)
        self.ctx.close()
        return self.nc

    def load(self, dst_ap, src_ap, gp=False, slow=False):
        q = self.nc.gpsimd if gp else self.nc.sync
        ctr = self.gq if gp else self.ld
        kw = dict(allow_slow_non_contiguous=True) if slow else {}
        ctr.tick(q.dma_start(out=dst_ap, in_=src_ap, **kw), 16)

    def store(self, dst_ap, src_ap, slow=False):
        kw = dict(allow_slow_non_contiguous=True) if slow else {}
        self.st.tick(self.nc.sync.dma_start(out=dst_ap, in_=src_ap, **kw), 16)

    def fence(self, consumer, *ctrs):
        eng = consumer._eng if isinstance(consumer, EngProxy) else consumer
        for c in ctrs:
            p = self._proxy_of.get(id(c))
            if p is not None:
                p.flush()
            eng.wait_ge(c.sem, c.v)



def build_p1():
    kb = KB()
    nc = kb.nc
    xs = kb.inp("xs", [3 + TOK, D])
    w_in0 = kb.inp("w_in0", [128, 8 * EI])
    w_in1 = kb.inp("w_in1", [128, 8 * EI])
    conv_w = kb.inp("conv_w", [128, 16 * 4])
    conv_b = kb.inp("conv_b", [128, 16])
    w_dtl = kb.inp("w_dtl", [128, 16 * DTR])
    w_Bp = kb.inp("w_Bp", [128, 16 * DST])
    w_Cp = kb.inp("w_Cp", [128, 16 * DST])
    w_dtp = kb.inp("w_dtp", [DTR, EI])
    dtp_b = kb.inp("dtp_b", [128, 16])
    ident = kb.inp("ident", [128, 128], BF16)

    o_xn0 = kb.out("o_xn0", [TOK, D])
    o_sz = kb.out("o_sz", [EI, TOK])
    o_xconv = kb.out("o_xconv", [EI, TOK])
    o_dt = kb.out("o_dt", [EI, TOK])
    o_B = kb.out("o_B", [DST, TOK])
    o_C = kb.out("o_C", [DST, TOK])

    NT = TOK // 128
    T2 = 3 + TOK
    kb.open()
    v, s, t = kb.v, kb.s, kb.t

    idt = kb.sb([128, 128], BF16, "idt")
    rn_fm = kb.sb([128, 8 * T2], BF16, "rnfm")
    wh = kb.sb([128, 8 * EI], BF16, "wh")
    xcb = kb.sb([128, 16 * TOK], BF16, "xcb")
    cw = kb.sb([128, 16 * 4], F32, "cw")
    cb = kb.sb([128, 16], F32, "cb")
    dtpb = kb.sb([128, 16], F32, "dtpb")
    wb_dtl = kb.sb([128, 16 * DTR], BF16, "wdtl")
    wb_B = kb.sb([128, 16 * DST], BF16, "wB")
    wb_C = kb.sb([128, 16 * DST], BF16, "wC")
    wb_dtp = kb.sb([DTR, EI], BF16, "wdtp")
    dtl_b = kb.sb([DTR, TOK], BF16, "dtl")

    kb.load(idt[:], ident)
    kb.load(cw[:], conv_w)
    kb.load(cb[:], conv_b)
    kb.load(dtpb[:], dtp_b)
    kb.load(wb_dtl[:], w_dtl, gp=True)
    kb.load(wb_B[:], w_Bp, gp=True)
    kb.load(wb_C[:], w_Cp, gp=True)
    kb.load(wb_dtp[:], w_dtp, gp=True)

    x_tm = kb.sb([128, D], F32, "xtm")
    sq = kb.sb([128, D], F32, "sq")
    s1 = kb.sb([128, 1], F32, "s1")
    s2 = kb.sb([128, 1], F32, "s2")
    var = kb.sb([128, 1], F32, "var")
    rstd = kb.sb([128, 1], F32, "rstd")
    rrms = kb.sb([128, 1], F32, "rrms")
    xn_tm = kb.sb([128, D], F32, "xntm")
    rn_tm = kb.sb([128, D], BF16, "rntm")
    ps_tr = kb.ps([128, 128], BF16, "pstr")

    def ln_rms_tile(rows, x_ap, col0):
        kb.fence(v, kb.ld, kb.st)
        kb.fence(s, kb.ld, kb.st)
        v.reduce_sum(s1[0:rows], x_ap, axis=AX.X)
        s.activation(sq[0:rows], x_ap, AF.Square)
        kb.fence(v, kb.se)
        v.reduce_sum(s2[0:rows], sq[0:rows], axis=AX.X)
        v.tensor_scalar_mul(s1[0:rows], s1[0:rows], -1.0 / D)
        v.tensor_mul(var[0:rows], s1[0:rows], s1[0:rows])
        v.scalar_tensor_tensor(var[0:rows], s2[0:rows], 1.0 / D, var[0:rows],
                               ALU.mult, ALU.subtract)
        v.tensor_scalar_add(var[0:rows], var[0:rows], 1e-5)
        kb.fence(s, kb.ve)
        s.activation(rstd[0:rows], var[0:rows], AF.Sqrt)
        kb.fence(v, kb.se)
        v.reciprocal(rstd[0:rows], rstd[0:rows])
        v.tensor_scalar(xn_tm[0:rows], x_ap, s1[0:rows], rstd[0:rows],
                        ALU.add, ALU.mult)
        v.tensor_scalar_add(rrms[0:rows], var[0:rows], -1e-5)
        v.tensor_mul(rrms[0:rows], rrms[0:rows], rstd[0:rows])
        v.tensor_mul(rrms[0:rows], rrms[0:rows], rstd[0:rows])
        v.tensor_scalar_add(rrms[0:rows], rrms[0:rows], F32EPS)
        kb.fence(s, kb.ve)
        s.activation(rrms[0:rows], rrms[0:rows], AF.Ln)
        kb.fence(v, kb.se)
        v.tensor_scalar_mul(rrms[0:rows], rrms[0:rows], -0.5)
        kb.fence(s, kb.ve)
        s.activation(rrms[0:rows], rrms[0:rows], AF.Exp)
        kb.fence(v, kb.se)
        v.tensor_scalar_mul(rn_tm[0:rows], xn_tm[0:rows], rrms[0:rows])
        kb.fence(t, kb.ve)
        for c in range(8):
            kb.pe.tick(t.transpose(ps_tr[0:128, 0:rows],
                                   rn_tm[0:rows, 128 * c:128 * (c + 1)],
                                   idt[0:rows, 0:rows]))
            kb.fence(v, kb.pe)
            v.tensor_copy(rn_fm[:, c * T2 + col0: c * T2 + col0 + rows],
                          ps_tr[0:128, 0:rows])
            kb.fence(t, kb.ve)

    xh = kb.sb([3, D], F32, "xh")
    kb.load(xh[:], xs[0:3, :])
    ln_rms_tile(3, xh[:], 0)
    for i in range(NT):
        kb.fence(nc.sync, kb.ve, kb.se)
        kb.load(x_tm[:], xs[3 + 128 * i: 3 + 128 * (i + 1), :])
        ln_rms_tile(128, x_tm[:], 3 + 128 * i)
        kb.fence(nc.sync, kb.ve)
        kb.store(o_xn0[128 * i:128 * (i + 1), :], xn_tm[0:128, :])

    ps_mm = [kb.ps([128, 512], F32, "pmm") for _ in range(6)]
    bank_ready = [None] * 6
    xin_c = kb.sb([128, T2], F32, "xinc")
    conv_t = kb.sb([128, TOK], F32, "convt")
    z_t = conv_t

    def tiles_for(total, off):
        out, p = [], off
        while p < off + total:
            w = min(512, off + total - p)
            out.append((p, w))
            p += w
        return out

    kb.load(wh[:], w_in0, gp=True)

    def bank_wait(bk):
        if bank_ready[bk] is not None:
            c, val = bank_ready[bk]
            t.wait_ge(c.sem, val)

    def in_proj_chunk(o, tiles, dst, dst_off, evac_eng):
        for j, (c0, w) in enumerate(tiles):
            bk = (o * len(tiles) + j) % 4
            bank_wait(bk)
            for k in range(8):
                mi = t.matmul(ps_mm[bk][:, 0:w],
                              wh[:, k * EI + 128 * o: k * EI + 128 * (o + 1)],
                              rn_fm[:, k * T2 + c0: k * T2 + c0 + w],
                              start=(k == 0), stop=(k == 7))
            kb.pe.tick(mi)
            kb.fence(evac_eng, kb.pe)
            evac_eng.tensor_copy(dst[:, c0 + dst_off:c0 + dst_off + w],
                                 ps_mm[bk][:, 0:w])
            bank_ready[bk] = evac_eng._ctr.mark()

    xin_tiles = tiles_for(T2, 0)
    kb.fence(t, kb.ve, kb.gq)
    for o in range(16):
        in_proj_chunk(o, xin_tiles, xin_c, 0, v)
        cw_o = [cw[:, o * 4 + k: o * 4 + k + 1] for k in range(4)]
        v.tensor_scalar(conv_t[:], xin_c[:, 0:TOK], cw_o[0], cb[:, o:o + 1],
                        ALU.mult, ALU.add)
        for k in (1, 2, 3):
            v.scalar_tensor_tensor(conv_t[:], xin_c[:, k:k + TOK],
                                   cw_o[k], conv_t[:], ALU.mult, ALU.add)
        kb.fence(s, kb.ve, kb.st)
        s.activation(xin_c[:, 0:TOK], conv_t[:], AF.Silu)
        s.activation(xcb[:, o * TOK:(o + 1) * TOK], conv_t[:], AF.Silu)
        kb.fence(nc.sync, kb.se)
        kb.store(o_xconv[128 * o:128 * (o + 1), :], xin_c[:, 0:TOK])
        kb.fence(v, kb.st)

    kb.fence(nc.gpsimd, kb.pe)
    kb.load(wh[:], w_in1, gp=True)
    z_tiles = tiles_for(TOK, 3)
    kb.fence(t, kb.gq)
    for o in range(16):
        in_proj_chunk(o, z_tiles, z_t, -3, v)
        kb.fence(s, kb.ve)
        s.activation(z_t[:], z_t[:], AF.Silu)
        kb.fence(nc.sync, kb.se)
        kb.store(o_sz[128 * o:128 * (o + 1), :], z_t[:])
        kb.fence(v, kb.st)
        kb.fence(s, kb.st)

    bc_stage = kb.sb([DST, 2 * 512], F32, "bcst")
    kb.fence(t, kb.se, kb.gq)
    for j, (c0, w) in enumerate(tiles_for(TOK, 0)):
        for (dst_out, wmat, rows) in ((None, wb_dtl, DTR), (o_B, wb_B, DST),
                                      (o_C, wb_C, DST)):
            bk = {DTR: j % 2, DST: 4 + j % 2}[rows]
            bank_wait(bk)
            cols = {DTR: DTR, DST: DST}[rows]
            for k in range(16):
                mi = t.matmul(ps_mm[bk][0:rows, 0:w],
                              wmat[:, k * cols:(k + 1) * cols],
                              xcb[:, k * TOK + c0: k * TOK + c0 + w],
                              start=(k == 0), stop=(k == 15))
            kb.pe.tick(mi)
            kb.fence(v, kb.pe)
            if dst_out is None:
                v.tensor_copy(dtl_b[0:rows, c0:c0 + w], ps_mm[bk][0:rows, 0:w])
            else:
                slot = 0 if dst_out is o_B else 1
                kb.fence(v, kb.st)
                v.tensor_copy(bc_stage[0:DST, 512 * slot:512 * slot + w],
                              ps_mm[bk][0:rows, 0:w])
                kb.fence(nc.sync, kb.ve)
                kb.store(dst_out[:, c0:c0 + w],
                         bc_stage[0:DST, 512 * slot:512 * slot + w])
            bank_ready[bk] = kb.ve.mark()

    t_relu = kb.sb([128, 512], F32, "trelu")
    t_sp = kb.sb([128, 512], F32, "tsp")
    dt_stage = kb.sb([128, 2 * 512], F32, "dtstage")
    kb.fence(t, kb.ve)
    for o in range(16):
        for j, (c0, w) in enumerate(tiles_for(TOK, 0)):
            bk = 2 + (o * 4 + j) % 2
            slot = (o * 4 + j) % 2
            bank_wait(bk)
            mi = t.matmul(ps_mm[bk][:, 0:w], wb_dtp[:, 128 * o:128 * (o + 1)],
                          dtl_b[:, c0:c0 + w], start=True, stop=True)
            kb.pe.tick(mi)
            kb.fence(v, kb.pe)
            b_o = dtpb[:, o:o + 1]
            v.tensor_scalar(t_relu[:, 0:w], ps_mm[bk][:, 0:w], b_o, 0.0,
                            ALU.add, ALU.max)
            kb.fence(s, kb.ve)
            s.activation(t_sp[:, 0:w], ps_mm[bk][:, 0:w], AF.Abs, bias=b_o)
            bank_ready[bk] = kb.se.mark()
            s.activation(t_sp[:, 0:w], t_sp[:, 0:w], AF.Exp, scale=-1.0)
            s.activation(t_sp[:, 0:w], t_sp[:, 0:w], AF.Ln, bias=1.0)
            kb.fence(v, kb.se, kb.st)
            v.tensor_add(dt_stage[:, 512 * slot:512 * slot + w],
                         t_relu[:, 0:w], t_sp[:, 0:w])
            kb.fence(nc.sync, kb.ve)
            kb.store(o_dt[128 * o:128 * (o + 1), c0:c0 + w],
                     dt_stage[:, 512 * slot:512 * slot + w])
    return kb.finish()


def chunk_fm(a, p=128):
    K, c = a.shape
    return np.ascontiguousarray(
        a.reshape(K // p, p, c).transpose(1, 0, 2).reshape(p, -1))


_IDENT = np.eye(128).astype(BF)


def p1_inmaps(inputs):
    x = inputs["L0_attn_x"]
    w_inT = inputs["L0_attn_in_proj_w"].T
    w_in0 = chunk_fm(np.ascontiguousarray(w_inT[:, 0:EI]))
    w_in1 = chunk_fm(np.ascontiguousarray(w_inT[:, EI:2 * EI]))
    conv_w = chunk_fm(inputs["L0_attn_conv_w"])
    conv_b = chunk_fm(inputs["L0_attn_conv_b"][:, None])
    xp = inputs["L0_attn_x_proj_w"].T
    w_dtl = chunk_fm(np.ascontiguousarray(xp[:, 0:DTR]))
    w_Bp = chunk_fm(np.ascontiguousarray(xp[:, DTR:DTR + DST]))
    w_Cp = chunk_fm(np.ascontiguousarray(xp[:, DTR + DST:DTR + 2 * DST]))
    w_dtp = np.ascontiguousarray(inputs["L0_attn_dt_proj_w"].T)
    dtp_b = chunk_fm(inputs["L0_attn_dt_proj_b"][:, None])
    shared = dict(w_in0=w_in0, w_in1=w_in1, conv_w=conv_w, conv_b=conv_b,
                  w_dtl=w_dtl, w_Bp=w_Bp, w_Cp=w_Cp, w_dtp=w_dtp,
                  dtp_b=dtp_b, ident=_IDENT)
    maps = []
    for c in range(NCORE):
        b, ch = c // 4, c % 4
        g0 = ch * TOK
        xs = np.zeros((3 + TOK, D), np.float32)
        xs[3:] = x[b, g0:g0 + TOK]
        if g0 >= 3:
            xs[0:3] = x[b, g0 - 3:g0]
        maps.append(dict(xs=np.ascontiguousarray(xs), **shared))
    return maps


def kernel(**inputs):
    raise NotImplementedError("under construction")


def build_p2():
    kb = KB()
    nc = kb.nc
    dts = kb.inp("dts", [512, T])
    xcs = kb.inp("xcs", [512, T])
    A_log_s = kb.inp("A_log_s", [512, DST])
    D_s = kb.inp("D_s", [128, 4])
    Bsf = kb.inp("Bsf", [DST, T])
    Csf = kb.inp("Csf", [DST, T])
    o_y2 = kb.out("o_y2", [512, T])

    TC = 512
    NTC = T // TC
    kb.open()
    v, s, g, t = kb.v, kb.s, kb.g, kb.t

    A_f = kb.sb([128, 4 * DST], F32, "Af")
    Dv = kb.sb([128, 4], F32, "Dv")
    carries = kb.sb([128, 4 * DST], F32, "carry")
    dA = kb.sb([128, DST * TC], F32, "dA")
    wB = kb.sb([128, DST * TC], F32, "wB")
    hh = kb.sb([128, DST * TC], F32, "hh")
    Bb = kb.sb([128, DST * TC], F32, "Bb")
    Cb = kb.sb([128, DST * TC], F32, "Cb")
    dt_t = kb.sb([128, 2 * TC], F32, "dtt")
    xc_t = kb.sb([128, 2 * TC], F32, "xct")
    w_t = kb.sb([128, TC], F32, "wt")
    y_st = kb.sb([128, 2 * TC], F32, "yst")

    kb.load(Dv[:], D_s)
    al = kb.sb([128, 4 * DST], F32, "al")
    for ec in range(4):
        kb.load(al[:, ec * DST:(ec + 1) * DST],
                A_log_s[ec * 128:(ec + 1) * 128, :])
    kb.fence(s, kb.ld)
    s.activation(A_f[:], al[:], AF.Exp)
    kb.fence(v, kb.se)
    v.tensor_scalar_mul(A_f[:], A_f[:], -1.0)
    v.memset(carries[:], 0.0)

    for tc in range(NTC):
        kb.fence(nc.sync, kb.ge)
        for ss in range(DST):
            kb.load(Bb[:, ss * TC:(ss + 1) * TC],
                    Bsf[ss:ss + 1, tc * TC:(tc + 1) * TC]
                    .partition_broadcast(128).squeeze(1))
            kb.load(Cb[:, ss * TC:(ss + 1) * TC],
                    Csf[ss:ss + 1, tc * TC:(tc + 1) * TC]
                    .partition_broadcast(128).squeeze(1))
        for ec in range(4):
            buf = ec % 2
            kb.fence(nc.sync, kb.se, kb.ge, kb.ve)
            kb.load(dt_t[:, buf * TC:(buf + 1) * TC],
                    dts[ec * 128:(ec + 1) * 128, tc * TC:(tc + 1) * TC])
            kb.load(xc_t[:, buf * TC:(buf + 1) * TC],
                    xcs[ec * 128:(ec + 1) * 128, tc * TC:(tc + 1) * TC])
            dtb = dt_t[:, buf * TC:(buf + 1) * TC]
            xcb_ = xc_t[:, buf * TC:(buf + 1) * TC]
            kb.fence(s, kb.ld, kb.ve, kb.ge)
            for ss in range(DST):
                s.activation(dA[:, ss * TC:(ss + 1) * TC], dtb, AF.Exp,
                             scale=A_f[:, ec * DST + ss:ec * DST + ss + 1])
            kb.fence(g, kb.ld, kb.ve)
            g.tensor_mul(w_t[:], dtb, xcb_)
            for ss in range(DST):
                g.tensor_mul(wB[:, ss * TC:(ss + 1) * TC], w_t[:],
                             Bb[:, ss * TC:(ss + 1) * TC])
            kb.fence(v, kb.se, kb.ge)
            for ss in range(DST):
                v.tensor_tensor_scan(
                    hh[:, ss * TC:(ss + 1) * TC],
                    dA[:, ss * TC:(ss + 1) * TC],
                    wB[:, ss * TC:(ss + 1) * TC],
                    carries[:, ec * DST + ss:ec * DST + ss + 1],
                    ALU.mult, ALU.add)
            v.tensor_copy(carries[:, ec * DST:(ec + 1) * DST],
                          hh[:, TC - 1::TC])
            kb.fence(g, kb.ve)
            for ss in range(DST):
                g.tensor_mul(dA[:, ss * TC:(ss + 1) * TC],
                             hh[:, ss * TC:(ss + 1) * TC],
                             Cb[:, ss * TC:(ss + 1) * TC])
            kb.fence(v, kb.ge)
            half = DST // 2
            while half >= 1:
                v.tensor_add(dA[:, 0:half * TC], dA[:, 0:half * TC],
                             dA[:, half * TC:2 * half * TC])
                half //= 2
            kb.fence(v, kb.st)
            v.scalar_tensor_tensor(y_st[:, buf * TC:(buf + 1) * TC],
                                   xcb_, Dv[:, ec:ec + 1], dA[:, 0:TC],
                                   ALU.mult, ALU.add)
            kb.fence(nc.sync, kb.ve)
            kb.store(o_y2[ec * 128:(ec + 1) * 128, tc * TC:(tc + 1) * TC],
                     y_st[:, buf * TC:(buf + 1) * TC])
    return kb.finish()


def p2_inmaps(inputs, p1_res):
    A_log = inputs["L0_attn_A_log"]
    Dvec = inputs["L0_attn_D"]
    maps = []
    for c in range(NCORE):
        b, sl = c // 4, c % 4
        dts = np.concatenate([p1_res[4 * b + j]["o_dt"] for j in range(4)], 1)
        xcs = np.concatenate([p1_res[4 * b + j]["o_xconv"] for j in range(4)], 1)
        Bsf = np.concatenate([p1_res[4 * b + j]["o_B"] for j in range(4)], 1)
        Csf = np.concatenate([p1_res[4 * b + j]["o_C"] for j in range(4)], 1)
        rows = slice(sl * 512, (sl + 1) * 512)
        maps.append(dict(
            dts=np.ascontiguousarray(dts[rows]),
            xcs=np.ascontiguousarray(xcs[rows]),
            A_log_s=np.ascontiguousarray(A_log[rows]),
            D_s=chunk_fm(Dvec[rows][:, None]),
            Bsf=np.ascontiguousarray(Bsf),
            Csf=np.ascontiguousarray(Csf),
        ))
    return maps


TH = HALO + TOK
OW = TOK + 1


def tiles_of(total, off=0, wmax=512):
    out, p = [], off
    while p < off + total:
        w = min(wmax, off + total - p)
        out.append((p, w))
        p += w
    return out


def build_attn(debug=False):
    kb = KB()
    nc = kb.nc
    xB = kb.inp("xB", [128, 8 * TH])
    wqkv = kb.inp("wqkv", [128, 8 * 3072])
    wo = kb.inp("wo", [128, 8 * 1024])
    qsv = kb.inp("qsv", [128, 1])
    ksv = kb.inp("ksv", [128, 1])
    csq8 = kb.inp("csq8", [128, TH])
    ssq8 = kb.inp("ssq8", [128, TH])
    csk = kb.inp("csk", [128, TH])
    ssk = kb.inp("ssk", [128, TH])
    maskg = kb.inp("maskg", [128, 4 * 256], BF16)
    mask0 = kb.inp("mask0", [128, 4 * 256], BF16)
    ident = kb.inp("ident", [128, 128], BF16)
    sel33b = kb.inp("sel33b", [128, 33], BF16)
    selB33 = kb.inp("selB33", [33, 128])
    ones1 = kb.inp("ones1", [1, 128])
    onesb = kb.inp("onesb", [128, 1], BF16)
    onesr = kb.inp("onesr", [1, 128], BF16)
    identf = kb.inp("identf", [128, 128])

    o_xA = kb.out("o_xA", [128, 8 * OW])
    o_dbg1 = kb.out("o_dbg1", [128, TH]) if debug else None
    o_dbg2 = kb.out("o_dbg2", [128, TH]) if debug else None

    kb.open()
    v, s, g, t = kb.v, kb.s, kb.g, kb.t

    idt = kb.sb([128, 128], BF16, "idt")
    qsb = kb.sb([128, 1], F32, "qsb")
    ksb = kb.sb([128, 1], F32, "ksb")
    sel33b_t = kb.sb([128, 33], BF16, "sel33")
    selB33_t = kb.sb([33, 128], F32, "selB")
    ones1_t = kb.sb([1, 128], F32, "ones1")
    onesb_t = kb.sb([128, 1], BF16, "onesb")
    onesr_t = kb.sb([1, 128], BF16, "onesr")
    maskg_t = kb.sb([128, 4 * 256], BF16, "mg")
    mask0_t = kb.sb([128, 4 * 256], BF16, "m0")

    xqb = kb.sb([128, 8 * TH], BF16, "xqb")
    x_f = kb.sb([128, TH], F32, "xf")
    t1 = kb.sb([128, TH], F32, "t1")
    qpb = kb.sb([128, TH], BF16, "qpb")
    kpb = kb.sb([128, TH], BF16, "kpb")
    vpb = kb.sb([128, TH], BF16, "vpb")
    V_tm = kb.sb([128, 20 * 130], BF16, "vtm")
    O_all = kb.sb([128, 8 * OW], BF16, "oall")
    hilo = kb.sb([128, 2 * TH], BF16, "hilo")
    rnr = kb.sb([33, TH], F32, "rnr")
    mcol = kb.sb([128, 36], F32, "mcol")
    msb = kb.sb([36, 128], F32, "msb")
    negrow = kb.sb([1, 2 * 2304], F32, "negrow")
    mrow_d = kb.nc.dram_tensor("mrow_d", [36, 128], F32)
    idtf = kb.sb([128, 128], F32, "idtf")
    lrow = kb.sb([1, 256], F32, "lrow")
    lbv = kb.sb([64, 256], F32, "lbv")
    Pt2 = kb.sb([128, 2 * 1024], BF16, "pt")
    wpart = kb.sb([128, 2 * 8 * 128], BF16, "wp")
    tbl = kb.sb([128, 2 * 512], F32, "tbl")
    xbt = kb.sb([128, 2 * 512], F32, "xbt")
    xa_st = kb.sb([128, 2 * 512], F32, "xast")

    psf = [kb.ps([128, 512], F32, "psf") for _ in range(7)]
    psb = kb.ps([128, 1024], BF16, "psb")
    bank_ready = {}

    def bank_wait(bk):
        if bk in bank_ready:
            c, val = bank_ready[bk]
            t.wait_ge(c.sem, val)

    kb.load(idt[:], ident)
    kb.load(qsb[:], qsv)
    kb.load(ksb[:], ksv)
    kb.load(sel33b_t[:], sel33b)
    kb.load(selB33_t[:], selB33)
    kb.load(ones1_t[:], ones1)
    kb.load(onesb_t[:], onesb)
    kb.load(onesr_t[:], onesr)
    kb.load(idtf[:], identf)
    kb.load(maskg_t[:], maskg)
    kb.load(mask0_t[:], mask0)
    kb.load(xqb[:], xB, gp=True)
    kb.fence(v, kb.ld)
    v.memset(rnr[:], 1.0)
    v.memset(O_all[:], 0.0)
    v.memset(V_tm[:], 1.0)
    v.memset(x_f[:, 0:2], 0.0)
    v.memset(t1[:, 0:2], 0.0)

    qkv_tiles = tiles_of(2560, 2)

    def rhs_slice(k, c0, w):
        off = k * TH + c0 - (1 if k >= 4 else 0)
        return xqb[:, off:off + w]

    def qk_prep(part, hp):
        csT, ssT = (csq8, ssq8) if part == "q" else (csk, ssk)
        scl = qsb if part == "q" else ksb
        dstb = qpb if part == "q" else kpb
        if debug and hp == 7 and part == "k":
            kb.fence(nc.sync, kb.ve)
            kb.store(o_dbg1, x_f[:])
        kb.fence(s, kb.ve)
        s.activation(t1[:], x_f[:], AF.Square)
        kb.fence(v, kb.se)
        v.tensor_copy(hilo[:, 0:TH], t1[:])
        v.tensor_sub(hilo[:, TH:2 * TH], t1[:], hilo[:, 0:TH])
        kb.fence(t, kb.ve)
        for j, (c0, w) in enumerate(qkv_tiles):
            bank_wait(3)
            t.matmul(psf[3][0:33, 0:w], sel33b_t[:], hilo[:, c0:c0 + w],
                     start=True, stop=False)
            mi = t.matmul(psf[3][0:33, 0:w], sel33b_t[:],
                          hilo[:, TH + c0:TH + c0 + w],
                          start=False, stop=True)
            kb.pe.tick(mi)
            kb.fence(v, kb.pe)
            v.tensor_copy(rnr[0:33, c0:c0 + w], psf[3][0:33, 0:w])
            bank_ready[3] = kb.ve.mark()
        v.tensor_scalar_max(rnr[:], rnr[:], 1e-24)
        kb.fence(s, kb.ve)
        s.activation(rnr[:], rnr[:], AF.Ln)
        kb.fence(v, kb.se)
        v.tensor_scalar_mul(rnr[:], rnr[:], -0.5)
        kb.fence(s, kb.ve)
        s.activation(rnr[:], rnr[:], AF.Exp)
        kb.fence(t, kb.se)
        for j, (c0, w) in enumerate(qkv_tiles):
            bank_wait(2)
            mi = t.matmul(psf[2][:, 0:w], selB33_t[:], rnr[:, c0:c0 + w],
                          start=True, stop=True)
            kb.pe.tick(mi)
            kb.fence(v, kb.pe)
            v.scalar_tensor_tensor(t1[:, c0:c0 + w], x_f[:, c0:c0 + w],
                                   scl[:, 0:1], psf[2][:, 0:w],
                                   ALU.mult, ALU.mult)
            bank_ready[2] = kb.ve.mark()
        if debug and hp == 7 and part == "k":
            kb.fence(nc.sync, kb.ve)
            kb.store(o_dbg2, t1[:])
        for (dst0, src0, sgn) in ((0, 32, -1.0), (32, 0, 1.0),
                                  (64, 96, -1.0), (96, 64, 1.0)):
            v.tensor_scalar_mul(x_f[dst0:dst0 + 32, :],
                                t1[src0:src0 + 32, :], sgn)
        for j, (c0, w) in enumerate(qkv_tiles):
            bf = (j % 2) * 512
            kb.fence(nc.sync, kb.ve)
            kb.load(tbl[:, bf:bf + w], csT[:, c0:c0 + w])
            kb.fence(v, kb.ld)
            v.tensor_mul(t1[:, c0:c0 + w], t1[:, c0:c0 + w], tbl[:, bf:bf + w])
            kb.fence(nc.sync, kb.ve)
            kb.load(tbl[:, bf:bf + w], ssT[:, c0:c0 + w])
            kb.fence(v, kb.ld)
            v.tensor_mul(x_f[:, c0:c0 + w], x_f[:, c0:c0 + w],
                         tbl[:, bf:bf + w])
            v.tensor_add(t1[:, c0:c0 + w], t1[:, c0:c0 + w],
                         x_f[:, c0:c0 + w])
        v.tensor_copy(dstb[:], t1[:])
        return

    for hp in range(8):
        for i3, part in enumerate(("q", "k", "v")):
            wslc = wqkv.rearrange("p (k e) -> p k e", k=8)[
                :, :, i3 * 1024 + hp * 128: i3 * 1024 + (hp + 1) * 128]
            kb.fence(nc.gpsimd, kb.pe)
            kb.load(wpart[:, 0:8 * 128].rearrange("p (k e) -> p k e", k=8),
                    wslc, gp=True)
            kb.fence(t, kb.gq, kb.ve, kb.se)
            for j, (c0, w) in enumerate(qkv_tiles):
                bk = j % 2
                bank_wait(bk)
                for k in range(8):
                    mi = t.matmul(psf[bk][:, 0:w],
                                  wpart[:, k * 128:(k + 1) * 128],
                                  rhs_slice(k, c0, w),
                                  start=(k == 0), stop=(k == 7))
                kb.pe.tick(mi)
                if part == "v":
                    kb.fence(s, kb.pe)
                    s.copy(vpb[:, c0:c0 + w], psf[bk][:, 0:w])
                    bank_ready[bk] = kb.se.mark()
                else:
                    kb.fence(v, kb.pe)
                    v.tensor_copy(x_f[:, c0:c0 + w], psf[bk][:, 0:w])
                    bank_ready[bk] = kb.ve.mark()
            if part != "v":
                qk_prep(part, hp)
        kb.fence(t, kb.se)
        for jt in range(20):
            kb.pe.tick(t.transpose(psb[:, 0:128],
                                   vpb[:, 2 + 128 * jt: 2 + 128 * (jt + 1)],
                                   idt[:]))
            kb.fence(v, kb.pe)
            v.tensor_copy(V_tm[:, 130 * jt:130 * jt + 64], psb[:, 0:64])
            v.tensor_copy(V_tm[:, 130 * jt + 65:130 * jt + 129],
                          psb[:, 64:128])
            kb.fence(t, kb.ve)
        for h in (0, 1):
            hb = 64 * h
            for qt in range(18):
                qt0 = 258 + 128 * qt
                k0 = qt0 - 256
                bank_wait(2)
                mi = t.matmul(psf[2][:, 0:384],
                              qpb[hb:hb + 64, qt0:qt0 + 128],
                              kpb[hb:hb + 64, k0:k0 + 384],
                              start=True, stop=True)
                kb.pe.tick(mi)
                kb.fence(v, kb.pe)
                v.reduce_max(mcol[:, h * 18 + qt:h * 18 + qt + 1],
                             psf[2][:, 0:384], axis=AX.X, negate=True)
                bank_ready[2] = kb.ve.mark()
        kb.fence(t, kb.ve)
        bank_wait(2)
        kb.pe.tick(t.transpose(psf[2][0:36, 0:128], mcol[:], idtf[0:128, 0:128]))
        kb.fence(v, kb.pe)
        v.tensor_copy(msb[:], psf[2][0:36, 0:128])
        bank_ready[2] = kb.ve.mark()
        kb.fence(nc.sync, kb.ve, kb.ld)
        kb.store(mrow_d.ap(), msb[:])
        kb.fence(nc.sync, kb.st)
        kb.load(negrow[:], mrow_d.ap().rearrange("a b -> (a b)").unsqueeze(0))
        kb.fence(t, kb.ld)
        groups = [(wi, h) for wi in range(9) for h in (0, 1)]

        def emit_S(i):
            wi, h = groups[i]
            par = i % 2
            qc0 = 258 + 256 * wi
            kc0 = qc0 - 256
            hb = 64 * h
            for kt in range(4):
                bk = (4 if par == 0 else 0) + kt // 2
                co = (kt % 2) * 256
                if kt % 2 == 0:
                    bank_wait(bk)
                t.matmul(psf[bk][:, co:co + 256],
                         kpb[hb:hb + 64, kc0 + 128 * kt:kc0 + 128 * (kt + 1)],
                         qpb[hb:hb + 64, qc0:qc0 + 256],
                         start=True, stop=False)
                mi = t.matmul(psf[bk][:, co:co + 256],
                              ones1_t[0:1, :],
                              negrow[0:1, h * 2304 + 256 * wi:
                                     h * 2304 + 256 * (wi + 1)],
                              start=False, stop=True)
                kb.pe.tick(mi)
            bank_ready[("smm", par)] = kb.pe.mark()

        emit_S(0)
        for i, (wi, h) in enumerate(groups):
            par = i % 2
            b0, b1 = (4, 5) if par == 0 else (0, 1)
            hb = 64 * h
            mask_t = mask0_t if wi == 1 else maskg_t
            cm, vm = bank_ready[("smm", par)]
            s._eng.wait_ge(cm.sem, vm)
            s._eng.activation(Pt2[:, par * 1024:par * 1024 + 512], psf[b0][:],
                              AF.Exp)
            kb.se.tick(s._eng.activation(
                Pt2[:, par * 1024 + 512:par * 1024 + 1024], psf[b1][:],
                AF.Exp))
            bank_ready[b0] = kb.se.mark()
            bank_ready[b1] = kb.se.mark()
            kb.fence(g, kb.se)
            g.tensor_mul(Pt2[:, par * 1024:(par + 1) * 1024],
                         Pt2[:, par * 1024:(par + 1) * 1024], mask_t[:])
            gm = kb.ge.mark()
            if i + 1 < len(groups):
                emit_S(i + 1)
            t.wait_ge(gm[0].sem, gm[1])
            bank_wait(6)
            jt0 = (258 + 256 * wi - 256 - 2) // 128
            for kt in range(4):
                mi = t.matmul(psf[6][0:65, 0:256],
                              V_tm[:, 130 * (jt0 + kt) + 65 * h:
                                   130 * (jt0 + kt) + 65 * h + 65],
                              Pt2[:, par * 1024 + 256 * kt:
                                  par * 1024 + 256 * (kt + 1)],
                              start=(kt == 0), stop=(kt == 3))
            kb.pe.tick(mi)
            kb.fence(v, kb.pe)
            v.tensor_copy(lrow[:], psf[6][64:65, 0:256])
            kb.fence(s, kb.ve)
            s.activation(lrow[:], lrow[:], AF.Ln)
            s.activation(lrow[:], lrow[:], AF.Exp, scale=-1.0)
            kb.fence(t, kb.se)
            bank_wait(3)
            mi = t.matmul(psf[3][0:64, 0:256], ones1_t[0:1, 0:64], lrow[:],
                          start=True, stop=True)
            kb.pe.tick(mi)
            kb.fence(v, kb.pe)
            v.tensor_copy(lbv[:, 0:256], psf[3][0:64, 0:256])
            bank_ready[3] = kb.ve.mark()
            if wi == 0:
                v.tensor_mul(O_all[hb:hb + 64, hp * OW:hp * OW + 1],
                             psf[6][0:64, 255:256], lbv[:, 255:256])
            else:
                oc0 = hp * OW + 1 + 256 * (wi - 1)
                v.tensor_mul(O_all[hb:hb + 64, oc0:oc0 + 256],
                             psf[6][0:64, 0:256], lbv[:, 0:256])
            bank_ready[6] = kb.ve.mark()

    ow_tiles = tiles_of(OW, 0)
    for oc in range(8):
        wslc = wo.rearrange("p (k e) -> p k e", k=8)[
            :, :, oc * 128:(oc + 1) * 128]
        kb.fence(nc.gpsimd, kb.pe)
        kb.load(wpart[:, 0:8 * 128].rearrange("p (k e) -> p k e", k=8),
                wslc, gp=True)
        kb.fence(t, kb.gq)
        for j, (c0, w) in enumerate(ow_tiles):
            bk = j % 2
            bf = (j % 2) * 512
            kb.fence(nc.sync, kb.ve)
            kb.load(xbt[:, bf:bf + w], xB[:, oc * TH + 513 + c0:
                                          oc * TH + 513 + c0 + w],
                    slow=(w < 8))
            bank_wait(bk)
            for k in range(8):
                mi = t.matmul(psf[bk][:, 0:w],
                              wpart[:, k * 128:(k + 1) * 128],
                              O_all[:, k * OW + c0:k * OW + c0 + w],
                              start=(k == 0), stop=(k == 7))
            kb.pe.tick(mi)
            kb.fence(v, kb.pe, kb.ld, kb.st)
            v.tensor_add(xa_st[:, bf:bf + w], psf[bk][:, 0:w],
                         xbt[:, bf:bf + w])
            bank_ready[bk] = kb.ve.mark()
            kb.fence(nc.sync, kb.ve)
            kb.store(o_xA[:, oc * OW + c0:oc * OW + c0 + w],
                     xa_st[:, bf:bf + w], slow=(w < 8))
    return kb.finish()


def _rotary_tables():
    j = np.arange(32)
    inv_freq = 1.0 / 10000.0 ** (2 * j / 64.0)
    base = (2 * j + 0.4 * 64) / (1.4 * 64)
    tpos = np.arange(-HALO, T, dtype=np.float64)
    fr = np.outer(inv_freq, tpos)
    sc = base[:, None] ** ((tpos[None, :] - T // 2) / 128.0)
    cos = np.cos(fr)
    sin = np.sin(fr)
    d2 = lambda a: np.concatenate([a, a], 0)
    csq8 = d2(8.0 * cos * sc).astype(np.float32)
    ssq8 = d2(8.0 * sin * sc).astype(np.float32)
    csk = d2(cos / sc).astype(np.float32)
    ssk = d2(sin / sc).astype(np.float32)
    dup = lambda a: np.ascontiguousarray(np.tile(a, (2, 1)))
    return tuple(dup(a) for a in (csq8, ssq8, csk, ssk))


_TABLES = None


def rotary_tables():
    global _TABLES
    if _TABLES is None:
        _TABLES = _rotary_tables()
    return _TABLES


def _mask_tiles(zero_prev):
    m = np.zeros((128, 4 * 256), dtype=BF)
    for kt in range(4):
        for p in range(128):
            c = kt * 128 + p
            r = np.arange(256)
            ok = (r <= c) & (c <= r + 256)
            if zero_prev and kt < 2:
                ok = ok & False
            m[p, kt * 256:(kt + 1) * 256] = ok.astype(BF)
    return m


_SEL = None


def attn_consts():
    global _SEL
    if _SEL is None:
        sel33b = np.zeros((128, 33), dtype=BF)
        sel33b[0:64, 0] = 1
        sel33b[64:128, 32] = 1
        selB33 = np.zeros((33, 128), dtype=np.float32)
        selB33[0, 0:64] = 1
        selB33[32, 64:128] = 1
        ones1 = np.ones((1, 128), dtype=np.float32)
        onesb = np.ones((128, 1), dtype=BF)
        onesr = np.ones((1, 128), dtype=BF)
        maskg = _mask_tiles(False)
        mask0 = _mask_tiles(True)
        _SEL = dict(sel33b=sel33b, selB33=selB33, ones1=ones1, onesb=onesb,
                    onesr=onesr, maskg=maskg, mask0=mask0, ident=_IDENT,
                    identf=np.eye(128, dtype=np.float32))
    return _SEL


def slice_tok_cm(x_full, b, g0, lead, width):
    xs = np.zeros((width, D), np.float32)
    lo = g0 - lead
    src0 = max(0, lo)
    src1 = min(T, lo + width)
    xs[src0 - lo:src1 - lo, :] = x_full[b, src0:src1]
    return chunk_fm(np.ascontiguousarray(xs.T))


def attn_inmaps(inputs, layer, xB_full):
    cst = attn_consts()
    csq8, ssq8, csk, ssk = rotary_tables()
    wqkv = chunk_fm(inputs["L0_ffn_Wqkv"][layer])
    wo = chunk_fm(inputs["L0_ffn_Wo"][layer])
    qsv = np.ascontiguousarray(
        np.tile(inputs["L0_ffn_q_scale"][layer], 2)[:, None])
    ksv = np.ascontiguousarray(
        np.tile(inputs["L0_ffn_k_scale"][layer], 2)[:, None])
    maps = []
    for c in range(NCORE):
        b, ch = c // 4, c % 4
        g0 = ch * TOK
        cols = slice(g0 - 514 + HALO, g0 + TOK + HALO)
        maps.append(dict(
            xB=slice_tok_cm(xB_full, b, g0, 514, TH),
            wqkv=wqkv, wo=wo, qsv=qsv, ksv=ksv,
            csq8=np.ascontiguousarray(csq8[:, cols]),
            ssq8=np.ascontiguousarray(ssq8[:, cols]),
            csk=np.ascontiguousarray(csk[:, cols]),
            ssk=np.ascontiguousarray(ssk[:, cols]),
            maskg=cst["maskg"],
            mask0=cst["mask0"] if ch == 0 else cst["maskg"],
            ident=cst["ident"], sel33b=cst["sel33b"], selB33=cst["selB33"],
            ones1=cst["ones1"], onesb=cst["onesb"], onesr=cst["onesr"],
            identf=cst["identf"],
        ))
    return maps


HID = 2816


def build_ffn():
    kb = KB()
    nc = kb.nc
    xA = kb.inp("xA", [128, 8 * OW])
    w1a = kb.inp("w1a", [128, 8 * HID])
    w1g = kb.inp("w1g", [128, 8 * HID])
    w2 = kb.inp("w2", [128, 22 * 1024])
    b1a = kb.inp("b1a", [128, 22])
    b1g = kb.inp("b1g", [128, 22])
    b2v = kb.inp("b2v", [128, 8])
    lng = kb.inp("lng", [128, 8])
    lnb = kb.inp("lnb", [128, 8])
    ones1 = kb.inp("ones1", [1, 128])
    onesbf = kb.inp("onesbf", [128, 1], BF16)

    o_xB = kb.out("o_xB", [128, 8 * TOK])

    kb.open()
    v, s, g, t = kb.v, kb.s, kb.g, kb.t

    xA_f = kb.sb([128, 8 * OW], F32, "xaf")
    xnf = kb.sb([128, 8 * TOK], BF16, "xnf")
    hilo = kb.sb([128, 4 * 512], BF16, "hilo")
    sq = kb.sb([128, 512], F32, "sq")
    rows = kb.sb([1, 4 * TOK], F32, "rows")
    hbuf = kb.sb([128, 22 * 512], BF16, "hbuf")
    gl = kb.sb([128, 512], F32, "gl")
    tmp = kb.sb([128, 512], F32, "tmp")
    wca = kb.sb([128, 2 * 8 * 128], BF16, "wca")
    wcg = kb.sb([128, 2 * 8 * 128], BF16, "wcg")
    wc2 = kb.sb([128, 2 * 1024], BF16, "wc2")
    b1a_t = kb.sb([128, 22], F32, "b1at")
    b1g_t = kb.sb([128, 22], F32, "b1gt")
    b2_t = kb.sb([128, 8], F32, "b2t")
    lng_t = kb.sb([128, 8], F32, "lngt")
    lnb_t = kb.sb([128, 8], F32, "lnbt")
    ones1_t = kb.sb([1, 128], F32, "ones1")
    onesbf_t = kb.sb([128, 1], BF16, "onesbf")
    xst = kb.sb([128, 2 * 512], F32, "xst")

    psf = [kb.ps([128, 512], F32, "psf") for _ in range(8)]
    bank_ready = [None] * 8

    def bank_wait(bk):
        if bank_ready[bk] is not None:
            c, val = bank_ready[bk]
            t.wait_ge(c.sem, val)

    kb.load(xA_f[:], xA)
    kb.load(b1a_t[:], b1a)
    kb.load(b1g_t[:], b1g)
    kb.load(b2_t[:], b2v)
    kb.load(lng_t[:], lng)
    kb.load(lnb_t[:], lnb)
    kb.load(ones1_t[:], ones1)
    kb.load(onesbf_t[:], onesbf)

    def xsh(c, c0, w):
        off = c * OW + c0 + (1 if c < 4 else 0)
        return xA_f[:, off:off + w]

    t_tiles = tiles_of(TOK, 0)
    kb.fence(v, kb.ld)
    kb.fence(s, kb.ld)
    for (c0, w) in t_tiles:
        bank_wait(0)
        bank_wait(1)
        for c in range(8):
            v.tensor_copy(hilo[:, 0:w], xsh(c, c0, w))
            v.tensor_sub(hilo[:, w:2 * w], xsh(c, c0, w), hilo[:, 0:w])
            kb.fence(s, kb.ve)
            s.activation(sq[:, 0:w], xsh(c, c0, w), AF.Square)
            kb.fence(v, kb.se)
            v.tensor_copy(hilo[:, 2 * w:3 * w], sq[:, 0:w])
            v.tensor_sub(hilo[:, 3 * w:4 * w], sq[:, 0:w], hilo[:, 2 * w:3 * w])
            kb.fence(t, kb.ve)
            t.matmul(psf[0][0:1, 0:w], onesbf_t[:], hilo[:, 0:w],
                     start=(c == 0), stop=False)
            mi = t.matmul(psf[0][0:1, 0:w], onesbf_t[:], hilo[:, w:2 * w],
                          start=False, stop=(c == 7))
            kb.pe.tick(mi)
            t.matmul(psf[1][0:1, 0:w], onesbf_t[:], hilo[:, 2 * w:3 * w],
                     start=(c == 0), stop=False)
            mi = t.matmul(psf[1][0:1, 0:w], onesbf_t[:], hilo[:, 3 * w:4 * w],
                          start=False, stop=(c == 7))
            kb.pe.tick(mi)
            if c < 7:
                kb.fence(v, kb.pe)
        kb.fence(v, kb.pe)
        v.tensor_copy(rows[0:1, c0:c0 + w], psf[0][0:1, 0:w])
        v.tensor_copy(rows[0:1, TOK + c0:TOK + c0 + w], psf[1][0:1, 0:w])
        bank_ready[0] = kb.ve.mark()
        bank_ready[1] = kb.ve.mark()
    v.tensor_scalar_mul(rows[0:1, 2 * TOK:3 * TOK], rows[0:1, 0:TOK],
                        -1.0 / D)
    v.tensor_mul(rows[0:1, 0:TOK], rows[0:1, 2 * TOK:3 * TOK],
                 rows[0:1, 2 * TOK:3 * TOK])
    v.scalar_tensor_tensor(rows[0:1, 3 * TOK:4 * TOK],
                           rows[0:1, TOK:2 * TOK], 1.0 / D,
                           rows[0:1, 0:TOK], ALU.mult, ALU.subtract)
    v.tensor_scalar_add(rows[0:1, 3 * TOK:4 * TOK],
                        rows[0:1, 3 * TOK:4 * TOK], 1e-5)
    kb.fence(s, kb.ve)
    s.activation(rows[0:1, 3 * TOK:4 * TOK], rows[0:1, 3 * TOK:4 * TOK],
                 AF.Ln)
    kb.fence(v, kb.se)
    v.tensor_scalar_mul(rows[0:1, 3 * TOK:4 * TOK],
                        rows[0:1, 3 * TOK:4 * TOK], -0.5)
    kb.fence(s, kb.ve)
    s.activation(rows[0:1, 3 * TOK:4 * TOK], rows[0:1, 3 * TOK:4 * TOK],
                 AF.Exp)
    kb.fence(t, kb.se)
    for c in range(8):
        for j, (c0, w) in enumerate(t_tiles):
            bank_wait(2)
            mi = t.matmul(psf[2][:, 0:w], ones1_t[:],
                          rows[0:1, 2 * TOK + c0:2 * TOK + c0 + w],
                          start=True, stop=True)
            kb.pe.tick(mi)
            bank_wait(3)
            mi = t.matmul(psf[3][:, 0:w], ones1_t[:],
                          rows[0:1, 3 * TOK + c0:3 * TOK + c0 + w],
                          start=True, stop=True)
            kb.pe.tick(mi)
            kb.fence(v, kb.pe)
            v.tensor_add(tmp[:, 0:w], xsh(c, c0, w), psf[2][:, 0:w])
            bank_ready[2] = kb.ve.mark()
            v.tensor_mul(tmp[:, 0:w], tmp[:, 0:w], psf[3][:, 0:w])
            bank_ready[3] = kb.ve.mark()
            v.tensor_scalar(xnf[:, c * TOK + c0:c * TOK + c0 + w],
                            tmp[:, 0:w], lng_t[:, c:c + 1],
                            lnb_t[:, c:c + 1], ALU.mult, ALU.add)

    WB1 = 8 * 128
    wmark = {}

    def wprefetch(hc, buf):
        if ("w1", buf) in wmark:
            c, val = wmark[("w1", buf)]
            nc.gpsimd.wait_ge(c.sem, val)
        kb.load(wca[:, buf * WB1:(buf + 1) * WB1]
                .rearrange("p (k e) -> p k e", k=8),
                w1a.rearrange("p (k e) -> p k e", k=8)
                [:, :, hc * 128:(hc + 1) * 128], gp=True)
        kb.load(wcg[:, buf * WB1:(buf + 1) * WB1]
                .rearrange("p (k e) -> p k e", k=8),
                w1g.rearrange("p (k e) -> p k e", k=8)
                [:, :, hc * 128:(hc + 1) * 128], gp=True)
        ldm[buf] = kb.gq.v

    ldm = [0, 0]
    ldm2 = [0, 0]
    for th in range(4):
        tc0 = th * 512
        wprefetch(0, 0)
        for hc in range(22):
            buf = hc % 2
            if hc + 1 < 22:
                wprefetch(hc + 1, 1 - buf)
            t.wait_ge(kb.gq.sem, ldm[buf])
            kb.fence(t, kb.ve)
            bank_wait(0)
            bank_wait(1)
            for k in range(8):
                t.matmul(psf[0][:],
                         wca[:, buf * WB1 + k * 128:buf * WB1 + (k + 1) * 128],
                         xnf[:, k * TOK + tc0:k * TOK + tc0 + 512],
                         start=(k == 0), stop=(k == 7))
                mi = t.matmul(psf[1][:],
                              wcg[:, buf * WB1 + k * 128:
                                   buf * WB1 + (k + 1) * 128],
                              xnf[:, k * TOK + tc0:k * TOK + tc0 + 512],
                              start=(k == 0), stop=(k == 7))
            kb.pe.tick(mi)
            wmark[("w1", buf)] = kb.pe.mark()
            kb.fence(s, kb.pe)
            s.activation(gl[:], psf[1][:], AF.Gelu, bias=b1g_t[:, hc:hc + 1])
            bank_ready[1] = kb.se.mark()
            kb.fence(v, kb.se)
            v.scalar_tensor_tensor(hbuf[:, hc * 512:(hc + 1) * 512],
                                   psf[0][:], b1a_t[:, hc:hc + 1], gl[:],
                                   ALU.add, ALU.mult)
            bank_ready[0] = kb.ve.mark()
        for bk in range(8):
            bank_wait(bk)
        if ("w2", 0) in wmark:
            c, val = wmark[("w2", 0)]
            nc.gpsimd.wait_ge(c.sem, val)
        kb.load(wc2[:, 0:1024], w2[:, 0:1024], gp=True)
        ldm2[0] = kb.gq.v
        for hc in range(22):
            buf = hc % 2
            if hc + 1 < 22:
                if ("w2", 1 - buf) in wmark:
                    c, val = wmark[("w2", 1 - buf)]
                    nc.gpsimd.wait_ge(c.sem, val)
                kb.load(wc2[:, (1 - buf) * 1024:(2 - buf) * 1024],
                        w2[:, (hc + 1) * 1024:(hc + 2) * 1024], gp=True)
                ldm2[1 - buf] = kb.gq.v
            t.wait_ge(kb.gq.sem, ldm2[buf])
            for oc in range(8):
                mi = t.matmul(psf[oc][:],
                              wc2[:, buf * 1024 + oc * 128:
                                   buf * 1024 + (oc + 1) * 128],
                              hbuf[:, hc * 512:(hc + 1) * 512],
                              start=(hc == 0), stop=(hc == 21))
            kb.pe.tick(mi)
            wmark[("w2", buf)] = kb.pe.mark()
        for oc in range(8):
            kb.fence(v, kb.pe, kb.st)
            v.scalar_tensor_tensor(xst[:, (oc % 2) * 512:(oc % 2) * 512 + 512],
                                   psf[oc][:], b2_t[:, oc:oc + 1],
                                   xA_f[:, oc * OW + 1 + tc0:
                                        oc * OW + 1 + tc0 + 512],
                                   ALU.add, ALU.add)
            bank_ready[oc] = kb.ve.mark()
            kb.fence(nc.sync, kb.ve)
            kb.store(o_xB[:, oc * TOK + tc0:oc * TOK + tc0 + 512],
                     xst[:, (oc % 2) * 512:(oc % 2) * 512 + 512])
    return kb.finish()


def ffn_inmaps(inputs, layer, xA_cms):
    W1 = inputs["L0_ffn_W1"][layer]
    b1 = inputs["L0_ffn_b1"][layer]
    W2 = inputs["L0_ffn_W2"][layer]
    w1a = np.zeros((D, HID), np.float32)
    w1g = np.zeros((D, HID), np.float32)
    w1a[:, 0:2730] = W1[:, 0:2730]
    w1g[:, 0:2730] = W1[:, 2730:5460]
    b1a = np.zeros((HID, 1), np.float32)
    b1g = np.zeros((HID, 1), np.float32)
    b1a[0:2730, 0] = b1[0:2730]
    b1g[0:2730, 0] = b1[2730:5460]
    w2p = np.zeros((HID, 1024), np.float32)
    w2p[0:2730] = W2
    cst = attn_consts()
    shared = dict(
        w1a=chunk_fm(w1a), w1g=chunk_fm(w1g), w2=chunk_fm(w2p),
        b1a=chunk_fm(b1a), b1g=chunk_fm(b1g),
        b2v=chunk_fm(inputs["L0_ffn_b2"][layer][:, None]),
        lng=chunk_fm(inputs["L0_ffn_ln_g"][layer][:, None]),
        lnb=chunk_fm(inputs["L0_ffn_ln_b"][layer][:, None]),
        ones1=cst["ones1"],
        onesbf=np.ones((128, 1), dtype=BF),
    )
    return [dict(xA=xA_cms[c], **shared) for c in range(NCORE)]


def build_p8a(debug=False):
    kb = KB()
    nc = kb.nc
    xn0 = kb.inp("xn0", [128, 8 * TOK])
    sz = kb.inp("sz", [128, 16 * TOK])
    y2 = kb.inp("y2", [128, 16 * TOK])
    xorig = kb.inp("xorig", [128, 8 * TOK])
    xB6 = kb.inp("xB6", [128, 8 * TOK])
    w_op = kb.inp("w_op", [128, 16 * 1024])
    w_cfc = kb.inp("w_cfc", [128, 8 * EI])
    w_cpj = kb.inp("w_cpj", [128, 16 * 1024])
    ident = kb.inp("ident", [128, 128], BF16)
    onesbf = kb.inp("onesbf", [128, 1], BF16)
    ones1 = kb.inp("ones1", [1, 128])

    o_x1 = kb.out("o_x1", [128, 8 * TOK])
    o_dbg1 = kb.out("o_dbg1", [128, 8 * TOK]) if debug else None
    o_dbg2 = kb.out("o_dbg2", [128, 8 * TOK], BF16) if debug else None

    kb.open()
    v, s, g, t = kb.v, kb.s, kb.g, kb.t
    idt = kb.sb([128, 128], BF16, "idt")
    onesb_t = kb.sb([128, 1], BF16, "onesb")
    ones1_t = kb.sb([1, 128], F32, "ones1")
    x1i = kb.sb([128, 8 * TOK], F32, "x1i")
    yqb = kb.sb([128, 16 * TOK], BF16, "yqb")
    hrelu = kb.sb([128, 16 * 512], BF16, "hrelu")
    wbuf = kb.sb([128, 16 * 128], BF16, "wbuf")
    stage = kb.sb([128, 2 * 512], F32, "stage")
    bstage = kb.sb([128, 2 * 512], BF16, "bstage")
    hilo = kb.sb([128, 4 * 512], BF16, "hilo")
    sq = kb.sb([128, 512], F32, "sq")
    rows = kb.sb([1, TOK], F32, "rows")
    strm = kb.sb([128, 2 * 512], F32, "strm")
    strm2 = kb.sb([128, 2 * 512], F32, "strm2")

    psf = [kb.ps([128, 512], F32, "psf") for _ in range(7)]
    psb = kb.ps([128, 1024], BF16, "psb")
    bank_ready = [None] * 8

    def bank_wait(bk):
        if bank_ready[bk] is not None:
            c, val = bank_ready[bk]
            t.wait_ge(c.sem, val)

    kb.load(idt[:], ident)
    kb.load(onesb_t[:], onesbf)
    kb.load(ones1_t[:], ones1)
    t_tiles = tiles_of(TOK, 0)

    def wslice(wcm, K, oc, dst):
        kb.fence(nc.gpsimd, kb.pe)
        kb.load(dst[:, 0:K * 128].rearrange("p (k e) -> p k e", k=K),
                wcm.rearrange("p (k e) -> p k e", k=K)
                [:, :, oc * 128:(oc + 1) * 128], gp=True)
        kb.fence(t, kb.gq)

    kb.fence(v, kb.ld)
    for c in range(16):
        for j, (c0, w) in enumerate(t_tiles):
            bf = (j % 2) * 512
            kb.fence(nc.sync, kb.ge)
            kb.load(strm[:, bf:bf + w], y2[:, c * TOK + c0:c * TOK + c0 + w])
            kb.load(strm2[:, bf:bf + w], sz[:, c * TOK + c0:c * TOK + c0 + w])
            kb.fence(g, kb.ld)
            g.tensor_mul(yqb[:, c * TOK + c0:c * TOK + c0 + w],
                         strm[:, bf:bf + w], strm2[:, bf:bf + w])

    for oc in range(8):
        wslice(w_op, 16, oc, wbuf)
        for j, (c0, w) in enumerate(t_tiles):
            bk = j % 2
            bank_wait(bk)
            kb.fence(t, kb.ge) if (oc == 0 and j == 0) else None
            for k in range(16):
                mi = t.matmul(psf[bk][:, 0:w],
                              wbuf[:, k * 128:(k + 1) * 128],
                              yqb[:, k * TOK + c0:k * TOK + c0 + w],
                              start=(k == 0), stop=(k == 15))
            kb.pe.tick(mi)
            kb.fence(nc.sync, kb.ve)
            kb.load(strm[:, 0:w], xn0[:, oc * TOK + c0:oc * TOK + c0 + w])
            kb.fence(v, kb.pe, kb.ld)
            v.tensor_add(x1i[:, oc * TOK + c0:oc * TOK + c0 + w],
                         psf[bk][:, 0:w], strm[:, 0:w])
            bank_ready[bk] = kb.ve.mark()

    if debug:
        kb.fence(nc.sync, kb.ve)
        kb.store(o_dbg1, x1i[:])
    for (c0, w) in t_tiles:
        bank_wait(0)
        for c in range(8):
            kb.fence(s, kb.ve)
            s.activation(sq[:, 0:w], x1i[:, c * TOK + c0:c * TOK + c0 + w],
                         AF.Square)
            kb.fence(v, kb.se)
            v.tensor_copy(hilo[:, 0:w], sq[:, 0:w])
            v.tensor_sub(hilo[:, w:2 * w], sq[:, 0:w], hilo[:, 0:w])
            kb.fence(t, kb.ve)
            t.matmul(psf[0][0:1, 0:w], onesb_t[:], hilo[:, 0:w],
                     start=(c == 0), stop=False)
            mi = t.matmul(psf[0][0:1, 0:w], onesb_t[:], hilo[:, w:2 * w],
                          start=False, stop=(c == 7))
            kb.pe.tick(mi)
            if c < 7:
                kb.fence(v, kb.pe)
        kb.fence(v, kb.pe)
        v.tensor_copy(rows[0:1, c0:c0 + w], psf[0][0:1, 0:w])
        bank_ready[0] = kb.ve.mark()
    v.tensor_scalar_mul(rows[0:1, 0:TOK], rows[0:1, 0:TOK], 1.0 / D)
    v.tensor_scalar_add(rows[0:1, 0:TOK], rows[0:1, 0:TOK], F32EPS)
    kb.fence(s, kb.ve)
    s.activation(rows[0:1, 0:TOK], rows[0:1, 0:TOK], AF.Ln)
    kb.fence(v, kb.se)
    v.tensor_scalar_mul(rows[0:1, 0:TOK], rows[0:1, 0:TOK], -0.5)
    kb.fence(s, kb.ve)
    s.activation(rows[0:1, 0:TOK], rows[0:1, 0:TOK], AF.Exp)
    kb.fence(t, kb.se)
    for c in range(8):
        for j, (c0, w) in enumerate(t_tiles):
            bank_wait(2)
            mi = t.matmul(psf[2][:, 0:w], ones1_t[:], rows[0:1, c0:c0 + w],
                          start=True, stop=True)
            kb.pe.tick(mi)
            kb.fence(v, kb.pe)
            v.tensor_mul(yqb[:, c * TOK + c0:c * TOK + c0 + w],
                         x1i[:, c * TOK + c0:c * TOK + c0 + w],
                         psf[2][:, 0:w])
            bank_ready[2] = kb.ve.mark()

    if debug:
        kb.fence(nc.sync, kb.ve)
        kb.store(o_dbg2, yqb[:, 0:8 * TOK])
    for th in range(4):
        tc0 = th * 512
        for hc in range(16):
            wslice(w_cfc, 8, hc, wbuf)
            bank_wait(hc % 2)
            for k in range(8):
                mi = t.matmul(psf[hc % 2][:], wbuf[:, k * 128:(k + 1) * 128],
                              yqb[:, k * TOK + tc0:k * TOK + tc0 + 512],
                              start=(k == 0), stop=(k == 7))
            kb.pe.tick(mi)
            kb.fence(v, kb.pe, kb.se)
            v.tensor_scalar_max(sq[:], psf[hc % 2][:], 0.0)
            bank_ready[hc % 2] = kb.ve.mark()
            kb.fence(s, kb.ve)
            s.activation(hrelu[:, hc * 512:(hc + 1) * 512], sq[:], AF.Square)
        kb.fence(t, kb.se)
        for ocg in range(2):
            for bk in range(2, 6):
                bank_wait(bk)
            for hc in range(16):
                kb.fence(nc.gpsimd, kb.pe)
                kb.load(wbuf[:, 0:4 * 128].rearrange("p (k e) -> p k e", k=4),
                        w_cpj.rearrange("p (k e) -> p k e", k=16)
                        [:, hc, ocg * 512:(ocg + 1) * 512]
                        .rearrange("p (a e) -> p a e", a=4), gp=True)
                kb.fence(t, kb.gq)
                for oc4 in range(4):
                    mi = t.matmul(psf[2 + oc4][:],
                                  wbuf[:, oc4 * 128:(oc4 + 1) * 128],
                                  hrelu[:, hc * 512:(hc + 1) * 512],
                                  start=(hc == 0), stop=(hc == 15))
                kb.pe.tick(mi)
            for oc4 in range(4):
                oc = ocg * 4 + oc4
                kb.fence(nc.sync, kb.ve)
                kb.load(strm[:, 0:512],
                        xB6[:, oc * TOK + tc0:oc * TOK + tc0 + 512])
                kb.load(strm2[:, 0:512],
                        xorig[:, oc * TOK + tc0:oc * TOK + tc0 + 512])
                kb.fence(v, kb.pe, kb.ld, kb.st)
                v.tensor_add(stage[:, 0:512], psf[2 + oc4][:], strm[:, 0:512])
                bank_ready[2 + oc4] = kb.ve.mark()
                v.tensor_add(stage[:, 0:512], stage[:, 0:512], strm2[:, 0:512])
                v.tensor_add(x1i[:, oc * TOK + tc0:oc * TOK + tc0 + 512],
                             x1i[:, oc * TOK + tc0:oc * TOK + tc0 + 512],
                             stage[:, 0:512])
                kb.fence(nc.sync, kb.ve)
                kb.store(o_x1[:, oc * TOK + tc0:oc * TOK + tc0 + 512],
                         x1i[:, oc * TOK + tc0:oc * TOK + tc0 + 512])

    return kb.finish()


def build_p8b():
    kb = KB()
    nc = kb.nc
    x1in = kb.inp("x1in", [128, 8 * TOK])
    w_qkv1 = kb.inp("w_qkv1", [128, 8 * 3072])
    ident = kb.inp("ident", [128, 128], BF16)
    onesbf = kb.inp("onesbf", [128, 1], BF16)
    ones1 = kb.inp("ones1", [1, 128])

    o_xn1 = kb.out("o_xn1", [128, 8 * TOK], BF16)
    o_q = kb.out("o_q", [128, 8 * TOK], BF16)
    o_k = kb.out("o_k", [128, 8 * TOK], BF16)
    o_vtm = kb.out("o_vtm", [128, 16 * 1024], BF16)
    o_kmax = kb.out("o_kmax", [1, 16])

    kb.open()
    v, s, g, t = kb.v, kb.s, kb.g, kb.t
    idt = kb.sb([128, 128], BF16, "idt")
    onesb_t = kb.sb([128, 1], BF16, "onesb")
    ones1_t = kb.sb([1, 128], F32, "ones1")
    x1i = kb.sb([128, 8 * TOK], F32, "x1i")
    xn1b = kb.sb([128, 8 * TOK], BF16, "xn1b")
    wbuf = kb.sb([128, 16 * 128], BF16, "wbuf")
    stage = kb.sb([128, 2 * 512], F32, "stage")
    bstage = kb.sb([128, 2 * 512], BF16, "bstage")
    hilo = kb.sb([128, 4 * 512], BF16, "hilo")
    sq = kb.sb([128, 512], F32, "sq")
    rows = kb.sb([1, 4 * TOK], F32, "rows")
    kmaxr = kb.sb([1, 16], F32, "kmaxr")
    krow = kb.sb([1, 512], F32, "krow")
    psf = [kb.ps([128, 512], F32, "psf") for _ in range(7)]
    psb = kb.ps([128, 1024], BF16, "psb")
    bank_ready = [None] * 8

    def bank_wait(bk):
        if bank_ready[bk] is not None:
            c, val = bank_ready[bk]
            t.wait_ge(c.sem, val)

    kb.load(idt[:], ident)
    kb.load(onesb_t[:], onesbf)
    kb.load(ones1_t[:], ones1)
    kb.load(x1i[:], x1in)
    t_tiles = tiles_of(TOK, 0)
    kb.fence(v, kb.ld)
    kb.fence(s, kb.ld)
    kb.fence(t, kb.ld)

    for (c0, w) in t_tiles:
        bank_wait(0)
        bank_wait(1)
        for c in range(8):
            xs_ap = x1i[:, c * TOK + c0:c * TOK + c0 + w]
            v.tensor_copy(hilo[:, 0:w], xs_ap)
            v.tensor_sub(hilo[:, w:2 * w], xs_ap, hilo[:, 0:w])
            kb.fence(s, kb.ve)
            s.activation(sq[:, 0:w], xs_ap, AF.Square)
            kb.fence(v, kb.se)
            v.tensor_copy(hilo[:, 2 * w:3 * w], sq[:, 0:w])
            v.tensor_sub(hilo[:, 3 * w:4 * w], sq[:, 0:w], hilo[:, 2 * w:3 * w])
            kb.fence(t, kb.ve)
            t.matmul(psf[0][0:1, 0:w], onesb_t[:], hilo[:, 0:w],
                     start=(c == 0), stop=False)
            mi = t.matmul(psf[0][0:1, 0:w], onesb_t[:], hilo[:, w:2 * w],
                          start=False, stop=(c == 7))
            kb.pe.tick(mi)
            t.matmul(psf[1][0:1, 0:w], onesb_t[:], hilo[:, 2 * w:3 * w],
                     start=(c == 0), stop=False)
            mi = t.matmul(psf[1][0:1, 0:w], onesb_t[:], hilo[:, 3 * w:4 * w],
                          start=False, stop=(c == 7))
            kb.pe.tick(mi)
            kb.fence(v, kb.pe)
        v.tensor_copy(rows[0:1, c0:c0 + w], psf[0][0:1, 0:w])
        v.tensor_copy(rows[0:1, TOK + c0:TOK + c0 + w], psf[1][0:1, 0:w])
        bank_ready[0] = kb.ve.mark()
        bank_ready[1] = kb.ve.mark()
    v.tensor_scalar_mul(rows[0:1, 2 * TOK:3 * TOK], rows[0:1, 0:TOK],
                        -1.0 / D)
    v.tensor_mul(rows[0:1, 0:TOK], rows[0:1, 2 * TOK:3 * TOK],
                 rows[0:1, 2 * TOK:3 * TOK])
    v.scalar_tensor_tensor(rows[0:1, 3 * TOK:4 * TOK],
                           rows[0:1, TOK:2 * TOK], 1.0 / D,
                           rows[0:1, 0:TOK], ALU.mult, ALU.subtract)
    v.tensor_scalar_add(rows[0:1, 3 * TOK:4 * TOK],
                        rows[0:1, 3 * TOK:4 * TOK], 1e-5)
    kb.fence(s, kb.ve)
    s.activation(rows[0:1, 3 * TOK:4 * TOK], rows[0:1, 3 * TOK:4 * TOK], AF.Ln)
    kb.fence(v, kb.se)
    v.tensor_scalar_mul(rows[0:1, 3 * TOK:4 * TOK],
                        rows[0:1, 3 * TOK:4 * TOK], -0.5)
    kb.fence(s, kb.ve)
    s.activation(rows[0:1, 3 * TOK:4 * TOK], rows[0:1, 3 * TOK:4 * TOK],
                 AF.Exp)
    kb.fence(t, kb.se)
    for c in range(8):
        for j, (c0, w) in enumerate(t_tiles):
            bank_wait(2)
            mi = t.matmul(psf[2][:, 0:w], ones1_t[:],
                          rows[0:1, 2 * TOK + c0:2 * TOK + c0 + w],
                          start=True, stop=True)
            kb.pe.tick(mi)
            bank_wait(3)
            mi = t.matmul(psf[3][:, 0:w], ones1_t[:],
                          rows[0:1, 3 * TOK + c0:3 * TOK + c0 + w],
                          start=True, stop=True)
            kb.pe.tick(mi)
            kb.fence(v, kb.pe)
            v.tensor_add(stage[:, 0:w], x1i[:, c * TOK + c0:c * TOK + c0 + w],
                         psf[2][:, 0:w])
            bank_ready[2] = kb.ve.mark()
            v.tensor_mul(xn1b[:, c * TOK + c0:c * TOK + c0 + w],
                         stage[:, 0:w], psf[3][:, 0:w])
            bank_ready[3] = kb.ve.mark()
        kb.fence(nc.sync, kb.ve)
        kb.store(o_xn1[:, c * TOK:(c + 1) * TOK], xn1b[:, c * TOK:(c + 1) * TOK])

    v.memset(kmaxr[:], 0.0)
    for part, i3 in (("q", 0), ("k", 1), ("v", 2)):
        for oc in range(8):
            kb.fence(nc.gpsimd, kb.pe)
            kb.load(wbuf[:, 0:8 * 128].rearrange("p (k e) -> p k e", k=8),
                    w_qkv1.rearrange("p (k e) -> p k e", k=8)
                    [:, :, i3 * 1024 + oc * 128:i3 * 1024 + (oc + 1) * 128],
                    gp=True)
            kb.fence(t, kb.gq)
            for j, (c0, w) in enumerate(t_tiles):
                bk = j % 2
                bank_wait(bk)
                for k in range(8):
                    mi = t.matmul(psf[bk][:, 0:w],
                                  wbuf[:, k * 128:(k + 1) * 128],
                                  xn1b[:, k * TOK + c0:k * TOK + c0 + w],
                                  start=(k == 0), stop=(k == 7))
                kb.pe.tick(mi)
                kb.fence(s, kb.pe, kb.st, kb.ve)
                s.copy(bstage[:, (j % 2) * 512:(j % 2) * 512 + w],
                       psf[bk][:, 0:w])
                bank_ready[bk] = kb.se.mark()
                dst = {"q": o_q, "k": o_k, "v": None}[part]
                if part != "v":
                    kb.fence(nc.sync, kb.se)
                    kb.store(dst[:, oc * TOK + c0:oc * TOK + c0 + w],
                             bstage[:, (j % 2) * 512:(j % 2) * 512 + w])
                if part == "k":
                    kb.fence(v, kb.se, kb.pe)
                    v.tensor_mul(hilo[:, 0:w],
                                 bstage[:, (j % 2) * 512:(j % 2) * 512 + w],
                                 bstage[:, (j % 2) * 512:(j % 2) * 512 + w])
                    kb.fence(t, kb.ve)
                    bank_wait(4)
                    mi = t.matmul(psf[4][0:1, 0:w], onesb_t[0:64, :],
                                  hilo[0:64, 0:w], start=True, stop=True)
                    kb.pe.tick(mi)
                    bank_wait(5)
                    mi = t.matmul(psf[5][0:1, 0:w], onesb_t[64:128, :],
                                  hilo[64:128, 0:w], start=True, stop=True)
                    kb.pe.tick(mi)
                    kb.fence(v, kb.pe)
                    v.reduce_max(krow[0:1, 0:1], psf[4][0:1, 0:w], axis=AX.X)
                    bank_ready[4] = kb.ve.mark()
                    v.tensor_tensor(kmaxr[0:1, 2 * oc:2 * oc + 1],
                                    kmaxr[0:1, 2 * oc:2 * oc + 1],
                                    krow[0:1, 0:1], ALU.max)
                    v.reduce_max(krow[0:1, 0:1], psf[5][0:1, 0:w], axis=AX.X)
                    bank_ready[5] = kb.ve.mark()
                    v.tensor_tensor(kmaxr[0:1, 2 * oc + 1:2 * oc + 2],
                                    kmaxr[0:1, 2 * oc + 1:2 * oc + 2],
                                    krow[0:1, 0:1], ALU.max)
                if part == "v":
                    for jb in range(w // 128):
                        kb.fence(t, kb.se, kb.ve)
                        kb.pe.tick(t.transpose(
                            psb[:, 0:128],
                            bstage[:, (j % 2) * 512 + jb * 128:
                                   (j % 2) * 512 + (jb + 1) * 128]
                            .bitcast(BF16) if False else
                            bstage[:, (j % 2) * 512 + jb * 128:
                                   (j % 2) * 512 + (jb + 1) * 128],
                            idt[:]))
                        kb.fence(v, kb.pe, kb.st)
                        tt = (c0 + jb * 128) // 128
                        v.tensor_copy(hilo[:, 0:128], psb[:, 0:128])
                        kb.fence(nc.sync, kb.ve)
                        kb.store(o_vtm[:, tt * 1024 + oc * 128:
                                       tt * 1024 + (oc + 1) * 128],
                                 hilo[:, 0:128])
    kb.fence(s, kb.ve)
    s.activation(kmaxr[:], kmaxr[:], AF.Sqrt)
    kb.fence(nc.sync, kb.se)
    kb.store(o_kmax, kmaxr[:])
    return kb.finish()


def p8a_inmaps(inputs, xn0_full, sz_cores, y2_full, xB6_cores):
    x = inputs["L0_attn_x"]
    w_op = chunk_fm(np.ascontiguousarray(inputs["L0_attn_out_proj_w"].T))
    w_cfc = chunk_fm(np.ascontiguousarray(inputs["L0_attn_c_fc_w"].T))
    w_cpj = chunk_fm(np.ascontiguousarray(inputs["L0_attn_c_proj_w"].T))
    cst = attn_consts()
    maps = []
    for c in range(NCORE):
        b, ch = c // 4, c % 4
        g0 = ch * TOK
        maps.append(dict(
            xn0=slice_tok_cm(xn0_full, b, g0, 0, TOK),
            sz=chunk_fm(sz_cores[c]),
            y2=chunk_fm(np.ascontiguousarray(y2_full[b][:, g0:g0 + TOK])),
            xorig=slice_tok_cm(x, b, g0, 0, TOK),
            xB6=xB6_cores[c],
            w_op=w_op, w_cfc=w_cfc, w_cpj=w_cpj,
            ident=cst["ident"], onesbf=np.ones((128, 1), dtype=BF),
            ones1=cst["ones1"],
        ))
    return maps


def p8b_inmaps(inputs, x1_cores):
    w_qkv1 = chunk_fm(inputs["L1_attn_w_qkv"])
    cst = attn_consts()
    return [dict(x1in=x1_cores[c], w_qkv1=w_qkv1, ident=cst["ident"],
                 onesbf=np.ones((128, 1), dtype=BF), ones1=cst["ones1"])
            for c in range(NCORE)]



def build_p9():
    kb = KB()
    nc = kb.nc
    x1in = kb.inp("x1in", [128, 8 * TOK])
    xn1 = kb.inp("xn1", [128, 8 * TOK], BF16)
    qin = kb.inp("qin", [128, 8 * TOK], BF16)
    kfull = kb.inp("kfull", [16, 64 * T], BF16)
    vtmf = kb.inp("vtmf", [16, T * 64], BF16)
    kpart = kb.inp("kpart", [1, 64])
    w1 = kb.inp("w1", [128, 8 * 4096])
    w2 = kb.inp("w2", [128, 8 * 4096])
    h1 = kb.inp("h1", [128, 32 * 4096])
    h2 = kb.inp("h2", [128, 32 * 4096])
    w3 = kb.inp("w3", [128, 32 * 1024])
    s1v = kb.inp("s1v", [128, 32])
    s2v = kb.inp("s2v", [128, 32])
    h1s = kb.inp("h1s", [128, 32])
    h2s = kb.inp("h2s", [128, 32])
    w3s = kb.inp("w3s", [128, 8])
    onesbf = kb.inp("onesbf", [128, 1], BF16)
    ones1 = kb.inp("ones1", [1, 128])

    o_out = kb.out("o_out", [128, 8 * TOK])

    kb.open()
    v, s, g, t = kb.v, kb.s, kb.g, kb.t
    onesb_t = kb.sb([128, 1], BF16, "onesb")
    ones1_t = kb.sb([1, 128], F32, "ones1")
    xn1b = kb.sb([128, 8 * TOK], BF16, "xn1b")
    O_all = kb.sb([128, 8 * TOK], BF16, "oall")
    kh = kb.sb([65, T], BF16, "kh")
    vh = kb.sb([128, 64 * 65], BF16, "vh")
    qh = kb.sb([65, TOK], BF16, "qh")
    qsq = kb.sb([64, TOK], BF16, "qsq")
    Pt = kb.sb([128, 2 * 1024], BF16, "pt")
    srow = kb.sb([1, TOK], F32, "srow")
    negm = kb.sb([1, TOK], F32, "negm")
    khat = kb.sb([1, 16], F32, "khat")
    khn = kb.sb([1, 16], F32, "khn")
    kpt = kb.sb([1, 64], F32, "kpt")
    lrow = kb.sb([1, 512], F32, "lrow")
    lbv = kb.sb([64, 512], F32, "lbv")
    hq1 = kb.sb([128, 32 * 512], BF16, "hq1")
    hq2 = kb.sb([128, 32 * 512], BF16, "hq2")
    gl = kb.sb([128, 512], F32, "gl")
    wbuf = kb.sb([128, 2 * 32 * 128], BF16, "wbuf")
    sc1 = kb.sb([128, 32], F32, "sc1")
    sc2 = kb.sb([128, 32], F32, "sc2")
    sch1 = kb.sb([128, 32], F32, "sch1")
    sch2 = kb.sb([128, 32], F32, "sch2")
    scw3 = kb.sb([128, 8], F32, "scw3")
    stage = kb.sb([128, 512], F32, "stage")
    strm = kb.sb([128, 512], F32, "strm")

    psS = [kb.ps([128, 1024], F32, "psS") for _ in range(2)]
    psO = kb.ps([128, 512], F32, "psO")
    psX = [kb.ps([128, 512], F32, "psX") for _ in range(2)]
    bank_ready = {}

    def bank_wait(key):
        if key in bank_ready:
            c, val = bank_ready[key]
            t.wait_ge(c.sem, val)

    kb.load(onesb_t[:], onesbf)
    kb.load(ones1_t[:], ones1)
    kb.load(xn1b[:], xn1)
    kb.load(kpt[:], kpart)
    kb.load(sc1[:], s1v)
    kb.load(sc2[:], s2v)
    kb.load(sch1[:], h1s)
    kb.load(sch2[:], h2s)
    kb.load(scw3[:], w3s)
    t_tiles = tiles_of(TOK, 0)

    kb.fence(v, kb.ld)
    for h in range(16):
        v.reduce_max(khat[0:1, h:h + 1], kpt[0:1, h:64:16], axis=AX.X)
    v.tensor_scalar_mul(khn[:], khat[:], -1.0)
    v.memset(kh[64:65, :], 1.0)
    v.memset(vh[:, 0:64 * 65], 1.0)

    for h in range(16):
        hb, oc = 64 * (h % 2), h // 2
        kb.fence(nc.sync, kb.pe, kb.ve)
        kb.load(kh[0:64, 0:T], kfull[h:h + 1, :]
                .rearrange("h (d t) -> (h d) t", d=64))
        kb.load(vh[:, 0:64 * 65].rearrange("p (a d) -> p a d", d=65)
                [:, :, 0:64],
                vtmf[h:h + 1, :].rearrange("h (a p d) -> (h p) a d",
                                           p=128, d=64))
        kb.fence(nc.sync, kb.ve, kb.pe, kb.se)
        kb.load(qh[0:64, :], qin[hb:hb + 64, oc * TOK:(oc + 1) * TOK])
        kb.fence(v, kb.ld)
        v.tensor_scalar_mul(qh[0:64, :], qh[0:64, :], 1.0 / 32.0)
        kb.fence(s, kb.ve)
        s.activation(qsq[:], qh[0:64, :], AF.Square)
        kb.fence(t, kb.se)
        for j2, (c0, w) in enumerate(t_tiles):
            bank_wait(("x", j2 % 2))
            mi = t.matmul(psX[j2 % 2][0:1, 0:w], onesb_t[0:64, :],
                          qsq[0:64, c0:c0 + w], start=True, stop=True)
            kb.pe.tick(mi)
            kb.fence(v, kb.pe)
            v.tensor_copy(srow[0:1, c0:c0 + w], psX[j2 % 2][0:1, 0:w])
            bank_ready[("x", j2 % 2)] = kb.ve.mark()
        kb.fence(s, kb.ve)
        s.activation(srow[:], srow[:], AF.Sqrt)
        kb.fence(v, kb.se)
        v.tensor_scalar_mul(negm[:], srow[:], khn[0:1, h:h + 1])
        kb.fence(v, kb.se)
        v.tensor_copy(qh[64:65, :], negm[0:1, :])
        kb.fence(t, kb.ve, kb.ld)
        for qt in range(4):
            q0 = qt * 512
            bank_wait(("O",))

            def emit_S(kp):
                bk = kp % 2
                bank_wait(("S", bk))
                for half in range(2):
                    kt = 2 * kp + half
                    mi = t.matmul(psS[bk][:, half * 512:(half + 1) * 512],
                                  kh[0:65, 128 * kt:128 * (kt + 1)],
                                  qh[0:65, q0:q0 + 512],
                                  start=True, stop=True)
                kb.pe.tick(mi)
                bank_ready[("Smm", bk)] = kb.pe.mark()

            emit_S(0)
            for kp in range(32):
                bk = kp % 2
                c0m, v0m = bank_ready[("Smm", bk)]
                s._eng.wait_ge(c0m.sem, v0m)
                kb.se.tick(s._eng.activation(
                    Pt[:, bk * 1024:(bk + 1) * 1024], psS[bk][:], AF.Exp))
                bank_ready[("S", bk)] = kb.se.mark()
                exp_mark = kb.se.mark()
                if kp + 1 < 32:
                    emit_S(kp + 1)
                cm, vm = exp_mark
                t.wait_ge(cm.sem, vm)
                for half in range(2):
                    kt = 2 * kp + half
                    mi = t.matmul(psO[0:65, :],
                                  vh[:, 65 * kt:65 * kt + 65],
                                  Pt[:, bk * 1024 + half * 512:
                                     bk * 1024 + (half + 1) * 512],
                                  start=(kt == 0), stop=(kt == 63))
                kb.pe.tick(mi)
            kb.fence(v, kb.pe)
            v.tensor_copy(lrow[:], psO[64:65, :])
            kb.fence(s, kb.ve)
            s.activation(lrow[:], lrow[:], AF.Ln)
            s.activation(lrow[:], lrow[:], AF.Exp, scale=-1.0)
            kb.fence(t, kb.se)
            bank_wait(("x", 0))
            mi = t.matmul(psX[0][0:64, :], ones1_t[0:1, 0:64], lrow[:],
                          start=True, stop=True)
            kb.pe.tick(mi)
            kb.fence(v, kb.pe)
            v.tensor_copy(lbv[:], psX[0][0:64, :])
            bank_ready[("x", 0)] = kb.ve.mark()
            v.tensor_mul(O_all[hb:hb + 64, oc * TOK + q0:oc * TOK + q0 + 512],
                         psO[0:64, :], lbv[:])
            bank_ready[("O",)] = kb.ve.mark()

    WB = 32 * 128

    ldmk = [0, 0]

    def wload(wcm, K, oc, buf, cols=128):
        kb.load(wbuf[:, buf * WB:buf * WB + K * cols]
                .rearrange("p (k e) -> p k e", k=K),
                wcm.rearrange("p (k e) -> p k e", k=K)
                [:, :, oc * cols:(oc + 1) * cols], gp=True)
        ldmk[buf] = kb.gq.v

    wq_mark = [None, None]

    def wfence(buf):
        if wq_mark[buf] is not None:
            c, val = wq_mark[buf]
            nc.gpsimd.wait_ge(c.sem, val)

    for th in range(4):
        tc0 = th * 512
        SL = 8 * 128
        amark = {}

        def aload(wcm, oc, slot):
            if slot in amark:
                cc, vv = amark[slot]
                nc.gpsimd.wait_ge(cc.sem, vv)
            kb.load(wbuf[:, slot * SL:(slot + 1) * SL]
                    .rearrange("p (k e) -> p k e", k=8),
                    wcm.rearrange("p (k e) -> p k e", k=8)
                    [:, :, oc * 128:(oc + 1) * 128], gp=True)
            aldm[slot] = kb.gq.v

        aldm = [0, 0, 0, 0]
        aload(w1, 0, 0)
        aload(w2, 0, 2)
        for oc in range(32):
            b1s = oc % 2
            if oc + 1 < 32:
                aload(w1, oc + 1, 1 - b1s)
                aload(w2, oc + 1, 3 - b1s)
            t.wait_ge(kb.gq.sem, aldm[b1s])
            kb.fence(t, kb.ve)
            bank_wait(("x", 0))
            for k in range(8):
                mi = t.matmul(psX[0][:],
                              wbuf[:, b1s * SL + k * 128:
                                   b1s * SL + (k + 1) * 128],
                              xn1b[:, k * TOK + tc0:k * TOK + tc0 + 512],
                              start=(k == 0), stop=(k == 7))
            kb.pe.tick(mi)
            amark[b1s] = kb.pe.mark()
            kb.fence(s, kb.pe)
            s.activation(gl[:], psX[0][:], AF.Silu, scale=sc1[:, oc:oc + 1])
            bank_ready[("x", 0)] = kb.se.mark()
            t.wait_ge(kb.gq.sem, aldm[2 + b1s])
            bank_wait(("x", 1))
            for k in range(8):
                mi = t.matmul(psX[1][:],
                              wbuf[:, (2 + b1s) * SL + k * 128:
                                   (2 + b1s) * SL + (k + 1) * 128],
                              xn1b[:, k * TOK + tc0:k * TOK + tc0 + 512],
                              start=(k == 0), stop=(k == 7))
            kb.pe.tick(mi)
            amark[2 + b1s] = kb.pe.mark()
            kb.fence(v, kb.pe, kb.se)
            v.scalar_tensor_tensor(hq1[:, oc * 512:(oc + 1) * 512],
                                   psX[1][:], sc2[:, oc:oc + 1], gl[:],
                                   ALU.mult, ALU.mult)
            bank_ready[("x", 1)] = kb.ve.mark()
        wq_mark[0] = wq_mark[1] = kb.pe.mark()
        for (wcm, ssc, src_b, dst_b) in ((h1, sch1, hq1, hq2),
                                         (h2, sch2, hq2, hq1)):
            wfence(0)
            wload(wcm, 32, 0, 0)
            for oc in range(32):
                buf = oc % 2
                if oc < 31:
                    wfence(1 - buf)
                    wload(wcm, 32, oc + 1, 1 - buf)
                t.wait_ge(kb.gq.sem, ldmk[buf])
                bank_wait(("x", oc % 2))
                for k in range(32):
                    mi = t.matmul(psX[oc % 2][:],
                                  wbuf[:, buf * WB + k * 128:
                                       buf * WB + (k + 1) * 128],
                                  src_b[:, k * 512:(k + 1) * 512],
                                  start=(k == 0), stop=(k == 31))
                kb.pe.tick(mi)
                wq_mark[buf] = kb.pe.mark()
                kb.fence(s, kb.pe)
                s.activation(dst_b[:, oc * 512:(oc + 1) * 512],
                             psX[oc % 2][:],
                             AF.Gelu, scale=ssc[:, oc:oc + 1])
                bank_ready[("x", oc % 2)] = kb.se.mark()
        wfence(0)
        wload(w3, 32, 0, 0)
        for oc2 in range(8):
            buf = oc2 % 2
            if oc2 < 7:
                wfence(1 - buf)
                wload(w3, 32, oc2 + 1, 1 - buf)
            t.wait_ge(kb.gq.sem, ldmk[buf])
            kb.fence(t, kb.se)
            bank_wait(("x", 0))
            for k in range(32):
                mi = t.matmul(psX[0][:],
                              wbuf[:, buf * WB + k * 128:
                                   buf * WB + (k + 1) * 128],
                              hq1[:, k * 512:(k + 1) * 512],
                              start=(k == 0), stop=(k == 31))
            kb.pe.tick(mi)
            wq_mark[buf] = kb.pe.mark()
            kb.fence(nc.sync, kb.ve)
            kb.load(strm[:, 0:512], x1in[:, oc2 * TOK + tc0:
                                         oc2 * TOK + tc0 + 512])
            kb.fence(v, kb.pe, kb.ld, kb.st)
            v.scalar_tensor_tensor(stage[:, 0:512], psX[0][:],
                                   scw3[:, oc2:oc2 + 1], strm[:, 0:512],
                                   ALU.mult, ALU.add)
            bank_ready[("x", 0)] = kb.ve.mark()
            v.tensor_add(stage[:, 0:512], stage[:, 0:512],
                         O_all[:, oc2 * TOK + tc0:oc2 * TOK + tc0 + 512])
            kb.fence(nc.sync, kb.ve)
            kb.store(o_out[:, oc2 * TOK + tc0:oc2 * TOK + tc0 + 512],
                     stage[:, 0:512])
    return kb.finish()


def p9_inmaps(inputs, x1_cores, xn1_cores, q_cores, k_cores, vtm_cores,
              kmax_cores):
    w1 = chunk_fm(np.ascontiguousarray(inputs["L1_ffn_w1_w"].T))
    w2 = chunk_fm(np.ascontiguousarray(inputs["L1_ffn_w2_w"].T))
    h1 = chunk_fm(np.ascontiguousarray(inputs["L1_ffn_h1_w"].T))
    h2 = chunk_fm(np.ascontiguousarray(inputs["L1_ffn_h2_w"].T))
    w3 = chunk_fm(np.ascontiguousarray(inputs["L1_ffn_w3_w"].T))
    s1v = chunk_fm(inputs["L1_ffn_w1_s"][:, None])
    s2v = chunk_fm(inputs["L1_ffn_w2_s"][:, None])
    h1s = chunk_fm(inputs["L1_ffn_h1_s"][:, None])
    h2s = chunk_fm(inputs["L1_ffn_h2_s"][:, None])
    w3s = chunk_fm(inputs["L1_ffn_w3_s"][:, None])
    cst = attn_consts()
    maps = []
    for bb in range(2):
        Kfm = np.concatenate(
            [from_cm_np(k_cores[4 * bb + j], TOK) for j in range(4)], axis=1)
        kf = np.ascontiguousarray(
            Kfm.reshape(16, 64, T)).reshape(16, 64 * T)
        Vtm = np.concatenate(
            [vtm_cores[4 * bb + j].reshape(128, 16, 1024)
             .transpose(1, 0, 2).reshape(TOK, 1024)
             for j in range(4)], axis=0)
        vtmf = np.ascontiguousarray(
            Vtm.reshape(T, 16, 64).transpose(1, 0, 2)).reshape(16, T * 64)
        kpart = np.zeros((1, 64), np.float32)
        for j in range(4):
            kpart[0, 16 * j:16 * (j + 1)] = kmax_cores[4 * bb + j][0]
        for j in range(4):
            c = 4 * bb + j
            maps.append(dict(
                x1in=x1_cores[c], xn1=xn1_cores[c], qin=q_cores[c],
                kfull=kf, vtmf=vtmf, kpart=kpart,
                w1=w1, w2=w2, h1=h1, h2=h2, w3=w3,
                s1v=s1v, s2v=s2v, h1s=h1s, h2s=h2s, w3s=w3s,
                onesbf=np.ones((128, 1), dtype=BF), ones1=cst["ones1"],
            ))
    assert len(maps) == NCORE
    return maps


def from_cm_np(a, cols):
    return np.ascontiguousarray(
        a.reshape(128, 8, cols).transpose(1, 0, 2).reshape(1024, cols))



_PROGS = {}


def _prog(name, builder):
    if name not in _PROGS:
        _PROGS[name] = builder()
    return _PROGS[name]


def _run(name, builder, maps):
    nc = _prog(name, builder)
    res = run_bass_kernel_spmd(nc, maps, core_ids=list(range(NCORE)))
    return res.results


def kernel(**inputs):
    inputs = {k: np.asarray(v) for k, v in inputs.items()}
    r1 = _run("p1", build_p1, p1_inmaps(inputs))
    xn0_full = np.stack(
        [np.concatenate([r1[4 * b + j]["o_xn0"] for j in range(4)], 0)
         for b in range(2)])
    r2 = _run("p2", build_p2, p2_inmaps(inputs, r1))
    y2_full = [np.concatenate([r2[4 * b + j]["o_y2"] for j in range(4)], 0)
               for b in range(2)]
    xB_full = xn0_full
    for layer in range(6):
        ra = _run("attn", build_attn, attn_inmaps(inputs, layer, xB_full))
        xA_cms = [ra[c]["o_xA"] for c in range(NCORE)]
        rf = _run("ffn", build_ffn, ffn_inmaps(inputs, layer, xA_cms))
        xB_full = np.stack(
            [np.concatenate(
                [from_cm_np(rf[4 * b + j]["o_xB"], TOK).T for j in range(4)],
                0) for b in range(2)])
    xB6_cores = [chunk_fm(np.ascontiguousarray(
        xB_full[c // 4, (c % 4) * TOK:(c % 4 + 1) * TOK].T))
        for c in range(NCORE)]
    r8a = _run("p8a", build_p8a,
               p8a_inmaps(inputs, xn0_full,
                          [r1[c]["o_sz"] for c in range(NCORE)],
                          y2_full, xB6_cores))
    x1_cores = [r8a[c]["o_x1"] for c in range(NCORE)]
    r8b = _run("p8b", build_p8b, p8b_inmaps(inputs, x1_cores))
    r9 = _run("p9", build_p9,
              p9_inmaps(inputs, x1_cores,
                        [r8b[c]["o_xn1"] for c in range(NCORE)],
                        [r8b[c]["o_q"] for c in range(NCORE)],
                        [r8b[c]["o_k"] for c in range(NCORE)],
                        [r8b[c]["o_vtm"] for c in range(NCORE)],
                        [r8b[c]["o_kmax"] for c in range(NCORE)]))
    out = np.empty((2, T, D), np.float32)
    for c in range(NCORE):
        b, ch = c // 4, c % 4
        out[b, ch * TOK:(ch + 1) * TOK] = from_cm_np(r9[c]["o_out"], TOK).T
    return out


# version 04 (final)
# speedup vs baseline: 1.7101x; 1.0357x over previous
import math
from contextlib import ExitStack

import numpy as np
import ml_dtypes
import concourse.bass as bass
import concourse.mybir as mybir
from concourse.bass_utils import run_bass_kernel_spmd

F32 = mybir.dt.float32
BF16 = mybir.dt.bfloat16
AF = mybir.ActivationFunctionType
ALU = mybir.AluOpType
AX = mybir.AxisListType

NCORE = 8
B = 2
T = 8192
TOK = 2048
D = 1024
EI = 2048
DST = 16
DTR = 64
HALO = 514
F32EPS = float(np.finfo(np.float32).eps)
BF = ml_dtypes.bfloat16


class Ctr:

    def __init__(self, sem):
        self.sem = sem
        self.v = 0

    def tick(self, instr, n=1):
        instr.then_inc(self.sem, n)
        self.v += n
        return instr

    def wait(self, eng):
        p = getattr(self, "_proxy", None)
        if p is not None:
            p.flush()
        eng.wait_ge(self.sem, self.v)

    def mark(self):
        p = getattr(self, "_proxy", None)
        if p is not None:
            p.flush()
        return (self, self.v)


_OPS = {
    "tensor_copy", "tensor_tensor", "tensor_add", "tensor_sub", "tensor_mul",
    "tensor_max", "tensor_scalar", "tensor_single_scalar", "tensor_scalar_add",
    "tensor_scalar_sub", "tensor_scalar_mul", "tensor_scalar_max",
    "tensor_scalar_min", "tensor_relu", "tensor_reduce", "reduce_max",
    "reduce_sum", "scalar_tensor_tensor", "tensor_tensor_scan", "memset",
    "memzero", "reciprocal", "select", "transpose", "bn_stats", "bn_aggr",
    "iota", "activation", "square", "sqrt", "mul", "copy", "add", "sign",
    "engine_nop", "stream_shuffle", "pool", "pool_avg", "pool_max",
    "copy_predicated",
}


class EngProxy:

    SMALL = 512

    def __init__(self, eng, ctr):
        self._eng = eng
        self._ctr = ctr
        self._last = None
        self._need_wait = False

    def flush(self):
        if self._last is not None:
            self._ctr.tick(self._last)
            self._last = None

    def mark(self):
        self.flush()
        return self._ctr.mark()

    def __getattr__(self, name):
        attr = getattr(self._eng, name)
        if name not in _OPS:
            return attr

        def wrap(*a, **k):
            if self._need_wait:
                self._eng.wait_ge(self._ctr.sem, self._ctr.v)
                self._need_wait = False
            instr = attr(*a, **k)
            try:
                fs = a[0].free_size()
            except Exception:
                fs = 0
            if fs < self.SMALL:
                self.flush()
                self._ctr.tick(instr)
                self._need_wait = True
            else:
                if self._last is not None:
                    self._ctr.tick(self._last)
                self._last = instr
            return instr

        return wrap

    def wait_ge(self, sem, v):
        return self._eng.wait_ge(sem, v)

    def wait(self, mark):
        ctr, v = mark if isinstance(mark, tuple) else (mark, mark.v)
        return self._eng.wait_ge(ctr.sem, v)


class KB:

    def __init__(self):
        self.nc = bass.Bass("TRN2", target_bir_lowering=False,
                            detect_race_conditions=False)
        self.ctx = ExitStack()
        self._n = 0

    def name(self, p):
        self._n += 1
        return f"{p}{self._n}"

    def inp(self, name, shape, dt=F32):
        return self.nc.dram_tensor(name, shape, dt, kind="ExternalInput").ap()

    def out(self, name, shape, dt=F32):
        return self.nc.dram_tensor(name, shape, dt, kind="ExternalOutput").ap()

    def sb(self, shape, dt=F32, tag="sb"):
        return self.ctx.enter_context(
            self.nc.sbuf_tensor(self.name(tag), shape, dt))

    def ps(self, shape, dt=F32, tag="ps"):
        return self.ctx.enter_context(
            self.nc.psum_tensor(self.name(tag), shape, dt))

    def ctr(self, tag="sem"):
        return Ctr(self.ctx.enter_context(self.nc.semaphore(self.name(tag))))

    def open(self):
        self.blk = self.ctx.enter_context(self.nc.Block())
        self.ld = self.ctr("ld")
        self.gq = self.ctr("gq")
        self.st = self.ctr("st")
        self.pe = self.ctr("pe")
        self.ve = self.ctr("ve")
        self.se = self.ctr("se")
        self.ge = self.ctr("ge")
        self.v = EngProxy(self.nc.vector, self.ve)
        self.s = EngProxy(self.nc.scalar, self.se)
        self.g = EngProxy(self.nc.gpsimd, self.ge)
        self.t = self.nc.tensor
        self._proxy_of = {id(self.ve): self.v, id(self.se): self.s,
                          id(self.ge): self.g}
        self.ve._proxy = self.v
        self.se._proxy = self.s
        self.ge._proxy = self.g
        return self

    def finish(self):
        self.nc.sync.wait_ge(self.st.sem, self.st.v)
        self.ctx.close()
        return self.nc

    def load(self, dst_ap, src_ap, gp=False, slow=False):
        q = self.nc.gpsimd if gp else self.nc.sync
        ctr = self.gq if gp else self.ld
        kw = dict(allow_slow_non_contiguous=True) if slow else {}
        ctr.tick(q.dma_start(out=dst_ap, in_=src_ap, **kw), 16)

    def store(self, dst_ap, src_ap, slow=False):
        kw = dict(allow_slow_non_contiguous=True) if slow else {}
        self.st.tick(self.nc.sync.dma_start(out=dst_ap, in_=src_ap, **kw), 16)

    def fence(self, consumer, *ctrs):
        eng = consumer._eng if isinstance(consumer, EngProxy) else consumer
        for c in ctrs:
            p = self._proxy_of.get(id(c))
            if p is not None:
                p.flush()
            eng.wait_ge(c.sem, c.v)



def build_p1():
    kb = KB()
    nc = kb.nc
    xs = kb.inp("xs", [3 + TOK, D])
    w_in0 = kb.inp("w_in0", [128, 8 * EI])
    w_in1 = kb.inp("w_in1", [128, 8 * EI])
    conv_w = kb.inp("conv_w", [128, 16 * 4])
    conv_b = kb.inp("conv_b", [128, 16])
    w_dtl = kb.inp("w_dtl", [128, 16 * DTR])
    w_Bp = kb.inp("w_Bp", [128, 16 * DST])
    w_Cp = kb.inp("w_Cp", [128, 16 * DST])
    w_dtp = kb.inp("w_dtp", [DTR, EI])
    dtp_b = kb.inp("dtp_b", [128, 16])
    ident = kb.inp("ident", [128, 128], BF16)

    o_xn0 = kb.out("o_xn0", [TOK, D])
    o_sz = kb.out("o_sz", [EI, TOK])
    o_xconv = kb.out("o_xconv", [EI, TOK])
    o_dt = kb.out("o_dt", [EI, TOK])
    o_B = kb.out("o_B", [DST, TOK])
    o_C = kb.out("o_C", [DST, TOK])

    NT = TOK // 128
    T2 = 3 + TOK
    kb.open()
    v, s, t = kb.v, kb.s, kb.t

    idt = kb.sb([128, 128], BF16, "idt")
    rn_fm = kb.sb([128, 8 * T2], BF16, "rnfm")
    wh = kb.sb([128, 8 * EI], BF16, "wh")
    xcb = kb.sb([128, 16 * TOK], BF16, "xcb")
    cw = kb.sb([128, 16 * 4], F32, "cw")
    cb = kb.sb([128, 16], F32, "cb")
    dtpb = kb.sb([128, 16], F32, "dtpb")
    wb_dtl = kb.sb([128, 16 * DTR], BF16, "wdtl")
    wb_B = kb.sb([128, 16 * DST], BF16, "wB")
    wb_C = kb.sb([128, 16 * DST], BF16, "wC")
    wb_dtp = kb.sb([DTR, EI], BF16, "wdtp")
    dtl_b = kb.sb([DTR, TOK], BF16, "dtl")

    kb.load(idt[:], ident)
    kb.load(cw[:], conv_w)
    kb.load(cb[:], conv_b)
    kb.load(dtpb[:], dtp_b)
    kb.load(wb_dtl[:], w_dtl, gp=True)
    kb.load(wb_B[:], w_Bp, gp=True)
    kb.load(wb_C[:], w_Cp, gp=True)
    kb.load(wb_dtp[:], w_dtp, gp=True)

    x_tm = kb.sb([128, D], F32, "xtm")
    sq = kb.sb([128, D], F32, "sq")
    s1 = kb.sb([128, 1], F32, "s1")
    s2 = kb.sb([128, 1], F32, "s2")
    var = kb.sb([128, 1], F32, "var")
    rstd = kb.sb([128, 1], F32, "rstd")
    rrms = kb.sb([128, 1], F32, "rrms")
    xn_tm = kb.sb([128, D], F32, "xntm")
    rn_tm = kb.sb([128, D], BF16, "rntm")
    ps_tr = kb.ps([128, 128], BF16, "pstr")

    def ln_rms_tile(rows, x_ap, col0):
        kb.fence(v, kb.ld, kb.st)
        kb.fence(s, kb.ld, kb.st)
        v.reduce_sum(s1[0:rows], x_ap, axis=AX.X)
        s.activation(sq[0:rows], x_ap, AF.Square)
        kb.fence(v, kb.se)
        v.reduce_sum(s2[0:rows], sq[0:rows], axis=AX.X)
        v.tensor_scalar_mul(s1[0:rows], s1[0:rows], -1.0 / D)
        v.tensor_mul(var[0:rows], s1[0:rows], s1[0:rows])
        v.scalar_tensor_tensor(var[0:rows], s2[0:rows], 1.0 / D, var[0:rows],
                               ALU.mult, ALU.subtract)
        v.tensor_scalar_add(var[0:rows], var[0:rows], 1e-5)
        kb.fence(s, kb.ve)
        s.activation(rstd[0:rows], var[0:rows], AF.Sqrt)
        kb.fence(v, kb.se)
        v.reciprocal(rstd[0:rows], rstd[0:rows])
        v.tensor_scalar(xn_tm[0:rows], x_ap, s1[0:rows], rstd[0:rows],
                        ALU.add, ALU.mult)
        v.tensor_scalar_add(rrms[0:rows], var[0:rows], -1e-5)
        v.tensor_mul(rrms[0:rows], rrms[0:rows], rstd[0:rows])
        v.tensor_mul(rrms[0:rows], rrms[0:rows], rstd[0:rows])
        v.tensor_scalar_add(rrms[0:rows], rrms[0:rows], F32EPS)
        kb.fence(s, kb.ve)
        s.activation(rrms[0:rows], rrms[0:rows], AF.Ln)
        kb.fence(v, kb.se)
        v.tensor_scalar_mul(rrms[0:rows], rrms[0:rows], -0.5)
        kb.fence(s, kb.ve)
        s.activation(rrms[0:rows], rrms[0:rows], AF.Exp)
        kb.fence(v, kb.se)
        v.tensor_scalar_mul(rn_tm[0:rows], xn_tm[0:rows], rrms[0:rows])
        kb.fence(t, kb.ve)
        for c in range(8):
            kb.pe.tick(t.transpose(ps_tr[0:128, 0:rows],
                                   rn_tm[0:rows, 128 * c:128 * (c + 1)],
                                   idt[0:rows, 0:rows]))
            kb.fence(v, kb.pe)
            v.tensor_copy(rn_fm[:, c * T2 + col0: c * T2 + col0 + rows],
                          ps_tr[0:128, 0:rows])
            kb.fence(t, kb.ve)

    xh = kb.sb([3, D], F32, "xh")
    kb.load(xh[:], xs[0:3, :])
    ln_rms_tile(3, xh[:], 0)
    for i in range(NT):
        kb.fence(nc.sync, kb.ve, kb.se)
        kb.load(x_tm[:], xs[3 + 128 * i: 3 + 128 * (i + 1), :])
        ln_rms_tile(128, x_tm[:], 3 + 128 * i)
        kb.fence(nc.sync, kb.ve)
        kb.store(o_xn0[128 * i:128 * (i + 1), :], xn_tm[0:128, :])

    ps_mm = [kb.ps([128, 512], F32, "pmm") for _ in range(6)]
    bank_ready = [None] * 6
    xin_c = kb.sb([128, T2], F32, "xinc")
    conv_t = kb.sb([128, TOK], F32, "convt")
    z_t = conv_t

    def tiles_for(total, off):
        out, p = [], off
        while p < off + total:
            w = min(512, off + total - p)
            out.append((p, w))
            p += w
        return out

    kb.load(wh[:], w_in0, gp=True)

    def bank_wait(bk):
        if bank_ready[bk] is not None:
            c, val = bank_ready[bk]
            t.wait_ge(c.sem, val)

    def in_proj_chunk(o, tiles, dst, dst_off, evac_eng):
        for j, (c0, w) in enumerate(tiles):
            bk = (o * len(tiles) + j) % 4
            bank_wait(bk)
            for k in range(8):
                mi = t.matmul(ps_mm[bk][:, 0:w],
                              wh[:, k * EI + 128 * o: k * EI + 128 * (o + 1)],
                              rn_fm[:, k * T2 + c0: k * T2 + c0 + w],
                              start=(k == 0), stop=(k == 7))
            kb.pe.tick(mi)
            kb.fence(evac_eng, kb.pe)
            evac_eng.tensor_copy(dst[:, c0 + dst_off:c0 + dst_off + w],
                                 ps_mm[bk][:, 0:w])
            bank_ready[bk] = evac_eng._ctr.mark()

    xin_tiles = tiles_for(T2, 0)
    kb.fence(t, kb.ve, kb.gq)
    for o in range(16):
        in_proj_chunk(o, xin_tiles, xin_c, 0, v)
        cw_o = [cw[:, o * 4 + k: o * 4 + k + 1] for k in range(4)]
        v.tensor_scalar(conv_t[:], xin_c[:, 0:TOK], cw_o[0], cb[:, o:o + 1],
                        ALU.mult, ALU.add)
        for k in (1, 2, 3):
            v.scalar_tensor_tensor(conv_t[:], xin_c[:, k:k + TOK],
                                   cw_o[k], conv_t[:], ALU.mult, ALU.add)
        kb.fence(s, kb.ve, kb.st)
        s.activation(xin_c[:, 0:TOK], conv_t[:], AF.Silu)
        s.activation(xcb[:, o * TOK:(o + 1) * TOK], conv_t[:], AF.Silu)
        kb.fence(nc.sync, kb.se)
        kb.store(o_xconv[128 * o:128 * (o + 1), :], xin_c[:, 0:TOK])
        kb.fence(v, kb.st)

    kb.fence(nc.gpsimd, kb.pe)
    kb.load(wh[:], w_in1, gp=True)
    z_tiles = tiles_for(TOK, 3)
    kb.fence(t, kb.gq)
    for o in range(16):
        in_proj_chunk(o, z_tiles, z_t, -3, v)
        kb.fence(s, kb.ve)
        s.activation(z_t[:], z_t[:], AF.Silu)
        kb.fence(nc.sync, kb.se)
        kb.store(o_sz[128 * o:128 * (o + 1), :], z_t[:])
        kb.fence(v, kb.st)
        kb.fence(s, kb.st)

    bc_stage = kb.sb([DST, 2 * 512], F32, "bcst")
    kb.fence(t, kb.se, kb.gq)
    for j, (c0, w) in enumerate(tiles_for(TOK, 0)):
        for (dst_out, wmat, rows) in ((None, wb_dtl, DTR), (o_B, wb_B, DST),
                                      (o_C, wb_C, DST)):
            bk = {DTR: j % 2, DST: 4 + j % 2}[rows]
            bank_wait(bk)
            cols = {DTR: DTR, DST: DST}[rows]
            for k in range(16):
                mi = t.matmul(ps_mm[bk][0:rows, 0:w],
                              wmat[:, k * cols:(k + 1) * cols],
                              xcb[:, k * TOK + c0: k * TOK + c0 + w],
                              start=(k == 0), stop=(k == 15))
            kb.pe.tick(mi)
            kb.fence(v, kb.pe)
            if dst_out is None:
                v.tensor_copy(dtl_b[0:rows, c0:c0 + w], ps_mm[bk][0:rows, 0:w])
            else:
                slot = 0 if dst_out is o_B else 1
                kb.fence(v, kb.st)
                v.tensor_copy(bc_stage[0:DST, 512 * slot:512 * slot + w],
                              ps_mm[bk][0:rows, 0:w])
                kb.fence(nc.sync, kb.ve)
                kb.store(dst_out[:, c0:c0 + w],
                         bc_stage[0:DST, 512 * slot:512 * slot + w])
            bank_ready[bk] = kb.ve.mark()

    t_relu = kb.sb([128, 512], F32, "trelu")
    t_sp = kb.sb([128, 512], F32, "tsp")
    dt_stage = kb.sb([128, 2 * 512], F32, "dtstage")
    kb.fence(t, kb.ve)
    for o in range(16):
        for j, (c0, w) in enumerate(tiles_for(TOK, 0)):
            bk = 2 + (o * 4 + j) % 2
            slot = (o * 4 + j) % 2
            bank_wait(bk)
            mi = t.matmul(ps_mm[bk][:, 0:w], wb_dtp[:, 128 * o:128 * (o + 1)],
                          dtl_b[:, c0:c0 + w], start=True, stop=True)
            kb.pe.tick(mi)
            kb.fence(v, kb.pe)
            b_o = dtpb[:, o:o + 1]
            v.tensor_scalar(t_relu[:, 0:w], ps_mm[bk][:, 0:w], b_o, 0.0,
                            ALU.add, ALU.max)
            kb.fence(s, kb.ve)
            s.activation(t_sp[:, 0:w], ps_mm[bk][:, 0:w], AF.Abs, bias=b_o)
            bank_ready[bk] = kb.se.mark()
            s.activation(t_sp[:, 0:w], t_sp[:, 0:w], AF.Exp, scale=-1.0)
            s.activation(t_sp[:, 0:w], t_sp[:, 0:w], AF.Ln, bias=1.0)
            kb.fence(v, kb.se, kb.st)
            v.tensor_add(dt_stage[:, 512 * slot:512 * slot + w],
                         t_relu[:, 0:w], t_sp[:, 0:w])
            kb.fence(nc.sync, kb.ve)
            kb.store(o_dt[128 * o:128 * (o + 1), c0:c0 + w],
                     dt_stage[:, 512 * slot:512 * slot + w])
    return kb.finish()


def chunk_fm(a, p=128):
    K, c = a.shape
    return np.ascontiguousarray(
        a.reshape(K // p, p, c).transpose(1, 0, 2).reshape(p, -1))


_IDENT = np.eye(128).astype(BF)


def p1_inmaps(inputs):
    x = inputs["L0_attn_x"]
    w_inT = inputs["L0_attn_in_proj_w"].T
    w_in0 = chunk_fm(np.ascontiguousarray(w_inT[:, 0:EI]))
    w_in1 = chunk_fm(np.ascontiguousarray(w_inT[:, EI:2 * EI]))
    conv_w = chunk_fm(inputs["L0_attn_conv_w"])
    conv_b = chunk_fm(inputs["L0_attn_conv_b"][:, None])
    xp = inputs["L0_attn_x_proj_w"].T
    w_dtl = chunk_fm(np.ascontiguousarray(xp[:, 0:DTR]))
    w_Bp = chunk_fm(np.ascontiguousarray(xp[:, DTR:DTR + DST]))
    w_Cp = chunk_fm(np.ascontiguousarray(xp[:, DTR + DST:DTR + 2 * DST]))
    w_dtp = np.ascontiguousarray(inputs["L0_attn_dt_proj_w"].T)
    dtp_b = chunk_fm(inputs["L0_attn_dt_proj_b"][:, None])
    shared = dict(w_in0=w_in0, w_in1=w_in1, conv_w=conv_w, conv_b=conv_b,
                  w_dtl=w_dtl, w_Bp=w_Bp, w_Cp=w_Cp, w_dtp=w_dtp,
                  dtp_b=dtp_b, ident=_IDENT)
    maps = []
    for c in range(NCORE):
        b, ch = c // 4, c % 4
        g0 = ch * TOK
        xs = np.zeros((3 + TOK, D), np.float32)
        xs[3:] = x[b, g0:g0 + TOK]
        if g0 >= 3:
            xs[0:3] = x[b, g0 - 3:g0]
        maps.append(dict(xs=np.ascontiguousarray(xs), **shared))
    return maps


def kernel(**inputs):
    raise NotImplementedError("under construction")


def build_p2():
    kb = KB()
    nc = kb.nc
    dts = kb.inp("dts", [512, T])
    xcs = kb.inp("xcs", [512, T])
    A_log_s = kb.inp("A_log_s", [512, DST])
    D_s = kb.inp("D_s", [128, 4])
    Bsf = kb.inp("Bsf", [DST, T])
    Csf = kb.inp("Csf", [DST, T])
    o_y2 = kb.out("o_y2", [512, T])

    TC = 512
    NTC = T // TC
    kb.open()
    v, s, g, t = kb.v, kb.s, kb.g, kb.t

    A_f = kb.sb([128, 4 * DST], F32, "Af")
    Dv = kb.sb([128, 4], F32, "Dv")
    carries = kb.sb([128, 4 * DST], F32, "carry")
    dA = kb.sb([128, DST * TC], F32, "dA")
    wB = kb.sb([128, DST * TC], F32, "wB")
    hh = kb.sb([128, DST * TC], F32, "hh")
    Bb = kb.sb([128, DST * TC], F32, "Bb")
    Cb = kb.sb([128, DST * TC], F32, "Cb")
    dt_t = kb.sb([128, 2 * TC], F32, "dtt")
    xc_t = kb.sb([128, 2 * TC], F32, "xct")
    w_t = kb.sb([128, TC], F32, "wt")
    y_st = kb.sb([128, 2 * TC], F32, "yst")

    kb.load(Dv[:], D_s)
    al = kb.sb([128, 4 * DST], F32, "al")
    for ec in range(4):
        kb.load(al[:, ec * DST:(ec + 1) * DST],
                A_log_s[ec * 128:(ec + 1) * 128, :])
    kb.fence(s, kb.ld)
    s.activation(A_f[:], al[:], AF.Exp)
    kb.fence(v, kb.se)
    v.tensor_scalar_mul(A_f[:], A_f[:], -1.0)
    v.memset(carries[:], 0.0)

    for tc in range(NTC):
        kb.fence(nc.sync, kb.ge)
        for ss in range(DST):
            kb.load(Bb[:, ss * TC:(ss + 1) * TC],
                    Bsf[ss:ss + 1, tc * TC:(tc + 1) * TC]
                    .partition_broadcast(128).squeeze(1))
            kb.load(Cb[:, ss * TC:(ss + 1) * TC],
                    Csf[ss:ss + 1, tc * TC:(tc + 1) * TC]
                    .partition_broadcast(128).squeeze(1))
        for ec in range(4):
            buf = ec % 2
            kb.fence(nc.sync, kb.se, kb.ge, kb.ve)
            kb.load(dt_t[:, buf * TC:(buf + 1) * TC],
                    dts[ec * 128:(ec + 1) * 128, tc * TC:(tc + 1) * TC])
            kb.load(xc_t[:, buf * TC:(buf + 1) * TC],
                    xcs[ec * 128:(ec + 1) * 128, tc * TC:(tc + 1) * TC])
            dtb = dt_t[:, buf * TC:(buf + 1) * TC]
            xcb_ = xc_t[:, buf * TC:(buf + 1) * TC]
            kb.fence(s, kb.ld, kb.ve, kb.ge)
            for ss in range(DST):
                s.activation(dA[:, ss * TC:(ss + 1) * TC], dtb, AF.Exp,
                             scale=A_f[:, ec * DST + ss:ec * DST + ss + 1])
            kb.fence(g, kb.ld, kb.ve)
            g.tensor_mul(w_t[:], dtb, xcb_)
            for ss in range(DST):
                g.tensor_mul(wB[:, ss * TC:(ss + 1) * TC], w_t[:],
                             Bb[:, ss * TC:(ss + 1) * TC])
            kb.fence(v, kb.se, kb.ge)
            for ss in range(DST):
                v.tensor_tensor_scan(
                    hh[:, ss * TC:(ss + 1) * TC],
                    dA[:, ss * TC:(ss + 1) * TC],
                    wB[:, ss * TC:(ss + 1) * TC],
                    carries[:, ec * DST + ss:ec * DST + ss + 1],
                    ALU.mult, ALU.add)
            v.tensor_copy(carries[:, ec * DST:(ec + 1) * DST],
                          hh[:, TC - 1::TC])
            kb.fence(g, kb.ve)
            for ss in range(DST):
                g.tensor_mul(dA[:, ss * TC:(ss + 1) * TC],
                             hh[:, ss * TC:(ss + 1) * TC],
                             Cb[:, ss * TC:(ss + 1) * TC])
            kb.fence(v, kb.ge)
            half = DST // 2
            while half >= 1:
                v.tensor_add(dA[:, 0:half * TC], dA[:, 0:half * TC],
                             dA[:, half * TC:2 * half * TC])
                half //= 2
            kb.fence(v, kb.st)
            v.scalar_tensor_tensor(y_st[:, buf * TC:(buf + 1) * TC],
                                   xcb_, Dv[:, ec:ec + 1], dA[:, 0:TC],
                                   ALU.mult, ALU.add)
            kb.fence(nc.sync, kb.ve)
            kb.store(o_y2[ec * 128:(ec + 1) * 128, tc * TC:(tc + 1) * TC],
                     y_st[:, buf * TC:(buf + 1) * TC])
    return kb.finish()


def p2_inmaps(inputs, p1_res):
    A_log = inputs["L0_attn_A_log"]
    Dvec = inputs["L0_attn_D"]
    maps = []
    for c in range(NCORE):
        b, sl = c // 4, c % 4
        dts = np.concatenate([p1_res[4 * b + j]["o_dt"] for j in range(4)], 1)
        xcs = np.concatenate([p1_res[4 * b + j]["o_xconv"] for j in range(4)], 1)
        Bsf = np.concatenate([p1_res[4 * b + j]["o_B"] for j in range(4)], 1)
        Csf = np.concatenate([p1_res[4 * b + j]["o_C"] for j in range(4)], 1)
        rows = slice(sl * 512, (sl + 1) * 512)
        maps.append(dict(
            dts=np.ascontiguousarray(dts[rows]),
            xcs=np.ascontiguousarray(xcs[rows]),
            A_log_s=np.ascontiguousarray(A_log[rows]),
            D_s=chunk_fm(Dvec[rows][:, None]),
            Bsf=np.ascontiguousarray(Bsf),
            Csf=np.ascontiguousarray(Csf),
        ))
    return maps


TH = HALO + TOK
OW = TOK + 1


def tiles_of(total, off=0, wmax=512):
    out, p = [], off
    while p < off + total:
        w = min(wmax, off + total - p)
        out.append((p, w))
        p += w
    return out


def build_attn(debug=False):
    kb = KB()
    nc = kb.nc
    xB = kb.inp("xB", [128, 8 * TH])
    wqkv = kb.inp("wqkv", [128, 8 * 3072])
    wo = kb.inp("wo", [128, 8 * 1024])
    qsv = kb.inp("qsv", [128, 1])
    ksv = kb.inp("ksv", [128, 1])
    csq8 = kb.inp("csq8", [128, TH])
    ssq8 = kb.inp("ssq8", [128, TH])
    csk = kb.inp("csk", [128, TH])
    ssk = kb.inp("ssk", [128, TH])
    maskg = kb.inp("maskg", [128, 4 * 256], BF16)
    mask0 = kb.inp("mask0", [128, 4 * 256], BF16)
    ident = kb.inp("ident", [128, 128], BF16)
    sel33b = kb.inp("sel33b", [128, 33], BF16)
    selB33 = kb.inp("selB33", [33, 128])
    ones1 = kb.inp("ones1", [1, 128])
    onesb = kb.inp("onesb", [128, 1], BF16)
    onesr = kb.inp("onesr", [1, 128], BF16)
    identf = kb.inp("identf", [128, 128])

    o_xA = kb.out("o_xA", [128, 8 * OW])
    o_dbg1 = kb.out("o_dbg1", [128, TH]) if debug else None
    o_dbg2 = kb.out("o_dbg2", [128, TH]) if debug else None

    kb.open()
    v, s, g, t = kb.v, kb.s, kb.g, kb.t

    idt = kb.sb([128, 128], BF16, "idt")
    qsb = kb.sb([128, 1], F32, "qsb")
    ksb = kb.sb([128, 1], F32, "ksb")
    sel33b_t = kb.sb([128, 33], BF16, "sel33")
    selB33_t = kb.sb([33, 128], F32, "selB")
    ones1_t = kb.sb([1, 128], F32, "ones1")
    onesb_t = kb.sb([128, 1], BF16, "onesb")
    onesr_t = kb.sb([1, 128], BF16, "onesr")
    maskg_t = kb.sb([128, 4 * 256], BF16, "mg")
    mask0_t = kb.sb([128, 4 * 256], BF16, "m0")

    xqb = kb.sb([128, 8 * TH], BF16, "xqb")
    x_f = kb.sb([128, TH], F32, "xf")
    t1 = kb.sb([128, TH], F32, "t1")
    qpb = kb.sb([128, TH], BF16, "qpb")
    kpb = kb.sb([128, TH], BF16, "kpb")
    vpb = kb.sb([128, TH], BF16, "vpb")
    V_tm = kb.sb([128, 20 * 130], BF16, "vtm")
    O_all = kb.sb([128, 8 * OW], BF16, "oall")
    hilo = kb.sb([128, 2 * TH], BF16, "hilo")
    rnr = kb.sb([33, TH], F32, "rnr")
    mcol = kb.sb([128, 36], F32, "mcol")
    msb = kb.sb([36, 128], F32, "msb")
    negrow = kb.sb([1, 2 * 2304], F32, "negrow")
    mrow_d = kb.nc.dram_tensor("mrow_d", [36, 128], F32)
    idtf = kb.sb([128, 128], F32, "idtf")
    lrow = kb.sb([1, 256], F32, "lrow")
    lbv = kb.sb([64, 256], F32, "lbv")
    Pt2 = kb.sb([128, 2 * 1024], BF16, "pt")
    wpart = kb.sb([128, 2 * 8 * 128], BF16, "wp")
    tbl = kb.sb([128, 2 * 512], F32, "tbl")
    xbt = kb.sb([128, 2 * 512], F32, "xbt")
    xa_st = kb.sb([128, 2 * 512], F32, "xast")

    psf = [kb.ps([128, 512], F32, "psf") for _ in range(7)]
    psb = kb.ps([128, 1024], BF16, "psb")
    bank_ready = {}

    def bank_wait(bk):
        if bk in bank_ready:
            c, val = bank_ready[bk]
            t.wait_ge(c.sem, val)

    kb.load(idt[:], ident)
    kb.load(qsb[:], qsv)
    kb.load(ksb[:], ksv)
    kb.load(sel33b_t[:], sel33b)
    kb.load(selB33_t[:], selB33)
    kb.load(ones1_t[:], ones1)
    kb.load(onesb_t[:], onesb)
    kb.load(onesr_t[:], onesr)
    kb.load(idtf[:], identf)
    kb.load(maskg_t[:], maskg)
    kb.load(mask0_t[:], mask0)
    kb.load(xqb[:], xB, gp=True)
    kb.fence(v, kb.ld)
    v.memset(rnr[:], 1.0)
    v.memset(O_all[:], 0.0)
    v.memset(V_tm[:], 1.0)
    v.memset(x_f[:, 0:2], 0.0)
    v.memset(t1[:, 0:2], 0.0)

    qkv_tiles = tiles_of(2560, 2)

    def rhs_slice(k, c0, w):
        off = k * TH + c0 - (1 if k >= 4 else 0)
        return xqb[:, off:off + w]

    def qk_prep(part, hp):
        csT, ssT = (csq8, ssq8) if part == "q" else (csk, ssk)
        scl = qsb if part == "q" else ksb
        dstb = qpb if part == "q" else kpb
        if debug and hp == 7 and part == "k":
            kb.fence(nc.sync, kb.ve)
            kb.store(o_dbg1, x_f[:])
        kb.fence(s, kb.ve)
        s.activation(t1[:], x_f[:], AF.Square)
        kb.fence(v, kb.se)
        v.tensor_copy(hilo[:, 0:TH], t1[:])
        v.tensor_sub(hilo[:, TH:2 * TH], t1[:], hilo[:, 0:TH])
        kb.fence(t, kb.ve)
        for j, (c0, w) in enumerate(qkv_tiles):
            bank_wait(3)
            t.matmul(psf[3][0:33, 0:w], sel33b_t[:], hilo[:, c0:c0 + w],
                     start=True, stop=False)
            mi = t.matmul(psf[3][0:33, 0:w], sel33b_t[:],
                          hilo[:, TH + c0:TH + c0 + w],
                          start=False, stop=True)
            kb.pe.tick(mi)
            kb.fence(v, kb.pe)
            v.tensor_copy(rnr[0:33, c0:c0 + w], psf[3][0:33, 0:w])
            bank_ready[3] = kb.ve.mark()
        v.tensor_scalar_max(rnr[:], rnr[:], 1e-24)
        kb.fence(s, kb.ve)
        s.activation(rnr[:], rnr[:], AF.Ln)
        kb.fence(v, kb.se)
        v.tensor_scalar_mul(rnr[:], rnr[:], -0.5)
        kb.fence(s, kb.ve)
        s.activation(rnr[:], rnr[:], AF.Exp)
        kb.fence(t, kb.se)
        for j, (c0, w) in enumerate(qkv_tiles):
            bank_wait(2)
            mi = t.matmul(psf[2][:, 0:w], selB33_t[:], rnr[:, c0:c0 + w],
                          start=True, stop=True)
            kb.pe.tick(mi)
            kb.fence(v, kb.pe)
            v.scalar_tensor_tensor(t1[:, c0:c0 + w], x_f[:, c0:c0 + w],
                                   scl[:, 0:1], psf[2][:, 0:w],
                                   ALU.mult, ALU.mult)
            bank_ready[2] = kb.ve.mark()
        if debug and hp == 7 and part == "k":
            kb.fence(nc.sync, kb.ve)
            kb.store(o_dbg2, t1[:])
        for (dst0, src0, sgn) in ((0, 32, -1.0), (32, 0, 1.0),
                                  (64, 96, -1.0), (96, 64, 1.0)):
            v.tensor_scalar_mul(x_f[dst0:dst0 + 32, :],
                                t1[src0:src0 + 32, :], sgn)
        for j, (c0, w) in enumerate(qkv_tiles):
            bf = (j % 2) * 512
            kb.fence(nc.sync, kb.ve)
            kb.load(tbl[:, bf:bf + w], csT[:, c0:c0 + w])
            kb.fence(v, kb.ld)
            v.tensor_mul(t1[:, c0:c0 + w], t1[:, c0:c0 + w], tbl[:, bf:bf + w])
            kb.fence(nc.sync, kb.ve)
            kb.load(tbl[:, bf:bf + w], ssT[:, c0:c0 + w])
            kb.fence(v, kb.ld)
            v.tensor_mul(x_f[:, c0:c0 + w], x_f[:, c0:c0 + w],
                         tbl[:, bf:bf + w])
            v.tensor_add(t1[:, c0:c0 + w], t1[:, c0:c0 + w],
                         x_f[:, c0:c0 + w])
        v.tensor_copy(dstb[:], t1[:])
        return

    def wslc_of(idx):
        if idx < 24:
            hp, i3 = divmod(idx, 3)
            return wqkv.rearrange("p (k e) -> p k e", k=8)[
                :, :, i3 * 1024 + hp * 128: i3 * 1024 + (hp + 1) * 128]
        oc = idx - 24
        return wo.rearrange("p (k e) -> p k e", k=8)[
            :, :, oc * 128:(oc + 1) * 128]

    wldm = [0, 0]
    wuse = [None, None]

    def wload(idx):
        sl = idx % 2
        if wuse[sl] is not None:
            cm, vm = wuse[sl]
            nc.gpsimd.wait_ge(cm.sem, vm)
        kb.load(wpart[:, sl * 1024:sl * 1024 + 8 * 128]
                .rearrange("p (k e) -> p k e", k=8), wslc_of(idx), gp=True)
        wldm[sl] = kb.gq.v

    wload(0)
    for hp in range(8):
        for i3, part in enumerate(("q", "k", "v")):
            idx = hp * 3 + i3
            sl = idx % 2
            wb = wpart[:, sl * 1024:sl * 1024 + 8 * 128]
            kb.fence(t, kb.ve, kb.se)
            t.wait_ge(kb.gq.sem, wldm[sl])
            for j, (c0, w) in enumerate(qkv_tiles):
                bk = j % 2
                bank_wait(bk)
                for k in range(8):
                    mi = t.matmul(psf[bk][:, 0:w],
                                  wb[:, k * 128:(k + 1) * 128],
                                  rhs_slice(k, c0, w),
                                  start=(k == 0), stop=(k == 7))
                kb.pe.tick(mi)
                if part == "v":
                    kb.fence(s, kb.pe)
                    s.copy(vpb[:, c0:c0 + w], psf[bk][:, 0:w])
                    bank_ready[bk] = kb.se.mark()
                else:
                    kb.fence(v, kb.pe)
                    v.tensor_copy(x_f[:, c0:c0 + w], psf[bk][:, 0:w])
                    bank_ready[bk] = kb.ve.mark()
            wuse[sl] = kb.pe.mark()
            if idx + 1 < 32:
                wload(idx + 1)
            if part != "v":
                qk_prep(part, hp)
        kb.fence(t, kb.se)
        vtr = {}
        for jt in range(20):
            psl = 256 * (jt % 4)
            if jt % 4 in vtr:
                cm, vm = vtr[jt % 4]
                t.wait_ge(cm.sem, vm)
            kb.pe.tick(t.transpose(psb[:, psl:psl + 128],
                                   vpb[:, 2 + 128 * jt: 2 + 128 * (jt + 1)],
                                   idt[:]))
            kb.fence(v, kb.pe)
            v.tensor_copy(V_tm[:, 130 * jt:130 * jt + 64], psb[:, psl:psl + 64])
            v.tensor_copy(V_tm[:, 130 * jt + 65:130 * jt + 129],
                          psb[:, psl + 64:psl + 128])
            vtr[jt % 4] = kb.ve.mark()
        kb.fence(t, kb.ve)
        for h in (0, 1):
            hb = 64 * h
            for qt in range(18):
                qt0 = 258 + 128 * qt
                k0 = qt0 - 256
                bk = 2 + ((h * 18 + qt) % 2)
                bank_wait(bk)
                mi = t.matmul(psf[bk][:, 0:384],
                              qpb[hb:hb + 64, qt0:qt0 + 128],
                              kpb[hb:hb + 64, k0:k0 + 384],
                              start=True, stop=True)
                kb.pe.tick(mi)
                kb.fence(v, kb.pe)
                v.reduce_max(mcol[:, h * 18 + qt:h * 18 + qt + 1],
                             psf[bk][:, 0:384], axis=AX.X, negate=True)
                bank_ready[bk] = kb.ve.mark()
        kb.fence(t, kb.ve)
        bank_wait(2)
        kb.pe.tick(t.transpose(psf[2][0:36, 0:128], mcol[:], idtf[0:128, 0:128]))
        kb.fence(v, kb.pe)
        v.tensor_copy(msb[:], psf[2][0:36, 0:128])
        bank_ready[2] = kb.ve.mark()
        kb.fence(nc.sync, kb.ve, kb.ld)
        kb.store(mrow_d.ap(), msb[:])
        kb.fence(nc.sync, kb.st)
        kb.load(negrow[:], mrow_d.ap().rearrange("a b -> (a b)").unsqueeze(0))
        kb.fence(t, kb.ld)
        groups = [(wi, h) for wi in range(9) for h in (0, 1)]

        def emit_S(i):
            wi, h = groups[i]
            par = i % 2
            qc0 = 258 + 256 * wi
            kc0 = qc0 - 256
            hb = 64 * h
            for kt in range(4):
                bk = (4 if par == 0 else 0) + kt // 2
                co = (kt % 2) * 256
                if kt % 2 == 0:
                    bank_wait(bk)
                t.matmul(psf[bk][:, co:co + 256],
                         kpb[hb:hb + 64, kc0 + 128 * kt:kc0 + 128 * (kt + 1)],
                         qpb[hb:hb + 64, qc0:qc0 + 256],
                         start=True, stop=False)
                mi = t.matmul(psf[bk][:, co:co + 256],
                              ones1_t[0:1, :],
                              negrow[0:1, h * 2304 + 256 * wi:
                                     h * 2304 + 256 * (wi + 1)],
                              start=False, stop=True)
                kb.pe.tick(mi)
            bank_ready[("smm", par)] = kb.pe.mark()

        emit_S(0)
        for i, (wi, h) in enumerate(groups):
            par = i % 2
            b0, b1 = (4, 5) if par == 0 else (0, 1)
            hb = 64 * h
            mask_t = mask0_t if wi == 1 else maskg_t
            cm, vm = bank_ready[("smm", par)]
            s._eng.wait_ge(cm.sem, vm)
            s._eng.activation(Pt2[:, par * 1024:par * 1024 + 512], psf[b0][:],
                              AF.Exp)
            kb.se.tick(s._eng.activation(
                Pt2[:, par * 1024 + 512:par * 1024 + 1024], psf[b1][:],
                AF.Exp))
            bank_ready[b0] = kb.se.mark()
            bank_ready[b1] = kb.se.mark()
            kb.fence(g, kb.se)
            g.tensor_mul(Pt2[:, par * 1024:(par + 1) * 1024],
                         Pt2[:, par * 1024:(par + 1) * 1024], mask_t[:])
            gm = kb.ge.mark()
            if i + 1 < len(groups):
                emit_S(i + 1)
            t.wait_ge(gm[0].sem, gm[1])
            bank_wait(6)
            jt0 = (258 + 256 * wi - 256 - 2) // 128
            for kt in range(4):
                mi = t.matmul(psf[6][0:65, 0:256],
                              V_tm[:, 130 * (jt0 + kt) + 65 * h:
                                   130 * (jt0 + kt) + 65 * h + 65],
                              Pt2[:, par * 1024 + 256 * kt:
                                  par * 1024 + 256 * (kt + 1)],
                              start=(kt == 0), stop=(kt == 3))
            kb.pe.tick(mi)
            kb.fence(v, kb.pe)
            v.tensor_copy(lrow[:], psf[6][64:65, 0:256])
            kb.fence(s, kb.ve)
            s.activation(lrow[:], lrow[:], AF.Ln)
            s.activation(lrow[:], lrow[:], AF.Exp, scale=-1.0)
            kb.fence(t, kb.se)
            bank_wait(3)
            mi = t.matmul(psf[3][0:64, 0:256], ones1_t[0:1, 0:64], lrow[:],
                          start=True, stop=True)
            kb.pe.tick(mi)
            kb.fence(v, kb.pe)
            v.tensor_copy(lbv[:, 0:256], psf[3][0:64, 0:256])
            bank_ready[3] = kb.ve.mark()
            if wi == 0:
                v.tensor_mul(O_all[hb:hb + 64, hp * OW:hp * OW + 1],
                             psf[6][0:64, 255:256], lbv[:, 255:256])
            else:
                oc0 = hp * OW + 1 + 256 * (wi - 1)
                v.tensor_mul(O_all[hb:hb + 64, oc0:oc0 + 256],
                             psf[6][0:64, 0:256], lbv[:, 0:256])
            bank_ready[6] = kb.ve.mark()

    ow_tiles = tiles_of(OW, 0)
    kb.fence(t, kb.ve)
    for oc in range(8):
        idx = 24 + oc
        sl = idx % 2
        wb = wpart[:, sl * 1024:sl * 1024 + 8 * 128]
        t.wait_ge(kb.gq.sem, wldm[sl])
        for j, (c0, w) in enumerate(ow_tiles):
            bk = j % 2
            bf = (j % 2) * 512
            kb.fence(nc.sync, kb.ve)
            kb.load(xbt[:, bf:bf + w], xB[:, oc * TH + 513 + c0:
                                          oc * TH + 513 + c0 + w],
                    slow=(w < 8))
            bank_wait(bk)
            for k in range(8):
                mi = t.matmul(psf[bk][:, 0:w],
                              wb[:, k * 128:(k + 1) * 128],
                              O_all[:, k * OW + c0:k * OW + c0 + w],
                              start=(k == 0), stop=(k == 7))
            kb.pe.tick(mi)
            kb.fence(v, kb.pe, kb.ld, kb.st)
            v.tensor_add(xa_st[:, bf:bf + w], psf[bk][:, 0:w],
                         xbt[:, bf:bf + w])
            bank_ready[bk] = kb.ve.mark()
            kb.fence(nc.sync, kb.ve)
            kb.store(o_xA[:, oc * OW + c0:oc * OW + c0 + w],
                     xa_st[:, bf:bf + w], slow=(w < 8))
        wuse[sl] = kb.pe.mark()
        if idx + 1 < 32:
            wload(idx + 1)
    return kb.finish()


def _rotary_tables():
    j = np.arange(32)
    inv_freq = 1.0 / 10000.0 ** (2 * j / 64.0)
    base = (2 * j + 0.4 * 64) / (1.4 * 64)
    tpos = np.arange(-HALO, T, dtype=np.float64)
    fr = np.outer(inv_freq, tpos)
    sc = base[:, None] ** ((tpos[None, :] - T // 2) / 128.0)
    cos = np.cos(fr)
    sin = np.sin(fr)
    d2 = lambda a: np.concatenate([a, a], 0)
    csq8 = d2(8.0 * cos * sc).astype(np.float32)
    ssq8 = d2(8.0 * sin * sc).astype(np.float32)
    csk = d2(cos / sc).astype(np.float32)
    ssk = d2(sin / sc).astype(np.float32)
    dup = lambda a: np.ascontiguousarray(np.tile(a, (2, 1)))
    return tuple(dup(a) for a in (csq8, ssq8, csk, ssk))


_TABLES = None


def rotary_tables():
    global _TABLES
    if _TABLES is None:
        _TABLES = _rotary_tables()
    return _TABLES


def _mask_tiles(zero_prev):
    m = np.zeros((128, 4 * 256), dtype=BF)
    for kt in range(4):
        for p in range(128):
            c = kt * 128 + p
            r = np.arange(256)
            ok = (r <= c) & (c <= r + 256)
            if zero_prev and kt < 2:
                ok = ok & False
            m[p, kt * 256:(kt + 1) * 256] = ok.astype(BF)
    return m


_SEL = None


def attn_consts():
    global _SEL
    if _SEL is None:
        sel33b = np.zeros((128, 33), dtype=BF)
        sel33b[0:64, 0] = 1
        sel33b[64:128, 32] = 1
        selB33 = np.zeros((33, 128), dtype=np.float32)
        selB33[0, 0:64] = 1
        selB33[32, 64:128] = 1
        ones1 = np.ones((1, 128), dtype=np.float32)
        onesb = np.ones((128, 1), dtype=BF)
        onesr = np.ones((1, 128), dtype=BF)
        maskg = _mask_tiles(False)
        mask0 = _mask_tiles(True)
        _SEL = dict(sel33b=sel33b, selB33=selB33, ones1=ones1, onesb=onesb,
                    onesr=onesr, maskg=maskg, mask0=mask0, ident=_IDENT,
                    identf=np.eye(128, dtype=np.float32))
    return _SEL


def slice_tok_cm(x_full, b, g0, lead, width):
    xs = np.zeros((width, D), np.float32)
    lo = g0 - lead
    src0 = max(0, lo)
    src1 = min(T, lo + width)
    xs[src0 - lo:src1 - lo, :] = x_full[b, src0:src1]
    return chunk_fm(np.ascontiguousarray(xs.T))


def attn_inmaps(inputs, layer, xB_full):
    cst = attn_consts()
    csq8, ssq8, csk, ssk = rotary_tables()
    wqkv = chunk_fm(inputs["L0_ffn_Wqkv"][layer])
    wo = chunk_fm(inputs["L0_ffn_Wo"][layer])
    qsv = np.ascontiguousarray(
        np.tile(inputs["L0_ffn_q_scale"][layer], 2)[:, None])
    ksv = np.ascontiguousarray(
        np.tile(inputs["L0_ffn_k_scale"][layer], 2)[:, None])
    maps = []
    for c in range(NCORE):
        b, ch = c // 4, c % 4
        g0 = ch * TOK
        cols = slice(g0 - 514 + HALO, g0 + TOK + HALO)
        maps.append(dict(
            xB=slice_tok_cm(xB_full, b, g0, 514, TH),
            wqkv=wqkv, wo=wo, qsv=qsv, ksv=ksv,
            csq8=np.ascontiguousarray(csq8[:, cols]),
            ssq8=np.ascontiguousarray(ssq8[:, cols]),
            csk=np.ascontiguousarray(csk[:, cols]),
            ssk=np.ascontiguousarray(ssk[:, cols]),
            maskg=cst["maskg"],
            mask0=cst["mask0"] if ch == 0 else cst["maskg"],
            ident=cst["ident"], sel33b=cst["sel33b"], selB33=cst["selB33"],
            ones1=cst["ones1"], onesb=cst["onesb"], onesr=cst["onesr"],
            identf=cst["identf"],
        ))
    return maps


HID = 2816


def build_ffn():
    kb = KB()
    nc = kb.nc
    xA = kb.inp("xA", [128, 8 * OW])
    w1a = kb.inp("w1a", [128, 8 * HID])
    w1g = kb.inp("w1g", [128, 8 * HID])
    w2 = kb.inp("w2", [128, 22 * 1024])
    b1a = kb.inp("b1a", [128, 22])
    b1g = kb.inp("b1g", [128, 22])
    b2v = kb.inp("b2v", [128, 8])
    lng = kb.inp("lng", [128, 8])
    lnb = kb.inp("lnb", [128, 8])
    ones1 = kb.inp("ones1", [1, 128])
    onesbf = kb.inp("onesbf", [128, 1], BF16)

    o_xB = kb.out("o_xB", [128, 8 * TOK])

    kb.open()
    v, s, g, t = kb.v, kb.s, kb.g, kb.t

    xA_f = kb.sb([128, 8 * OW], F32, "xaf")
    xnf = kb.sb([128, 8 * TOK], BF16, "xnf")
    hilo = kb.sb([128, 4 * 512], BF16, "hilo")
    sq = kb.sb([128, 512], F32, "sq")
    rows = kb.sb([1, 4 * TOK], F32, "rows")
    hbuf = kb.sb([128, 22 * 512], BF16, "hbuf")
    gl = kb.sb([128, 512], F32, "gl")
    tmp = kb.sb([128, 512], F32, "tmp")
    wca = kb.sb([128, 2 * 8 * 128], BF16, "wca")
    wcg = kb.sb([128, 2 * 8 * 128], BF16, "wcg")
    wc2 = kb.sb([128, 2 * 1024], BF16, "wc2")
    b1a_t = kb.sb([128, 22], F32, "b1at")
    b1g_t = kb.sb([128, 22], F32, "b1gt")
    b2_t = kb.sb([128, 8], F32, "b2t")
    lng_t = kb.sb([128, 8], F32, "lngt")
    lnb_t = kb.sb([128, 8], F32, "lnbt")
    ones1_t = kb.sb([1, 128], F32, "ones1")
    onesbf_t = kb.sb([128, 1], BF16, "onesbf")
    xst = kb.sb([128, 2 * 512], F32, "xst")

    psf = [kb.ps([128, 512], F32, "psf") for _ in range(8)]
    bank_ready = [None] * 8

    def bank_wait(bk):
        if bank_ready[bk] is not None:
            c, val = bank_ready[bk]
            t.wait_ge(c.sem, val)

    kb.load(xA_f[:], xA)
    kb.load(b1a_t[:], b1a)
    kb.load(b1g_t[:], b1g)
    kb.load(b2_t[:], b2v)
    kb.load(lng_t[:], lng)
    kb.load(lnb_t[:], lnb)
    kb.load(ones1_t[:], ones1)
    kb.load(onesbf_t[:], onesbf)

    def xsh(c, c0, w):
        off = c * OW + c0 + (1 if c < 4 else 0)
        return xA_f[:, off:off + w]

    t_tiles = tiles_of(TOK, 0)
    kb.fence(v, kb.ld)
    kb.fence(s, kb.ld)
    for (c0, w) in t_tiles:
        bank_wait(0)
        bank_wait(1)
        for c in range(8):
            v.tensor_copy(hilo[:, 0:w], xsh(c, c0, w))
            v.tensor_sub(hilo[:, w:2 * w], xsh(c, c0, w), hilo[:, 0:w])
            kb.fence(s, kb.ve)
            s.activation(sq[:, 0:w], xsh(c, c0, w), AF.Square)
            kb.fence(v, kb.se)
            v.tensor_copy(hilo[:, 2 * w:3 * w], sq[:, 0:w])
            v.tensor_sub(hilo[:, 3 * w:4 * w], sq[:, 0:w], hilo[:, 2 * w:3 * w])
            kb.fence(t, kb.ve)
            t.matmul(psf[0][0:1, 0:w], onesbf_t[:], hilo[:, 0:w],
                     start=(c == 0), stop=False)
            mi = t.matmul(psf[0][0:1, 0:w], onesbf_t[:], hilo[:, w:2 * w],
                          start=False, stop=(c == 7))
            kb.pe.tick(mi)
            t.matmul(psf[1][0:1, 0:w], onesbf_t[:], hilo[:, 2 * w:3 * w],
                     start=(c == 0), stop=False)
            mi = t.matmul(psf[1][0:1, 0:w], onesbf_t[:], hilo[:, 3 * w:4 * w],
                          start=False, stop=(c == 7))
            kb.pe.tick(mi)
            if c < 7:
                kb.fence(v, kb.pe)
        kb.fence(v, kb.pe)
        v.tensor_copy(rows[0:1, c0:c0 + w], psf[0][0:1, 0:w])
        v.tensor_copy(rows[0:1, TOK + c0:TOK + c0 + w], psf[1][0:1, 0:w])
        bank_ready[0] = kb.ve.mark()
        bank_ready[1] = kb.ve.mark()
    v.tensor_scalar_mul(rows[0:1, 2 * TOK:3 * TOK], rows[0:1, 0:TOK],
                        -1.0 / D)
    v.tensor_mul(rows[0:1, 0:TOK], rows[0:1, 2 * TOK:3 * TOK],
                 rows[0:1, 2 * TOK:3 * TOK])
    v.scalar_tensor_tensor(rows[0:1, 3 * TOK:4 * TOK],
                           rows[0:1, TOK:2 * TOK], 1.0 / D,
                           rows[0:1, 0:TOK], ALU.mult, ALU.subtract)
    v.tensor_scalar_add(rows[0:1, 3 * TOK:4 * TOK],
                        rows[0:1, 3 * TOK:4 * TOK], 1e-5)
    kb.fence(s, kb.ve)
    s.activation(rows[0:1, 3 * TOK:4 * TOK], rows[0:1, 3 * TOK:4 * TOK],
                 AF.Ln)
    kb.fence(v, kb.se)
    v.tensor_scalar_mul(rows[0:1, 3 * TOK:4 * TOK],
                        rows[0:1, 3 * TOK:4 * TOK], -0.5)
    kb.fence(s, kb.ve)
    s.activation(rows[0:1, 3 * TOK:4 * TOK], rows[0:1, 3 * TOK:4 * TOK],
                 AF.Exp)
    kb.fence(t, kb.se)
    for c in range(8):
        for j, (c0, w) in enumerate(t_tiles):
            bank_wait(2)
            mi = t.matmul(psf[2][:, 0:w], ones1_t[:],
                          rows[0:1, 2 * TOK + c0:2 * TOK + c0 + w],
                          start=True, stop=True)
            kb.pe.tick(mi)
            bank_wait(3)
            mi = t.matmul(psf[3][:, 0:w], ones1_t[:],
                          rows[0:1, 3 * TOK + c0:3 * TOK + c0 + w],
                          start=True, stop=True)
            kb.pe.tick(mi)
            kb.fence(v, kb.pe)
            v.tensor_add(tmp[:, 0:w], xsh(c, c0, w), psf[2][:, 0:w])
            bank_ready[2] = kb.ve.mark()
            v.tensor_mul(tmp[:, 0:w], tmp[:, 0:w], psf[3][:, 0:w])
            bank_ready[3] = kb.ve.mark()
            v.tensor_scalar(xnf[:, c * TOK + c0:c * TOK + c0 + w],
                            tmp[:, 0:w], lng_t[:, c:c + 1],
                            lnb_t[:, c:c + 1], ALU.mult, ALU.add)

    WB1 = 8 * 128
    wmark = {}

    def wprefetch(hc, buf):
        if ("w1", buf) in wmark:
            c, val = wmark[("w1", buf)]
            nc.gpsimd.wait_ge(c.sem, val)
        kb.load(wca[:, buf * WB1:(buf + 1) * WB1]
                .rearrange("p (k e) -> p k e", k=8),
                w1a.rearrange("p (k e) -> p k e", k=8)
                [:, :, hc * 128:(hc + 1) * 128], gp=True)
        kb.load(wcg[:, buf * WB1:(buf + 1) * WB1]
                .rearrange("p (k e) -> p k e", k=8),
                w1g.rearrange("p (k e) -> p k e", k=8)
                [:, :, hc * 128:(hc + 1) * 128], gp=True)
        ldm[buf] = kb.gq.v

    ldm = [0, 0]
    ldm2 = [0, 0]
    for th in range(4):
        tc0 = th * 512
        wprefetch(0, 0)
        for hc in range(22):
            buf = hc % 2
            if hc + 1 < 22:
                wprefetch(hc + 1, 1 - buf)
            t.wait_ge(kb.gq.sem, ldm[buf])
            kb.fence(t, kb.ve)
            bank_wait(0)
            bank_wait(1)
            for k in range(8):
                t.matmul(psf[0][:],
                         wca[:, buf * WB1 + k * 128:buf * WB1 + (k + 1) * 128],
                         xnf[:, k * TOK + tc0:k * TOK + tc0 + 512],
                         start=(k == 0), stop=(k == 7))
                mi = t.matmul(psf[1][:],
                              wcg[:, buf * WB1 + k * 128:
                                   buf * WB1 + (k + 1) * 128],
                              xnf[:, k * TOK + tc0:k * TOK + tc0 + 512],
                              start=(k == 0), stop=(k == 7))
            kb.pe.tick(mi)
            wmark[("w1", buf)] = kb.pe.mark()
            kb.fence(s, kb.pe)
            s.activation(gl[:], psf[1][:], AF.Gelu, bias=b1g_t[:, hc:hc + 1])
            bank_ready[1] = kb.se.mark()
            kb.fence(v, kb.se)
            v.scalar_tensor_tensor(hbuf[:, hc * 512:(hc + 1) * 512],
                                   psf[0][:], b1a_t[:, hc:hc + 1], gl[:],
                                   ALU.add, ALU.mult)
            bank_ready[0] = kb.ve.mark()
        for bk in range(8):
            bank_wait(bk)
        if ("w2", 0) in wmark:
            c, val = wmark[("w2", 0)]
            nc.gpsimd.wait_ge(c.sem, val)
        kb.load(wc2[:, 0:1024], w2[:, 0:1024], gp=True)
        ldm2[0] = kb.gq.v
        for hc in range(22):
            buf = hc % 2
            if hc + 1 < 22:
                if ("w2", 1 - buf) in wmark:
                    c, val = wmark[("w2", 1 - buf)]
                    nc.gpsimd.wait_ge(c.sem, val)
                kb.load(wc2[:, (1 - buf) * 1024:(2 - buf) * 1024],
                        w2[:, (hc + 1) * 1024:(hc + 2) * 1024], gp=True)
                ldm2[1 - buf] = kb.gq.v
            t.wait_ge(kb.gq.sem, ldm2[buf])
            for oc in range(8):
                mi = t.matmul(psf[oc][:],
                              wc2[:, buf * 1024 + oc * 128:
                                   buf * 1024 + (oc + 1) * 128],
                              hbuf[:, hc * 512:(hc + 1) * 512],
                              start=(hc == 0), stop=(hc == 21))
            kb.pe.tick(mi)
            wmark[("w2", buf)] = kb.pe.mark()
        for oc in range(8):
            kb.fence(v, kb.pe, kb.st)
            v.scalar_tensor_tensor(xst[:, (oc % 2) * 512:(oc % 2) * 512 + 512],
                                   psf[oc][:], b2_t[:, oc:oc + 1],
                                   xA_f[:, oc * OW + 1 + tc0:
                                        oc * OW + 1 + tc0 + 512],
                                   ALU.add, ALU.add)
            bank_ready[oc] = kb.ve.mark()
            kb.fence(nc.sync, kb.ve)
            kb.store(o_xB[:, oc * TOK + tc0:oc * TOK + tc0 + 512],
                     xst[:, (oc % 2) * 512:(oc % 2) * 512 + 512])
    return kb.finish()


def ffn_inmaps(inputs, layer, xA_cms):
    W1 = inputs["L0_ffn_W1"][layer]
    b1 = inputs["L0_ffn_b1"][layer]
    W2 = inputs["L0_ffn_W2"][layer]
    w1a = np.zeros((D, HID), np.float32)
    w1g = np.zeros((D, HID), np.float32)
    w1a[:, 0:2730] = W1[:, 0:2730]
    w1g[:, 0:2730] = W1[:, 2730:5460]
    b1a = np.zeros((HID, 1), np.float32)
    b1g = np.zeros((HID, 1), np.float32)
    b1a[0:2730, 0] = b1[0:2730]
    b1g[0:2730, 0] = b1[2730:5460]
    w2p = np.zeros((HID, 1024), np.float32)
    w2p[0:2730] = W2
    cst = attn_consts()
    shared = dict(
        w1a=chunk_fm(w1a), w1g=chunk_fm(w1g), w2=chunk_fm(w2p),
        b1a=chunk_fm(b1a), b1g=chunk_fm(b1g),
        b2v=chunk_fm(inputs["L0_ffn_b2"][layer][:, None]),
        lng=chunk_fm(inputs["L0_ffn_ln_g"][layer][:, None]),
        lnb=chunk_fm(inputs["L0_ffn_ln_b"][layer][:, None]),
        ones1=cst["ones1"],
        onesbf=np.ones((128, 1), dtype=BF),
    )
    return [dict(xA=xA_cms[c], **shared) for c in range(NCORE)]


def build_p8a(debug=False):
    kb = KB()
    nc = kb.nc
    xn0 = kb.inp("xn0", [128, 8 * TOK])
    sz = kb.inp("sz", [128, 16 * TOK])
    y2 = kb.inp("y2", [128, 16 * TOK])
    xorig = kb.inp("xorig", [128, 8 * TOK])
    xB6 = kb.inp("xB6", [128, 8 * TOK])
    w_op = kb.inp("w_op", [128, 16 * 1024])
    w_cfc = kb.inp("w_cfc", [128, 8 * EI])
    w_cpj = kb.inp("w_cpj", [128, 16 * 1024])
    ident = kb.inp("ident", [128, 128], BF16)
    onesbf = kb.inp("onesbf", [128, 1], BF16)
    ones1 = kb.inp("ones1", [1, 128])

    o_x1 = kb.out("o_x1", [128, 8 * TOK])
    o_dbg1 = kb.out("o_dbg1", [128, 8 * TOK]) if debug else None
    o_dbg2 = kb.out("o_dbg2", [128, 8 * TOK], BF16) if debug else None

    kb.open()
    v, s, g, t = kb.v, kb.s, kb.g, kb.t
    idt = kb.sb([128, 128], BF16, "idt")
    onesb_t = kb.sb([128, 1], BF16, "onesb")
    ones1_t = kb.sb([1, 128], F32, "ones1")
    x1i = kb.sb([128, 8 * TOK], F32, "x1i")
    yqb = kb.sb([128, 16 * TOK], BF16, "yqb")
    hrelu = kb.sb([128, 16 * 512], BF16, "hrelu")
    wbuf = kb.sb([128, 16 * 128], BF16, "wbuf")
    stage = kb.sb([128, 2 * 512], F32, "stage")
    bstage = kb.sb([128, 2 * 512], BF16, "bstage")
    hilo = kb.sb([128, 4 * 512], BF16, "hilo")
    sq = kb.sb([128, 512], F32, "sq")
    rows = kb.sb([1, TOK], F32, "rows")
    strm = kb.sb([128, 2 * 512], F32, "strm")
    strm2 = kb.sb([128, 2 * 512], F32, "strm2")

    psf = [kb.ps([128, 512], F32, "psf") for _ in range(7)]
    psb = kb.ps([128, 1024], BF16, "psb")
    bank_ready = [None] * 8

    def bank_wait(bk):
        if bank_ready[bk] is not None:
            c, val = bank_ready[bk]
            t.wait_ge(c.sem, val)

    kb.load(idt[:], ident)
    kb.load(onesb_t[:], onesbf)
    kb.load(ones1_t[:], ones1)
    t_tiles = tiles_of(TOK, 0)

    def wslice(wcm, K, oc, dst):
        kb.fence(nc.gpsimd, kb.pe)
        kb.load(dst[:, 0:K * 128].rearrange("p (k e) -> p k e", k=K),
                wcm.rearrange("p (k e) -> p k e", k=K)
                [:, :, oc * 128:(oc + 1) * 128], gp=True)
        kb.fence(t, kb.gq)

    kb.fence(v, kb.ld)
    for c in range(16):
        for j, (c0, w) in enumerate(t_tiles):
            bf = (j % 2) * 512
            kb.fence(nc.sync, kb.ge)
            kb.load(strm[:, bf:bf + w], y2[:, c * TOK + c0:c * TOK + c0 + w])
            kb.load(strm2[:, bf:bf + w], sz[:, c * TOK + c0:c * TOK + c0 + w])
            kb.fence(g, kb.ld)
            g.tensor_mul(yqb[:, c * TOK + c0:c * TOK + c0 + w],
                         strm[:, bf:bf + w], strm2[:, bf:bf + w])

    for oc in range(8):
        wslice(w_op, 16, oc, wbuf)
        for j, (c0, w) in enumerate(t_tiles):
            bk = j % 2
            bank_wait(bk)
            kb.fence(t, kb.ge) if (oc == 0 and j == 0) else None
            for k in range(16):
                mi = t.matmul(psf[bk][:, 0:w],
                              wbuf[:, k * 128:(k + 1) * 128],
                              yqb[:, k * TOK + c0:k * TOK + c0 + w],
                              start=(k == 0), stop=(k == 15))
            kb.pe.tick(mi)
            kb.fence(nc.sync, kb.ve)
            kb.load(strm[:, 0:w], xn0[:, oc * TOK + c0:oc * TOK + c0 + w])
            kb.fence(v, kb.pe, kb.ld)
            v.tensor_add(x1i[:, oc * TOK + c0:oc * TOK + c0 + w],
                         psf[bk][:, 0:w], strm[:, 0:w])
            bank_ready[bk] = kb.ve.mark()

    if debug:
        kb.fence(nc.sync, kb.ve)
        kb.store(o_dbg1, x1i[:])
    for (c0, w) in t_tiles:
        bank_wait(0)
        for c in range(8):
            kb.fence(s, kb.ve)
            s.activation(sq[:, 0:w], x1i[:, c * TOK + c0:c * TOK + c0 + w],
                         AF.Square)
            kb.fence(v, kb.se)
            v.tensor_copy(hilo[:, 0:w], sq[:, 0:w])
            v.tensor_sub(hilo[:, w:2 * w], sq[:, 0:w], hilo[:, 0:w])
            kb.fence(t, kb.ve)
            t.matmul(psf[0][0:1, 0:w], onesb_t[:], hilo[:, 0:w],
                     start=(c == 0), stop=False)
            mi = t.matmul(psf[0][0:1, 0:w], onesb_t[:], hilo[:, w:2 * w],
                          start=False, stop=(c == 7))
            kb.pe.tick(mi)
            if c < 7:
                kb.fence(v, kb.pe)
        kb.fence(v, kb.pe)
        v.tensor_copy(rows[0:1, c0:c0 + w], psf[0][0:1, 0:w])
        bank_ready[0] = kb.ve.mark()
    v.tensor_scalar_mul(rows[0:1, 0:TOK], rows[0:1, 0:TOK], 1.0 / D)
    v.tensor_scalar_add(rows[0:1, 0:TOK], rows[0:1, 0:TOK], F32EPS)
    kb.fence(s, kb.ve)
    s.activation(rows[0:1, 0:TOK], rows[0:1, 0:TOK], AF.Ln)
    kb.fence(v, kb.se)
    v.tensor_scalar_mul(rows[0:1, 0:TOK], rows[0:1, 0:TOK], -0.5)
    kb.fence(s, kb.ve)
    s.activation(rows[0:1, 0:TOK], rows[0:1, 0:TOK], AF.Exp)
    kb.fence(t, kb.se)
    for c in range(8):
        for j, (c0, w) in enumerate(t_tiles):
            bank_wait(2)
            mi = t.matmul(psf[2][:, 0:w], ones1_t[:], rows[0:1, c0:c0 + w],
                          start=True, stop=True)
            kb.pe.tick(mi)
            kb.fence(v, kb.pe)
            v.tensor_mul(yqb[:, c * TOK + c0:c * TOK + c0 + w],
                         x1i[:, c * TOK + c0:c * TOK + c0 + w],
                         psf[2][:, 0:w])
            bank_ready[2] = kb.ve.mark()

    if debug:
        kb.fence(nc.sync, kb.ve)
        kb.store(o_dbg2, yqb[:, 0:8 * TOK])
    for th in range(4):
        tc0 = th * 512
        for hc in range(16):
            wslice(w_cfc, 8, hc, wbuf)
            bank_wait(hc % 2)
            for k in range(8):
                mi = t.matmul(psf[hc % 2][:], wbuf[:, k * 128:(k + 1) * 128],
                              yqb[:, k * TOK + tc0:k * TOK + tc0 + 512],
                              start=(k == 0), stop=(k == 7))
            kb.pe.tick(mi)
            kb.fence(v, kb.pe, kb.se)
            v.tensor_scalar_max(sq[:], psf[hc % 2][:], 0.0)
            bank_ready[hc % 2] = kb.ve.mark()
            kb.fence(s, kb.ve)
            s.activation(hrelu[:, hc * 512:(hc + 1) * 512], sq[:], AF.Square)
        kb.fence(t, kb.se)
        for ocg in range(2):
            for bk in range(2, 6):
                bank_wait(bk)
            for hc in range(16):
                kb.fence(nc.gpsimd, kb.pe)
                kb.load(wbuf[:, 0:4 * 128].rearrange("p (k e) -> p k e", k=4),
                        w_cpj.rearrange("p (k e) -> p k e", k=16)
                        [:, hc, ocg * 512:(ocg + 1) * 512]
                        .rearrange("p (a e) -> p a e", a=4), gp=True)
                kb.fence(t, kb.gq)
                for oc4 in range(4):
                    mi = t.matmul(psf[2 + oc4][:],
                                  wbuf[:, oc4 * 128:(oc4 + 1) * 128],
                                  hrelu[:, hc * 512:(hc + 1) * 512],
                                  start=(hc == 0), stop=(hc == 15))
                kb.pe.tick(mi)
            for oc4 in range(4):
                oc = ocg * 4 + oc4
                kb.fence(nc.sync, kb.ve)
                kb.load(strm[:, 0:512],
                        xB6[:, oc * TOK + tc0:oc * TOK + tc0 + 512])
                kb.load(strm2[:, 0:512],
                        xorig[:, oc * TOK + tc0:oc * TOK + tc0 + 512])
                kb.fence(v, kb.pe, kb.ld, kb.st)
                v.tensor_add(stage[:, 0:512], psf[2 + oc4][:], strm[:, 0:512])
                bank_ready[2 + oc4] = kb.ve.mark()
                v.tensor_add(stage[:, 0:512], stage[:, 0:512], strm2[:, 0:512])
                v.tensor_add(x1i[:, oc * TOK + tc0:oc * TOK + tc0 + 512],
                             x1i[:, oc * TOK + tc0:oc * TOK + tc0 + 512],
                             stage[:, 0:512])
                kb.fence(nc.sync, kb.ve)
                kb.store(o_x1[:, oc * TOK + tc0:oc * TOK + tc0 + 512],
                         x1i[:, oc * TOK + tc0:oc * TOK + tc0 + 512])

    return kb.finish()


def build_p8b():
    kb = KB()
    nc = kb.nc
    x1in = kb.inp("x1in", [128, 8 * TOK])
    w_qkv1 = kb.inp("w_qkv1", [128, 8 * 3072])
    ident = kb.inp("ident", [128, 128], BF16)
    onesbf = kb.inp("onesbf", [128, 1], BF16)
    ones1 = kb.inp("ones1", [1, 128])

    o_xn1 = kb.out("o_xn1", [128, 8 * TOK], BF16)
    o_q = kb.out("o_q", [128, 8 * TOK], BF16)
    o_k = kb.out("o_k", [128, 8 * TOK], BF16)
    o_vtm = kb.out("o_vtm", [128, 16 * 1024], BF16)
    o_kmax = kb.out("o_kmax", [1, 16])

    kb.open()
    v, s, g, t = kb.v, kb.s, kb.g, kb.t
    idt = kb.sb([128, 128], BF16, "idt")
    onesb_t = kb.sb([128, 1], BF16, "onesb")
    ones1_t = kb.sb([1, 128], F32, "ones1")
    x1i = kb.sb([128, 8 * TOK], F32, "x1i")
    xn1b = kb.sb([128, 8 * TOK], BF16, "xn1b")
    wbuf = kb.sb([128, 16 * 128], BF16, "wbuf")
    stage = kb.sb([128, 2 * 512], F32, "stage")
    bstage = kb.sb([128, 2 * 512], BF16, "bstage")
    hilo = kb.sb([128, 4 * 512], BF16, "hilo")
    sq = kb.sb([128, 512], F32, "sq")
    rows = kb.sb([1, 4 * TOK], F32, "rows")
    kmaxr = kb.sb([1, 16], F32, "kmaxr")
    krow = kb.sb([1, 512], F32, "krow")
    psf = [kb.ps([128, 512], F32, "psf") for _ in range(7)]
    psb = kb.ps([128, 1024], BF16, "psb")
    bank_ready = [None] * 8

    def bank_wait(bk):
        if bank_ready[bk] is not None:
            c, val = bank_ready[bk]
            t.wait_ge(c.sem, val)

    kb.load(idt[:], ident)
    kb.load(onesb_t[:], onesbf)
    kb.load(ones1_t[:], ones1)
    kb.load(x1i[:], x1in)
    t_tiles = tiles_of(TOK, 0)
    kb.fence(v, kb.ld)
    kb.fence(s, kb.ld)
    kb.fence(t, kb.ld)

    for (c0, w) in t_tiles:
        bank_wait(0)
        bank_wait(1)
        for c in range(8):
            xs_ap = x1i[:, c * TOK + c0:c * TOK + c0 + w]
            v.tensor_copy(hilo[:, 0:w], xs_ap)
            v.tensor_sub(hilo[:, w:2 * w], xs_ap, hilo[:, 0:w])
            kb.fence(s, kb.ve)
            s.activation(sq[:, 0:w], xs_ap, AF.Square)
            kb.fence(v, kb.se)
            v.tensor_copy(hilo[:, 2 * w:3 * w], sq[:, 0:w])
            v.tensor_sub(hilo[:, 3 * w:4 * w], sq[:, 0:w], hilo[:, 2 * w:3 * w])
            kb.fence(t, kb.ve)
            t.matmul(psf[0][0:1, 0:w], onesb_t[:], hilo[:, 0:w],
                     start=(c == 0), stop=False)
            mi = t.matmul(psf[0][0:1, 0:w], onesb_t[:], hilo[:, w:2 * w],
                          start=False, stop=(c == 7))
            kb.pe.tick(mi)
            t.matmul(psf[1][0:1, 0:w], onesb_t[:], hilo[:, 2 * w:3 * w],
                     start=(c == 0), stop=False)
            mi = t.matmul(psf[1][0:1, 0:w], onesb_t[:], hilo[:, 3 * w:4 * w],
                          start=False, stop=(c == 7))
            kb.pe.tick(mi)
            kb.fence(v, kb.pe)
        v.tensor_copy(rows[0:1, c0:c0 + w], psf[0][0:1, 0:w])
        v.tensor_copy(rows[0:1, TOK + c0:TOK + c0 + w], psf[1][0:1, 0:w])
        bank_ready[0] = kb.ve.mark()
        bank_ready[1] = kb.ve.mark()
    v.tensor_scalar_mul(rows[0:1, 2 * TOK:3 * TOK], rows[0:1, 0:TOK],
                        -1.0 / D)
    v.tensor_mul(rows[0:1, 0:TOK], rows[0:1, 2 * TOK:3 * TOK],
                 rows[0:1, 2 * TOK:3 * TOK])
    v.scalar_tensor_tensor(rows[0:1, 3 * TOK:4 * TOK],
                           rows[0:1, TOK:2 * TOK], 1.0 / D,
                           rows[0:1, 0:TOK], ALU.mult, ALU.subtract)
    v.tensor_scalar_add(rows[0:1, 3 * TOK:4 * TOK],
                        rows[0:1, 3 * TOK:4 * TOK], 1e-5)
    kb.fence(s, kb.ve)
    s.activation(rows[0:1, 3 * TOK:4 * TOK], rows[0:1, 3 * TOK:4 * TOK], AF.Ln)
    kb.fence(v, kb.se)
    v.tensor_scalar_mul(rows[0:1, 3 * TOK:4 * TOK],
                        rows[0:1, 3 * TOK:4 * TOK], -0.5)
    kb.fence(s, kb.ve)
    s.activation(rows[0:1, 3 * TOK:4 * TOK], rows[0:1, 3 * TOK:4 * TOK],
                 AF.Exp)
    kb.fence(t, kb.se)
    for c in range(8):
        for j, (c0, w) in enumerate(t_tiles):
            bank_wait(2)
            mi = t.matmul(psf[2][:, 0:w], ones1_t[:],
                          rows[0:1, 2 * TOK + c0:2 * TOK + c0 + w],
                          start=True, stop=True)
            kb.pe.tick(mi)
            bank_wait(3)
            mi = t.matmul(psf[3][:, 0:w], ones1_t[:],
                          rows[0:1, 3 * TOK + c0:3 * TOK + c0 + w],
                          start=True, stop=True)
            kb.pe.tick(mi)
            kb.fence(v, kb.pe)
            v.tensor_add(stage[:, 0:w], x1i[:, c * TOK + c0:c * TOK + c0 + w],
                         psf[2][:, 0:w])
            bank_ready[2] = kb.ve.mark()
            v.tensor_mul(xn1b[:, c * TOK + c0:c * TOK + c0 + w],
                         stage[:, 0:w], psf[3][:, 0:w])
            bank_ready[3] = kb.ve.mark()
        kb.fence(nc.sync, kb.ve)
        kb.store(o_xn1[:, c * TOK:(c + 1) * TOK], xn1b[:, c * TOK:(c + 1) * TOK])

    v.memset(kmaxr[:], 0.0)
    for part, i3 in (("q", 0), ("k", 1), ("v", 2)):
        for oc in range(8):
            kb.fence(nc.gpsimd, kb.pe)
            kb.load(wbuf[:, 0:8 * 128].rearrange("p (k e) -> p k e", k=8),
                    w_qkv1.rearrange("p (k e) -> p k e", k=8)
                    [:, :, i3 * 1024 + oc * 128:i3 * 1024 + (oc + 1) * 128],
                    gp=True)
            kb.fence(t, kb.gq)
            for j, (c0, w) in enumerate(t_tiles):
                bk = j % 2
                bank_wait(bk)
                for k in range(8):
                    mi = t.matmul(psf[bk][:, 0:w],
                                  wbuf[:, k * 128:(k + 1) * 128],
                                  xn1b[:, k * TOK + c0:k * TOK + c0 + w],
                                  start=(k == 0), stop=(k == 7))
                kb.pe.tick(mi)
                kb.fence(s, kb.pe, kb.st, kb.ve)
                s.copy(bstage[:, (j % 2) * 512:(j % 2) * 512 + w],
                       psf[bk][:, 0:w])
                bank_ready[bk] = kb.se.mark()
                dst = {"q": o_q, "k": o_k, "v": None}[part]
                if part != "v":
                    kb.fence(nc.sync, kb.se)
                    kb.store(dst[:, oc * TOK + c0:oc * TOK + c0 + w],
                             bstage[:, (j % 2) * 512:(j % 2) * 512 + w])
                if part == "k":
                    kb.fence(v, kb.se, kb.pe)
                    v.tensor_mul(hilo[:, 0:w],
                                 bstage[:, (j % 2) * 512:(j % 2) * 512 + w],
                                 bstage[:, (j % 2) * 512:(j % 2) * 512 + w])
                    kb.fence(t, kb.ve)
                    bank_wait(4)
                    mi = t.matmul(psf[4][0:1, 0:w], onesb_t[0:64, :],
                                  hilo[0:64, 0:w], start=True, stop=True)
                    kb.pe.tick(mi)
                    bank_wait(5)
                    mi = t.matmul(psf[5][0:1, 0:w], onesb_t[64:128, :],
                                  hilo[64:128, 0:w], start=True, stop=True)
                    kb.pe.tick(mi)
                    kb.fence(v, kb.pe)
                    v.reduce_max(krow[0:1, 0:1], psf[4][0:1, 0:w], axis=AX.X)
                    bank_ready[4] = kb.ve.mark()
                    v.tensor_tensor(kmaxr[0:1, 2 * oc:2 * oc + 1],
                                    kmaxr[0:1, 2 * oc:2 * oc + 1],
                                    krow[0:1, 0:1], ALU.max)
                    v.reduce_max(krow[0:1, 0:1], psf[5][0:1, 0:w], axis=AX.X)
                    bank_ready[5] = kb.ve.mark()
                    v.tensor_tensor(kmaxr[0:1, 2 * oc + 1:2 * oc + 2],
                                    kmaxr[0:1, 2 * oc + 1:2 * oc + 2],
                                    krow[0:1, 0:1], ALU.max)
                if part == "v":
                    for jb in range(w // 128):
                        kb.fence(t, kb.se, kb.ve)
                        kb.pe.tick(t.transpose(
                            psb[:, 0:128],
                            bstage[:, (j % 2) * 512 + jb * 128:
                                   (j % 2) * 512 + (jb + 1) * 128]
                            .bitcast(BF16) if False else
                            bstage[:, (j % 2) * 512 + jb * 128:
                                   (j % 2) * 512 + (jb + 1) * 128],
                            idt[:]))
                        kb.fence(v, kb.pe, kb.st)
                        tt = (c0 + jb * 128) // 128
                        v.tensor_copy(hilo[:, 0:128], psb[:, 0:128])
                        kb.fence(nc.sync, kb.ve)
                        kb.store(o_vtm[:, tt * 1024 + oc * 128:
                                       tt * 1024 + (oc + 1) * 128],
                                 hilo[:, 0:128])
    kb.fence(s, kb.ve)
    s.activation(kmaxr[:], kmaxr[:], AF.Sqrt)
    kb.fence(nc.sync, kb.se)
    kb.store(o_kmax, kmaxr[:])
    return kb.finish()


def p8a_inmaps(inputs, xn0_full, sz_cores, y2_full, xB6_cores):
    x = inputs["L0_attn_x"]
    w_op = chunk_fm(np.ascontiguousarray(inputs["L0_attn_out_proj_w"].T))
    w_cfc = chunk_fm(np.ascontiguousarray(inputs["L0_attn_c_fc_w"].T))
    w_cpj = chunk_fm(np.ascontiguousarray(inputs["L0_attn_c_proj_w"].T))
    cst = attn_consts()
    maps = []
    for c in range(NCORE):
        b, ch = c // 4, c % 4
        g0 = ch * TOK
        maps.append(dict(
            xn0=slice_tok_cm(xn0_full, b, g0, 0, TOK),
            sz=chunk_fm(sz_cores[c]),
            y2=chunk_fm(np.ascontiguousarray(y2_full[b][:, g0:g0 + TOK])),
            xorig=slice_tok_cm(x, b, g0, 0, TOK),
            xB6=xB6_cores[c],
            w_op=w_op, w_cfc=w_cfc, w_cpj=w_cpj,
            ident=cst["ident"], onesbf=np.ones((128, 1), dtype=BF),
            ones1=cst["ones1"],
        ))
    return maps


def p8b_inmaps(inputs, x1_cores):
    w_qkv1 = chunk_fm(inputs["L1_attn_w_qkv"])
    cst = attn_consts()
    return [dict(x1in=x1_cores[c], w_qkv1=w_qkv1, ident=cst["ident"],
                 onesbf=np.ones((128, 1), dtype=BF), ones1=cst["ones1"])
            for c in range(NCORE)]



def build_p9():
    kb = KB()
    nc = kb.nc
    x1in = kb.inp("x1in", [128, 8 * TOK])
    xn1 = kb.inp("xn1", [128, 8 * TOK], BF16)
    qin = kb.inp("qin", [128, 8 * TOK], BF16)
    kfull = kb.inp("kfull", [16, 64 * T], BF16)
    vtmf = kb.inp("vtmf", [16, T * 64], BF16)
    kpart = kb.inp("kpart", [1, 64])
    w1 = kb.inp("w1", [128, 8 * 4096])
    w2 = kb.inp("w2", [128, 8 * 4096])
    h1 = kb.inp("h1", [128, 32 * 4096])
    h2 = kb.inp("h2", [128, 32 * 4096])
    w3 = kb.inp("w3", [128, 32 * 1024])
    s1v = kb.inp("s1v", [128, 32])
    s2v = kb.inp("s2v", [128, 32])
    h1s = kb.inp("h1s", [128, 32])
    h2s = kb.inp("h2s", [128, 32])
    w3s = kb.inp("w3s", [128, 8])
    onesbf = kb.inp("onesbf", [128, 1], BF16)
    ones1 = kb.inp("ones1", [1, 128])

    o_out = kb.out("o_out", [128, 8 * TOK])

    kb.open()
    v, s, g, t = kb.v, kb.s, kb.g, kb.t
    onesb_t = kb.sb([128, 1], BF16, "onesb")
    ones1_t = kb.sb([1, 128], F32, "ones1")
    xn1b = kb.sb([128, 8 * TOK], BF16, "xn1b")
    O_all = kb.sb([128, 8 * TOK], BF16, "oall")
    kh = kb.sb([65, T], BF16, "kh")
    vh = kb.sb([128, 64 * 65], BF16, "vh")
    qh = kb.sb([65, TOK], BF16, "qh")
    qsq = kb.sb([64, TOK], BF16, "qsq")
    Pt = kb.sb([128, 2 * 1024], BF16, "pt")
    srow = kb.sb([1, TOK], F32, "srow")
    negm = kb.sb([1, TOK], F32, "negm")
    khat = kb.sb([1, 16], F32, "khat")
    khn = kb.sb([1, 16], F32, "khn")
    kpt = kb.sb([1, 64], F32, "kpt")
    lrow = kb.sb([1, 512], F32, "lrow")
    lbv = kb.sb([64, 512], F32, "lbv")
    hq1 = kb.sb([128, 32 * 512], BF16, "hq1")
    hq2 = kb.sb([128, 32 * 512], BF16, "hq2")
    gl = kb.sb([128, 512], F32, "gl")
    wbuf = kb.sb([128, 2 * 32 * 128], BF16, "wbuf")
    sc1 = kb.sb([128, 32], F32, "sc1")
    sc2 = kb.sb([128, 32], F32, "sc2")
    sch1 = kb.sb([128, 32], F32, "sch1")
    sch2 = kb.sb([128, 32], F32, "sch2")
    scw3 = kb.sb([128, 8], F32, "scw3")
    stage = kb.sb([128, 512], F32, "stage")
    strm = kb.sb([128, 512], F32, "strm")

    psS = [kb.ps([128, 1024], F32, "psS") for _ in range(2)]
    psO = kb.ps([128, 512], F32, "psO")
    psX = [kb.ps([128, 512], F32, "psX") for _ in range(2)]
    bank_ready = {}

    def bank_wait(key):
        if key in bank_ready:
            c, val = bank_ready[key]
            t.wait_ge(c.sem, val)

    kb.load(onesb_t[:], onesbf)
    kb.load(ones1_t[:], ones1)
    kb.load(xn1b[:], xn1)
    kb.load(kpt[:], kpart)
    kb.load(sc1[:], s1v)
    kb.load(sc2[:], s2v)
    kb.load(sch1[:], h1s)
    kb.load(sch2[:], h2s)
    kb.load(scw3[:], w3s)
    t_tiles = tiles_of(TOK, 0)

    kb.fence(v, kb.ld)
    for h in range(16):
        v.reduce_max(khat[0:1, h:h + 1], kpt[0:1, h:64:16], axis=AX.X)
    v.tensor_scalar_mul(khn[:], khat[:], -1.0)
    v.memset(kh[64:65, :], 1.0)
    v.memset(vh[:, 0:64 * 65], 1.0)

    for h in range(16):
        hb, oc = 64 * (h % 2), h // 2
        kb.fence(nc.sync, kb.pe, kb.ve)
        kb.load(kh[0:64, 0:T], kfull[h:h + 1, :]
                .rearrange("h (d t) -> (h d) t", d=64))
        kb.load(vh[:, 0:64 * 65].rearrange("p (a d) -> p a d", d=65)
                [:, :, 0:64],
                vtmf[h:h + 1, :].rearrange("h (a p d) -> (h p) a d",
                                           p=128, d=64))
        kb.fence(nc.sync, kb.ve, kb.pe, kb.se)
        kb.load(qh[0:64, :], qin[hb:hb + 64, oc * TOK:(oc + 1) * TOK])
        kb.fence(v, kb.ld)
        v.tensor_scalar_mul(qh[0:64, :], qh[0:64, :], 1.0 / 32.0)
        kb.fence(s, kb.ve)
        s.activation(qsq[:], qh[0:64, :], AF.Square)
        kb.fence(t, kb.se)
        for j2, (c0, w) in enumerate(t_tiles):
            bank_wait(("x", j2 % 2))
            mi = t.matmul(psX[j2 % 2][0:1, 0:w], onesb_t[0:64, :],
                          qsq[0:64, c0:c0 + w], start=True, stop=True)
            kb.pe.tick(mi)
            kb.fence(v, kb.pe)
            v.tensor_copy(srow[0:1, c0:c0 + w], psX[j2 % 2][0:1, 0:w])
            bank_ready[("x", j2 % 2)] = kb.ve.mark()
        kb.fence(s, kb.ve)
        s.activation(srow[:], srow[:], AF.Sqrt)
        kb.fence(v, kb.se)
        v.tensor_scalar_mul(negm[:], srow[:], khn[0:1, h:h + 1])
        kb.fence(v, kb.se)
        v.tensor_copy(qh[64:65, :], negm[0:1, :])
        kb.fence(t, kb.ve, kb.ld)
        for qt in range(4):
            q0 = qt * 512
            bank_wait(("O",))

            def emit_S(kp):
                bk = kp % 2
                bank_wait(("S", bk))
                for half in range(2):
                    kt = 2 * kp + half
                    mi = t.matmul(psS[bk][:, half * 512:(half + 1) * 512],
                                  kh[0:65, 128 * kt:128 * (kt + 1)],
                                  qh[0:65, q0:q0 + 512],
                                  start=True, stop=True)
                kb.pe.tick(mi)
                bank_ready[("Smm", bk)] = kb.pe.mark()

            emit_S(0)
            for kp in range(32):
                bk = kp % 2
                c0m, v0m = bank_ready[("Smm", bk)]
                s._eng.wait_ge(c0m.sem, v0m)
                kb.se.tick(s._eng.activation(
                    Pt[:, bk * 1024:(bk + 1) * 1024], psS[bk][:], AF.Exp))
                bank_ready[("S", bk)] = kb.se.mark()
                exp_mark = kb.se.mark()
                if kp + 1 < 32:
                    emit_S(kp + 1)
                cm, vm = exp_mark
                t.wait_ge(cm.sem, vm)
                for half in range(2):
                    kt = 2 * kp + half
                    mi = t.matmul(psO[0:65, :],
                                  vh[:, 65 * kt:65 * kt + 65],
                                  Pt[:, bk * 1024 + half * 512:
                                     bk * 1024 + (half + 1) * 512],
                                  start=(kt == 0), stop=(kt == 63))
                kb.pe.tick(mi)
            kb.fence(v, kb.pe)
            v.tensor_copy(lrow[:], psO[64:65, :])
            kb.fence(s, kb.ve)
            s.activation(lrow[:], lrow[:], AF.Ln)
            s.activation(lrow[:], lrow[:], AF.Exp, scale=-1.0)
            kb.fence(t, kb.se)
            bank_wait(("x", 0))
            mi = t.matmul(psX[0][0:64, :], ones1_t[0:1, 0:64], lrow[:],
                          start=True, stop=True)
            kb.pe.tick(mi)
            kb.fence(v, kb.pe)
            v.tensor_copy(lbv[:], psX[0][0:64, :])
            bank_ready[("x", 0)] = kb.ve.mark()
            v.tensor_mul(O_all[hb:hb + 64, oc * TOK + q0:oc * TOK + q0 + 512],
                         psO[0:64, :], lbv[:])
            bank_ready[("O",)] = kb.ve.mark()

    WB = 32 * 128

    ldmk = [0, 0]

    def wload(wcm, K, oc, buf, cols=128):
        kb.load(wbuf[:, buf * WB:buf * WB + K * cols]
                .rearrange("p (k e) -> p k e", k=K),
                wcm.rearrange("p (k e) -> p k e", k=K)
                [:, :, oc * cols:(oc + 1) * cols], gp=True)
        ldmk[buf] = kb.gq.v

    wq_mark = [None, None]

    def wfence(buf):
        if wq_mark[buf] is not None:
            c, val = wq_mark[buf]
            nc.gpsimd.wait_ge(c.sem, val)

    for th in range(4):
        tc0 = th * 512
        SL = 8 * 128
        amark = {}

        def aload(wcm, oc, slot):
            if slot in amark:
                cc, vv = amark[slot]
                nc.gpsimd.wait_ge(cc.sem, vv)
            kb.load(wbuf[:, slot * SL:(slot + 1) * SL]
                    .rearrange("p (k e) -> p k e", k=8),
                    wcm.rearrange("p (k e) -> p k e", k=8)
                    [:, :, oc * 128:(oc + 1) * 128], gp=True)
            aldm[slot] = kb.gq.v

        aldm = [0, 0, 0, 0]
        aload(w1, 0, 0)
        aload(w2, 0, 2)
        for oc in range(32):
            b1s = oc % 2
            if oc + 1 < 32:
                aload(w1, oc + 1, 1 - b1s)
                aload(w2, oc + 1, 3 - b1s)
            t.wait_ge(kb.gq.sem, aldm[b1s])
            kb.fence(t, kb.ve)
            bank_wait(("x", 0))
            for k in range(8):
                mi = t.matmul(psX[0][:],
                              wbuf[:, b1s * SL + k * 128:
                                   b1s * SL + (k + 1) * 128],
                              xn1b[:, k * TOK + tc0:k * TOK + tc0 + 512],
                              start=(k == 0), stop=(k == 7))
            kb.pe.tick(mi)
            amark[b1s] = kb.pe.mark()
            kb.fence(s, kb.pe)
            s.activation(gl[:], psX[0][:], AF.Silu, scale=sc1[:, oc:oc + 1])
            bank_ready[("x", 0)] = kb.se.mark()
            t.wait_ge(kb.gq.sem, aldm[2 + b1s])
            bank_wait(("x", 1))
            for k in range(8):
                mi = t.matmul(psX[1][:],
                              wbuf[:, (2 + b1s) * SL + k * 128:
                                   (2 + b1s) * SL + (k + 1) * 128],
                              xn1b[:, k * TOK + tc0:k * TOK + tc0 + 512],
                              start=(k == 0), stop=(k == 7))
            kb.pe.tick(mi)
            amark[2 + b1s] = kb.pe.mark()
            kb.fence(v, kb.pe, kb.se)
            v.scalar_tensor_tensor(hq1[:, oc * 512:(oc + 1) * 512],
                                   psX[1][:], sc2[:, oc:oc + 1], gl[:],
                                   ALU.mult, ALU.mult)
            bank_ready[("x", 1)] = kb.ve.mark()
        wq_mark[0] = wq_mark[1] = kb.pe.mark()
        for (wcm, ssc, src_b, dst_b) in ((h1, sch1, hq1, hq2),
                                         (h2, sch2, hq2, hq1)):
            wfence(0)
            wload(wcm, 32, 0, 0)
            for oc in range(32):
                buf = oc % 2
                if oc < 31:
                    wfence(1 - buf)
                    wload(wcm, 32, oc + 1, 1 - buf)
                t.wait_ge(kb.gq.sem, ldmk[buf])
                bank_wait(("x", oc % 2))
                for k in range(32):
                    mi = t.matmul(psX[oc % 2][:],
                                  wbuf[:, buf * WB + k * 128:
                                       buf * WB + (k + 1) * 128],
                                  src_b[:, k * 512:(k + 1) * 512],
                                  start=(k == 0), stop=(k == 31))
                kb.pe.tick(mi)
                wq_mark[buf] = kb.pe.mark()
                kb.fence(s, kb.pe)
                s.activation(dst_b[:, oc * 512:(oc + 1) * 512],
                             psX[oc % 2][:],
                             AF.Gelu, scale=ssc[:, oc:oc + 1])
                bank_ready[("x", oc % 2)] = kb.se.mark()
        wfence(0)
        wload(w3, 32, 0, 0)
        for oc2 in range(8):
            buf = oc2 % 2
            if oc2 < 7:
                wfence(1 - buf)
                wload(w3, 32, oc2 + 1, 1 - buf)
            t.wait_ge(kb.gq.sem, ldmk[buf])
            kb.fence(t, kb.se)
            bank_wait(("x", 0))
            for k in range(32):
                mi = t.matmul(psX[0][:],
                              wbuf[:, buf * WB + k * 128:
                                   buf * WB + (k + 1) * 128],
                              hq1[:, k * 512:(k + 1) * 512],
                              start=(k == 0), stop=(k == 31))
            kb.pe.tick(mi)
            wq_mark[buf] = kb.pe.mark()
            kb.fence(nc.sync, kb.ve)
            kb.load(strm[:, 0:512], x1in[:, oc2 * TOK + tc0:
                                         oc2 * TOK + tc0 + 512])
            kb.fence(v, kb.pe, kb.ld, kb.st)
            v.scalar_tensor_tensor(stage[:, 0:512], psX[0][:],
                                   scw3[:, oc2:oc2 + 1], strm[:, 0:512],
                                   ALU.mult, ALU.add)
            bank_ready[("x", 0)] = kb.ve.mark()
            v.tensor_add(stage[:, 0:512], stage[:, 0:512],
                         O_all[:, oc2 * TOK + tc0:oc2 * TOK + tc0 + 512])
            kb.fence(nc.sync, kb.ve)
            kb.store(o_out[:, oc2 * TOK + tc0:oc2 * TOK + tc0 + 512],
                     stage[:, 0:512])
    return kb.finish()


def p9_inmaps(inputs, x1_cores, xn1_cores, q_cores, k_cores, vtm_cores,
              kmax_cores):
    w1 = chunk_fm(np.ascontiguousarray(inputs["L1_ffn_w1_w"].T))
    w2 = chunk_fm(np.ascontiguousarray(inputs["L1_ffn_w2_w"].T))
    h1 = chunk_fm(np.ascontiguousarray(inputs["L1_ffn_h1_w"].T))
    h2 = chunk_fm(np.ascontiguousarray(inputs["L1_ffn_h2_w"].T))
    w3 = chunk_fm(np.ascontiguousarray(inputs["L1_ffn_w3_w"].T))
    s1v = chunk_fm(inputs["L1_ffn_w1_s"][:, None])
    s2v = chunk_fm(inputs["L1_ffn_w2_s"][:, None])
    h1s = chunk_fm(inputs["L1_ffn_h1_s"][:, None])
    h2s = chunk_fm(inputs["L1_ffn_h2_s"][:, None])
    w3s = chunk_fm(inputs["L1_ffn_w3_s"][:, None])
    cst = attn_consts()
    maps = []
    for bb in range(2):
        Kfm = np.concatenate(
            [from_cm_np(k_cores[4 * bb + j], TOK) for j in range(4)], axis=1)
        kf = np.ascontiguousarray(
            Kfm.reshape(16, 64, T)).reshape(16, 64 * T)
        Vtm = np.concatenate(
            [vtm_cores[4 * bb + j].reshape(128, 16, 1024)
             .transpose(1, 0, 2).reshape(TOK, 1024)
             for j in range(4)], axis=0)
        vtmf = np.ascontiguousarray(
            Vtm.reshape(T, 16, 64).transpose(1, 0, 2)).reshape(16, T * 64)
        kpart = np.zeros((1, 64), np.float32)
        for j in range(4):
            kpart[0, 16 * j:16 * (j + 1)] = kmax_cores[4 * bb + j][0]
        for j in range(4):
            c = 4 * bb + j
            maps.append(dict(
                x1in=x1_cores[c], xn1=xn1_cores[c], qin=q_cores[c],
                kfull=kf, vtmf=vtmf, kpart=kpart,
                w1=w1, w2=w2, h1=h1, h2=h2, w3=w3,
                s1v=s1v, s2v=s2v, h1s=h1s, h2s=h2s, w3s=w3s,
                onesbf=np.ones((128, 1), dtype=BF), ones1=cst["ones1"],
            ))
    assert len(maps) == NCORE
    return maps


def from_cm_np(a, cols):
    return np.ascontiguousarray(
        a.reshape(128, 8, cols).transpose(1, 0, 2).reshape(1024, cols))



_PROGS = {}


def _prog(name, builder):
    if name not in _PROGS:
        _PROGS[name] = builder()
    return _PROGS[name]


def _run(name, builder, maps):
    nc = _prog(name, builder)
    res = run_bass_kernel_spmd(nc, maps, core_ids=list(range(NCORE)))
    return res.results


def kernel(**inputs):
    inputs = {k: np.asarray(v) for k, v in inputs.items()}
    r1 = _run("p1", build_p1, p1_inmaps(inputs))
    xn0_full = np.stack(
        [np.concatenate([r1[4 * b + j]["o_xn0"] for j in range(4)], 0)
         for b in range(2)])
    r2 = _run("p2", build_p2, p2_inmaps(inputs, r1))
    y2_full = [np.concatenate([r2[4 * b + j]["o_y2"] for j in range(4)], 0)
               for b in range(2)]
    xB_full = xn0_full
    for layer in range(6):
        ra = _run("attn", build_attn, attn_inmaps(inputs, layer, xB_full))
        xA_cms = [ra[c]["o_xA"] for c in range(NCORE)]
        rf = _run("ffn", build_ffn, ffn_inmaps(inputs, layer, xA_cms))
        xB_full = np.stack(
            [np.concatenate(
                [from_cm_np(rf[4 * b + j]["o_xB"], TOK).T for j in range(4)],
                0) for b in range(2)])
    xB6_cores = [chunk_fm(np.ascontiguousarray(
        xB_full[c // 4, (c % 4) * TOK:(c % 4 + 1) * TOK].T))
        for c in range(NCORE)]
    r8a = _run("p8a", build_p8a,
               p8a_inmaps(inputs, xn0_full,
                          [r1[c]["o_sz"] for c in range(NCORE)],
                          y2_full, xB6_cores))
    x1_cores = [r8a[c]["o_x1"] for c in range(NCORE)]
    r8b = _run("p8b", build_p8b, p8b_inmaps(inputs, x1_cores))
    r9 = _run("p9", build_p9,
              p9_inmaps(inputs, x1_cores,
                        [r8b[c]["o_xn1"] for c in range(NCORE)],
                        [r8b[c]["o_q"] for c in range(NCORE)],
                        [r8b[c]["o_k"] for c in range(NCORE)],
                        [r8b[c]["o_vtm"] for c in range(NCORE)],
                        [r8b[c]["o_kmax"] for c in range(NCORE)]))
    out = np.empty((2, T, D), np.float32)
    for c in range(NCORE):
        b, ch = c // 4, c % 4
        out[b, ch * TOK:(ch + 1) * TOK] = from_cm_np(r9[c]["o_out"], TOK).T
    return out
